# version 21
# baseline (speedup 1.0000x reference)
import numpy as np
import ml_dtypes
from contextlib import ExitStack

import concourse.bass as bass
import concourse.mybir as mybir
from concourse.bass_utils import run_bass_kernel_spmd

F32 = mybir.dt.float32
BF16 = mybir.dt.bfloat16
I32 = mybir.dt.int32
AF = mybir.ActivationFunctionType
ALU = mybir.AluOpType
AX = mybir.AxisListType

D = 1024
T = 4096
C = 128
H = 16
DK = 64
INNER = 4096
MRND = 256.0
NCORE = 8


class Sched:

    def __init__(self, nc):
        self.nc = nc
        self.ops = {k: [] for k in ("T", "V", "S", "SY", "GP")}
        self.cnt = {"T": 0, "V": 0, "S": 0, "G": 0}
        self.lastwait = {}
        self.sems = {}

    def set_engine_sems(self, semT, semV, semS, semG=None):
        self.sems = {"T": semT, "V": semV, "S": semS, "G": semG}

    def _waits(self, key, deps):
        out = []
        for dp in deps:
            if dp is None:
                continue
            sem, val = dp
            if val <= 0:
                continue
            k = (key, id(sem))
            if self.lastwait.get(k, -1) < val:
                self.lastwait[k] = val
                out.append((sem, val))
        return out

    def _emit(self, key, fn, deps, inc):
        self.ops[key].append((self._waits(key, deps), fn, inc))

    def t(self, fn, deps=()):
        self.cnt["T"] += 1
        self._emit("T", fn, deps, (self.sems["T"], 1))
        return (self.sems["T"], self.cnt["T"])

    def v(self, fn, deps=()):
        self.cnt["V"] += 1
        self._emit("V", fn, deps, (self.sems["V"], 1))
        return (self.sems["V"], self.cnt["V"])

    def s(self, fn, deps=()):
        self.cnt["S"] += 1
        self._emit("S", fn, deps, (self.sems["S"], 1))
        return (self.sems["S"], self.cnt["S"])

    def g(self, fn, deps=()):
        self.cnt["G"] += 1
        self._emit("GP", fn, deps, (self.sems["G"], 1))
        return (self.sems["G"], self.cnt["G"])

    def dma(self, q, stream, fn, deps=()):
        stream.count += 16
        self._emit(q, fn, deps, (stream.sem, 16))
        return (stream.sem, stream.count)

    def mark(self, key):
        return (self.sems[key], self.cnt[key])


class Stream:
    def __init__(self, sem):
        self.sem = sem
        self.count = 0


def build_program(nch=T // C, ntile=8, tile_ch=4, debug=False):
    nc = bass.Bass("TRN2", target_bir_lowering=False)

    x_d = nc.dram_tensor("x", [T, D], F32, kind="ExternalInput").ap()
    wq_d = nc.dram_tensor("wq", [D, D], F32, kind="ExternalInput").ap()
    wk_d = nc.dram_tensor("wk", [D, D], F32, kind="ExternalInput").ap()
    wv_d = nc.dram_tensor("wv", [D, D], F32, kind="ExternalInput").ap()
    wo_d = nc.dram_tensor("wo", [D, D], F32, kind="ExternalInput").ap()
    wg_d = nc.dram_tensor("wg", [INNER, D], F32, kind="ExternalInput").ap()
    wu_d = nc.dram_tensor("wu", [INNER, D], F32, kind="ExternalInput").ap()
    wd_d = nc.dram_tensor("wd", [D, INNER], F32, kind="ExternalInput").ap()
    trium_d = nc.dram_tensor("trium", [C, 4 * C], BF16, kind="ExternalInput").ap()
    ident_d = nc.dram_tensor("ident", [C, C], BF16, kind="ExternalInput").ap()
    y_d = nc.dram_tensor("y", [T, D], F32, kind="ExternalOutput").ap()

    s1_hbm = nc.dram_tensor("s1_hbm", [T, D], F32, kind="ExternalOutput" if debug else "Internal").ap()
    wgT_hbm = nc.dram_tensor("wgT_hbm", [8, 128, INNER], BF16, kind="Internal").ap()
    wuT_hbm = nc.dram_tensor("wuT_hbm", [8, 128, INNER], BF16, kind="Internal").ap()
    wdT_hbm = nc.dram_tensor("wdT_hbm", [32, 128, D], BF16, kind="Internal").ap()
    red_hbm = nc.dram_tensor("red_hbm", [128, 4], F32, kind="Internal").ap()
    dbg = {}
    if debug:
        for nm, shp in [("d_qt", [128, 8 * 4 * C]), ("d_kt", [128, 8 * 4 * C]),
                        ("d_kc", [C, 4 * D]), ("d_vc", [C, 4 * D]),
                        ("d_ot", [128, 8 * C]),
                        ("d_uts", [C, 16 * C]), ("d_bts", [C, 16 * C]),
                        ("d_ats", [C, 16 * C]), ("d_rts", [128, 8 * C]),
                        ("d_x2", [C, H * DK]), ("d_skf", [128, H * DK]),
                        ("d_df", [128, H * DK]), ("d_attn", [C, D])]:
            dbg[nm] = nc.dram_tensor(nm, shp, F32, kind="ExternalOutput").ap()
        for nm, shp in [("d_h", [C, INNER]), ("d_hq", [C, INNER]),
                        ("d_am", [C, 4]), ("d_xq", [C, D]), ("d_dsc", [C, 12])]:
            dbg[nm] = nc.dram_tensor(nm, shp, F32, kind="ExternalOutput").ap()
    sw_hbm = nc.dram_tensor("sw_hbm", [8, 1], F32, kind="Internal").ap()

    S = Sched(nc)
    NG = 4

    with ExitStack() as st:
        E = st.enter_context
        semT = E(nc.semaphore("semT"))
        semV = E(nc.semaphore("semV"))
        semS = E(nc.semaphore("semS"))
        semG = E(nc.semaphore("semG"))
        S.set_engine_sems(semT, semV, semS, semG)

        def mkstream(name):
            return Stream(E(nc.semaphore(name)))
        st_aux = mkstream("st_aux")
        st_x = [mkstream("st_x0"), mkstream("st_x1")]
        st_xb = [mkstream("st_xb0"), mkstream("st_xb1")]
        st_xt = [mkstream("st_xt0"), mkstream("st_xt1")]
        st_s1 = [mkstream("st_s10"), mkstream("st_s11")]
        st_p0in = [mkstream("st_p0a"), mkstream("st_p0b")]
        st_p0t = [mkstream("st_p0t0"), mkstream("st_p0t1")]
        st_p0s = mkstream("st_p0s")
        st_red = mkstream("st_red")
        st_s1l = [mkstream("st_s1l0"), mkstream("st_s1l1")]
        st_xqt = [mkstream("st_xqt0"), mkstream("st_xqt1")]
        st_hqt = [mkstream(f"st_hqt{i}") for i in range(4)]
        st_wgu = [mkstream("st_wgu0"), mkstream("st_wgu1")]
        st_wdp = [mkstream("st_wdp0"), mkstream("st_wdp1")]
        st_s1e = [mkstream("st_s1e0"), mkstream("st_s1e1")]
        st_y = [mkstream("st_y0"), mkstream("st_y1")]
        st_ot = [mkstream("st_ot0"), mkstream("st_ot1")]

        trium = E(nc.sbuf_tensor("trium_sb", [C, 4 * C], BF16))
        ident = E(nc.sbuf_tensor("ident_sb", [C, C], BF16))
        cons = E(nc.sbuf_tensor("cons", [128, 4], F32))
        swv = E(nc.sbuf_tensor("swv", [128, 8], F32))
        CEPS = cons[:, 0:1]
        CLO = cons[:, 1:2]
        CHI = cons[:, 2:3]
        CZERO = cons[:, 3:4]
        CNHALF = swv[:, 7:8]

        m_aux1 = S.dma("SY", st_aux, lambda e: e.dma_start(out=trium[:], in_=trium_d))
        m_aux2 = S.dma("SY", st_aux, lambda e: e.dma_start(out=ident[:], in_=ident_d))
        S.v(lambda e: e.memset(cons[:, 0:1], 1e-5))
        S.v(lambda e: e.memset(cons[:, 1:2], MRND - 0.5))
        S.v(lambda e: e.memset(cons[:, 2:3], MRND))
        S.v(lambda e: e.memset(cons[:, 3:4], 0.0))
        mk_cons = S.v(lambda e: e.memset(swv[:, 7:8], -0.5))

        p01 = ExitStack()
        E1 = p01.enter_context
        p0in = [E1(nc.sbuf_tensor(f"p0in{i}", [128, D], F32)) for i in range(2)]
        p0bf = [E1(nc.sbuf_tensor(f"p0bf{i}", [128, D], BF16)) for i in range(2)]
        p0t1 = E1(nc.sbuf_tensor("p0t1", [128, 512], F32))
        p0tr = [E1(nc.sbuf_tensor(f"p0tr{i}", [128, 1024], BF16)) for i in range(1)]
        p0acc = E1(nc.sbuf_tensor("p0acc", [128, 4], F32))
        p0red = E1(nc.sbuf_tensor("p0red", [1, 128], F32))
        p0sw = E1(nc.sbuf_tensor("p0sw", [1, 8], F32))

        wq_sb = E1(nc.sbuf_tensor("wq_sb", [128, 8 * D], BF16))
        wk_sb = E1(nc.sbuf_tensor("wk_sb", [128, 8 * D], BF16))
        wv_sb = E1(nc.sbuf_tensor("wv_sb", [128, 8 * D], BF16))
        wo_sb = E1(nc.sbuf_tensor("wo_sb", [128, 8 * D], BF16))
        x_c = [E1(nc.sbuf_tensor(f"x_c{i}", [C, D], F32)) for i in range(5)]
        xb_c = [E1(nc.sbuf_tensor(f"xb_c{i}", [C, D], BF16)) for i in range(2)]
        xT2 = [E1(nc.sbuf_tensor(f"xT2_{i}", [128, 4 * D], BF16)) for i in range(2)]
        qs_all = E1(nc.sbuf_tensor("qs_all", [128, 8 * 4 * C], BF16))
        ks_all = E1(nc.sbuf_tensor("ks_all", [128, 8 * 4 * C], BF16))
        qpd_all = E1(nc.sbuf_tensor("qpd_all", [128, 8 * 4 * 2 * C], BF16))
        k_c2 = E1(nc.sbuf_tensor("k_c2", [C, 4 * D], BF16))
        v_c2 = E1(nc.sbuf_tensor("v_c2", [C, 4 * D], BF16))
        bts_a = E1(nc.sbuf_tensor("bts_a", [C, 16 * C], BF16))
        ats_a = E1(nc.sbuf_tensor("ats_a", [C, 16 * C], BF16))
        uts_a = E1(nc.sbuf_tensor("uts_a", [C, 16 * C], BF16))
        rts_a = E1(nc.sbuf_tensor("rts_a", [128, 8 * C], BF16))
        x2s_a = E1(nc.sbuf_tensor("x2s_a", [C, H * DK], BF16))
        o_c = [E1(nc.sbuf_tensor(f"o_c{i}", [C, D], BF16)) for i in range(1)]
        ot_c = [E1(nc.sbuf_tensor(f"ot_c{i}", [128, 8 * C], BF16)) for i in range(1)]
        skf = E1(nc.sbuf_tensor("skf", [128, H * DK], F32))
        df = E1(nc.sbuf_tensor("df", [128, H * DK], F32))
        kks = E1(nc.sbuf_tensor("kks", [128, 512], F32))
        dds = E1(nc.sbuf_tensor("dds", [128, 512], F32))
        skb = E1(nc.sbuf_tensor("skb", [128, H * DK], BF16))
        db = E1(nc.sbuf_tensor("db", [128, H * DK], BF16))
        bn6 = E1(nc.sbuf_tensor("bn6", [C, 12], F32))
        bnmv = E1(nc.sbuf_tensor("bnmv", [C, 2], F32))
        rstd = E1(nc.sbuf_tensor("rstd", [C, 1], F32))
        nmr = E1(nc.sbuf_tensor("nmr", [C, 1], F32))
        pb = [E1(nc.psum_tensor(f"pb{i}", [128, 512], F32)) for i in range(8)]

        def ld_w(sb, dr):
            return S.dma("GP", st_aux, lambda e, sb=sb, dr=dr: e.dma_start(
                out=sb.ap().rearrange("p (ic c) -> p ic c", ic=8),
                in_=dr.rearrange("(ic p) c -> p ic c", p=128)))
        ld_w(wq_sb, wq_d)
        ld_w(wk_sb, wk_d)
        ld_w(wv_sb, wv_d)
        ld_w(wo_sb, wo_d)
        m_aux_all = (st_aux.sem, st_aux.count)

        m_st0 = [S.v(lambda e, t=t_: e.memset(t[:], 0.0))
                 for t_ in (skf, df, skb, db, qpd_all)]
        w_dep = [m_aux_all, m_aux1, mk_cons] + m_st0
        m_wo = m_aux_all

        W3 = [(wg_d, wgT_hbm), (wu_d, wuT_hbm), (wd_d, wdT_hbm)]
        p0A_units = []
        p0B_units = []
        for wi, (wdr, _) in enumerate(W3):
            rows, cols = wdr.shape[0], wdr.shape[1]
            for b in range(rows // 128):
                for ss in range(cols // D):
                    p0A_units.append((wi, b, ss))
                    p0B_units.append((wi, b, ss))

        p0st = {
            "ldbuf": 0, "use": [(semV, 0), (semV, 0)],
            "accfirst": [True, True, True], "Adone": (semV, 0),
            "trbuf": 0, "tr_free": [(st_p0s.sem, 0), (st_p0s.sem, 0)],
            "bank": 0,
            "bfbuf": 0, "bf_free": [[(semT, 0)], [(semT, 0)]],
            "qfree": (semV, 0),
        }
        sw_ready = []
        p0_wdone = {}

        def emit_p0A(u):
            wi, b, ss = u
            buf = p0st["ldbuf"]; p0st["ldbuf"] ^= 1
            wdr = W3[wi][0]
            ldm = S.dma("GP", st_p0in[buf],
                        lambda e, wdr=wdr, b=b, ss=ss, buf=buf: e.dma_start(
                            out=p0in[buf][:],
                            in_=wdr[b * 128:(b + 1) * 128, ss * D:(ss + 1) * D]),
                        deps=[p0st["use"][buf]])
            m1 = S.v(lambda e, buf=buf: e.reduce_sum(
                p0acc[:, 3:4], p0in[buf][:], axis=AX.X, apply_absolute_value=True),
                deps=[ldm])
            if p0st["accfirst"][wi]:
                p0st["accfirst"][wi] = False
                m2 = S.v(lambda e, wi=wi: e.tensor_copy(
                    p0acc[:, wi:wi + 1], p0acc[:, 3:4]), deps=[m1])
            else:
                m2 = S.v(lambda e, wi=wi: e.tensor_add(
                    p0acc[:, wi:wi + 1], p0acc[:, wi:wi + 1], p0acc[:, 3:4]), deps=[m1])
            p0st["use"][buf] = m2
            p0st["Adone"] = m2

        def emit_p0_scales():
            mst = S.dma("SY", st_red, lambda e: e.dma_start(
                out=red_hbm[:, 0:3], in_=p0acc[:, 0:3]), deps=[p0st["Adone"]])
            for wi in range(3):
                nel = float(W3[wi][0].shape[0] * W3[wi][0].shape[1])
                mld = S.dma("SY", st_red, lambda e, wi=wi: e.dma_start(
                    out=p0red[:], in_=red_hbm[:, wi:wi + 1].transpose([1, 0]),
                    allow_slow_non_contiguous=True),
                    deps=[mst, (semV, S.cnt["V"])])
                mr = S.v(lambda e: e.reduce_sum(p0sw[0:1, 6:7], p0red[:], axis=AX.X),
                         deps=[mld])
                mm = S.v(lambda e, nel=nel, wi=wi: e.tensor_scalar(
                    p0sw[0:1, 4 + (wi % 2) * 0 + 0:5] if False else p0sw[0:1, 6:7],
                    p0sw[0:1, 6:7], 1.0 / nel, 1e-5, ALU.mult, ALU.max), deps=[mr])
                mcpy = S.v(lambda e, wi=wi: e.tensor_copy(
                    p0sw[0:1, 3 + wi:4 + wi], p0sw[0:1, 6:7]), deps=[mm])
                mrc = S.v(lambda e, wi=wi: e.reciprocal(
                    p0sw[0:1, wi:wi + 1], p0sw[0:1, 6:7]), deps=[mcpy])
                mstw = S.dma("SY", st_red, lambda e, wi=wi: e.dma_start(
                    out=sw_hbm[wi:wi + 1, :], in_=p0sw[0:1, wi:wi + 1]), deps=[mrc])
                mstm = S.dma("SY", st_red, lambda e, wi=wi: e.dma_start(
                    out=sw_hbm[4 + wi:5 + wi, :], in_=p0sw[0:1, 3 + wi:4 + wi]),
                    deps=[mrc])
                mb1 = S.dma("SY", st_red, lambda e, wi=wi: e.dma_start(
                    out=swv[:, wi:wi + 1],
                    in_=sw_hbm[wi:wi + 1, 0:1].partition_broadcast(128)), deps=[mstw])
                mb2 = S.dma("SY", st_red, lambda e, wi=wi: e.dma_start(
                    out=swv[:, 4 + wi:5 + wi],
                    in_=sw_hbm[4 + wi:5 + wi, 0:1].partition_broadcast(128)),
                    deps=[mstm])
                sw_ready.append(mb1)
                sw_ready.append(mb2)

        def emit_p0B(u):
            wi, b, ss = u
            buf = p0st["ldbuf"]; p0st["ldbuf"] ^= 1
            wdr, wtd = W3[wi]
            ldm = S.dma("GP", st_p0in[buf],
                        lambda e, wdr=wdr, b=b, ss=ss, buf=buf: e.dma_start(
                            out=p0in[buf][:],
                            in_=wdr[b * 128:(b + 1) * 128, ss * D:(ss + 1) * D]),
                        deps=[p0st["use"][buf]])
            deps = [ldm, p0st["qfree"]] + sw_ready
            bfb = p0st["bfbuf"]; p0st["bfbuf"] ^= 1
            mq = None
            for sp in range(2):
                sl = slice(sp * 512, (sp + 1) * 512)
                mt1 = S.v(lambda e, wi=wi, buf=buf, sl=sl, bfb=bfb:
                          e.tensor_scalar(
                    p0bf[bfb][:, sl], p0in[buf][:, sl], swv[:, wi:wi + 1], 0.5,
                    ALU.mult, ALU.is_gt),
                    deps=deps + [p0st["qfree"]] + p0st["bf_free"][bfb])
                mt2 = S.v(lambda e, wi=wi, buf=buf, sl=sl: e.tensor_scalar(
                    p0t1[:], p0in[buf][:, sl], swv[:, wi:wi + 1], -0.5,
                    ALU.mult, ALU.is_lt), deps=deps)
                mq = S.v(lambda e, bfb=bfb, sl=sl: e.tensor_sub(
                    p0bf[bfb][:, sl], p0bf[bfb][:, sl], p0t1[:]),
                    deps=[mt1, mt2])
                p0st["qfree"] = mq
            p0st["use"][buf] = mq
            stg = 0
            ev_marks = []
            lastmm = None
            for rnd in range(2):
                bank = p0st["bank"]; p0st["bank"] = (p0st["bank"] + 1) % 8
                mm = None
                for k in range(4):
                    blk = rnd * 4 + k
                    mm = S.t(lambda e, bfb=bfb, blk=blk, k=k, bank=bank:
                             e.matmul(pb[bank][:, k * 128:(k + 1) * 128],
                                      p0bf[bfb][:, blk * 128:(blk + 1) * 128],
                                      ident[:], start=(k == 0), stop=True),
                             deps=[mq, m_aux2] +
                                  (pbfree[bank] if k == 0 else []))
                mev = S.s(lambda e, stg=stg, rnd=rnd, bank=bank: e.copy(
                    p0tr[stg][:, rnd * 512:(rnd + 1) * 512], pb[bank][:]),
                    deps=[mm, p0st["tr_free"][stg]])
                pbfree[bank] = [mev]
                ev_marks.append(mev)
                lastmm = mm
            if wi < 2:
                dst = wtd[0:8, :, b * 128:(b + 1) * 128]
            else:
                dst = wtd[ss * 8:ss * 8 + 8, :, b * 128:(b + 1) * 128]
            mstb = S.dma("GP", st_p0s,
                         lambda e, dst=dst, stg=stg: e.dma_start(
                             out=dst.transpose([1, 0, 2]),
                             in_=p0tr[stg].ap().rearrange(
                                 "p (ic q) -> p ic q", ic=8)),
                         deps=ev_marks)
            p0st["tr_free"][stg] = (st_p0s.sem, st_p0s.count)
            p0st["bf_free"][bfb] = [lastmm]
            p0_wdone[wi] = mstb

        xuse = [(semV, 0)] * 5
        xbuse = [(st_xt[0].sem, 0), (st_xt[1].sem, 0),
                 (st_xt[0].sem, 0), (st_xt[1].sem, 0)]
        xtuse = [(semT, 0), (semT, 0)]
        pbfree = [[(semV, 0)]] * 8
        st_cast = [(semV, 0)]
        scrfree = [(semV, 0)] * 8
        ln_free = [(semS, 0), (semS, 0)]
        ot_free = [(semT, 0), (semT, 0)]
        oc_free = [(st_ot[0].sem, 0), (st_ot[1].sem, 0)]
        s1_free = (st_s1[0].sem, 0)
        m_s1_stores = []

        def st_dep():
            return st_cast[0]

        p0_pump = [lambda n: None]

        def emit_xpipe(qi):
            qp = qi % 2
            mxts = []
            for half in range(4):
                ci = 4 * qi + half
                bx = ci % 2
                r0 = ci * C
                mxb = S.dma("GP", st_xb[ci % 2],
                            lambda e, r0=r0, bx=bx: e.dma_start(
                                out=xb_c[bx][:], in_=x_d[r0:r0 + C, :]),
                            deps=[xbuse[bx]])
                mxt = None
                for ic in range(8):
                    mxt = S.dma("SY", st_xt[ci % 2],
                                lambda e, ic=ic, bx=bx, qp=qp, half=half:
                                e.dma_start(
                                    out=xT2[qp][:, ic * 512 + half * 128:
                                                ic * 512 + half * 128 + 128],
                                    in_=xb_c[bx][:, ic * 128:(ic + 1) * 128],
                                    transpose=True),
                                deps=[mxb, xtuse[qp]])
                xbuse[bx] = mxt
                mxts.append(mxt)
            return mxts

        def emit_quad(qi, mxts, nq):
            qp = qi % 2
            mxs = []
            for half in range(4):
                ci = 4 * qi + half
                bi8 = ci % 5
                r0 = ci * C
                mxs.append(S.dma("SY", st_x[ci % 2],
                                 lambda e, r0=r0, bi8=bi8: e.dma_start(
                                     out=x_c[bi8][:], in_=x_d[r0:r0 + C, :]),
                                 deps=[xuse[bi8]]))
            dep0 = mxts + w_dep

            for r in range(4):
                lastb = {}
                for qsel in (True, False):
                    wsb = wq_sb if qsel else wk_sb
                    for pp2 in range(2):
                        p = r * 2 + pp2
                        bi = (0 if qsel else 4) + (r % 2) * 2 + pp2
                        for ic in range(8):
                            first = ic == 0
                            lastb[bi] = S.t(
                                lambda e, wsb=wsb, ic=ic, p=p, bi=bi,
                                first=first, qp=qp:
                                e.matmul(pb[bi][:, :],
                                         wsb[:, ic * D + p * 128:
                                             ic * D + p * 128 + 128],
                                         xT2[qp][:, ic * 512:(ic + 1) * 512],
                                         start=first, stop=(ic == 7)),
                                deps=dep0 + (pbfree[bi] if first else []))
                for pp2 in range(2):
                    p = r * 2 + pp2
                    bi = (r % 2) * 2 + pp2
                    mq1 = S.s(lambda e, p=p, bi=bi: e.copy(
                        qs_all[:, p * 512:p * 512 + 512], pb[bi][:]),
                        deps=[lastb[bi]])
                    mq2 = S.s(lambda e, p=p, bi=bi: e.copy(
                        qpd_all[0:64, p * 1024:(p + 1) * 1024].rearrange(
                            "p (c w) -> p c w", c=4)[:, :, 0:128],
                        pb[bi][0:64, :].rearrange("p (c w) -> p c w", c=4)),
                        deps=[lastb[bi]])
                    mq3 = S.s(lambda e, p=p, bi=bi: e.copy(
                        qpd_all[64:128, p * 1024:(p + 1) * 1024].rearrange(
                            "p (c w) -> p c w", c=4)[:, :, 128:256],
                        pb[bi][64:128, :].rearrange("p (c w) -> p c w", c=4)),
                        deps=[lastb[bi]])
                    pbfree[bi] = [mq3]
                    bk = 4 + (r % 2) * 2 + pp2
                    mk1 = S.s(lambda e, p=p, bk=bk: e.copy(
                        ks_all[:, p * 512:p * 512 + 512], pb[bk][:]),
                        deps=[lastb[bk]])
                    pbfree[bk] = [mk1]
            m_qk = S.mark("S")
            p0_pump[0](2)

            for sub in range(2):
                lastkv = {}
                for h2 in range(2):
                    half = sub * 2 + h2
                    for vsel in (False, True):
                        wsb = wv_sb if vsel else wk_sb
                        boff = h2 * 4 + (2 if vsel else 0)
                        for dhalf in range(2):
                            bi = boff + dhalf
                            for ic in range(8):
                                first = ic == 0
                                lastkv[bi] = S.t(
                                    lambda e, wsb=wsb, ic=ic, dhalf=dhalf,
                                    bi=bi, qp=qp, half=half, first=first:
                                    e.matmul(pb[bi][:, :],
                                             xT2[qp][:, ic * 512 + half * 128:
                                                     ic * 512 + half * 128 + 128],
                                             wsb[:, ic * D + dhalf * 512:
                                                 ic * D + dhalf * 512 + 512],
                                             start=first, stop=(ic == 7)),
                                    deps=dep0 + (pbfree[bi] if first else []))
                for bi in range(8):
                    h2 = bi // 4
                    half = sub * 2 + h2
                    vsel = (bi % 4) >= 2
                    dhalf = bi % 2
                    dstt = v_c2 if vsel else k_c2
                    co = half * D + dhalf * 512
                    pbfree[bi] = [S.s(lambda e, dstt=dstt, co=co, bi=bi:
                                      e.copy(dstt[:, co:co + 512], pb[bi][:]),
                                      deps=[lastkv[bi]])]
            xtuse[qp] = S.mark("T")
            m_kv = S.mark("S")
            if qi + 1 < nq:
                pend_T[qi + 1] = emit_xpipe(qi + 1)
            p0_pump[0](2)
            for half in range(4):
                emit_heads(4 * qi + half, half, m_qk, m_kv, mxs)
                p0_pump[0](2)
                emit_wo_ln(4 * qi + half, m_qk, m_kv)
                p0_pump[0](3)

        hstate = {}

        def emit_heads(ci, half, m_qk, m_kv, mxs):
            bi4 = ci % 5
            r0 = ci * C
            koff = half * D
            qoff = half * 128

            def qs(p):
                return qs_all[:, p * 512 + qoff: p * 512 + qoff + 128]

            def ks(p):
                return ks_all[:, p * 512 + qoff: p * 512 + qoff + 128]

            def qpd(p):
                return qpd_all[:, p * 1024 + half * 256:
                               p * 1024 + half * 256 + 256]

            def kc(h):
                return k_c2[:, koff + h * DK: koff + (h + 1) * DK]

            def vc(h):
                return v_c2[:, koff + h * DK: koff + (h + 1) * DK]

            def kcp(p):
                return k_c2[:, koff + p * 128: koff + p * 128 + 128]

            def vcp(p):
                return v_c2[:, koff + p * 128: koff + p * 128 + 128]

            def skbp(p):
                return skb[:, p * 128:(p + 1) * 128]

            def dbp(p):
                return db[:, p * 128:(p + 1) * 128]

            lA = {}
            lR = {}
            for g in range(4):
                for pp in range(2):
                    p = 2 * g + pp
                    rb = 4 + p // 4
                    lA[g] = S.t(lambda e, p=p, pp=pp, g=g: e.matmul(
                        pb[g][:, pp * 256: pp * 256 + 256], ks(p), qpd(p),
                        start=(pp == 0), stop=True),
                        deps=[m_qk] + (pbfree[g] if pp == 0 else []))
                    lR[rb] = S.t(lambda e, p=p, rb=rb: e.matmul(
                        pb[rb][:, (p % 4) * 128:(p % 4) * 128 + 128],
                        skbp(p), qs(p),
                        start=(p % 4 == 0), stop=True),
                        deps=[m_qk, st_dep()] +
                             (pbfree[rb] if p % 4 == 0 else []))
            evb = {}
            for g in range(4):
                g0 = g * 512
                mBT = S.v(lambda e, g=g, g0=g0: e.tensor_tensor(
                    bts_a[:, g0:g0 + 512], pb[g][:], trium[:], ALU.mult),
                    deps=[lA[g]])
                mATs = S.s(lambda e, g=g, g0=g0: e.copy(
                    ats_a[:, g0:g0 + 512], pb[g][:]), deps=[lA[g], mBT])
                pbfree[g] = [mATs]
                evb[g] = (mBT, mATs)
            mRT2 = {}
            for rb in (4, 5):
                mRT2[rb] = S.s(lambda e, rb=rb: e.copy(
                    rts_a[:, (rb - 4) * 512:(rb - 4) * 512 + 512],
                    pb[rb][:]), deps=[lR[rb]])
                pbfree[rb] = [mRT2[rb]]

            lXU = {}
            lX2 = {}
            for g in range(4):
                mBT, mATs = evb[g]
                for pp in range(2):
                    p = 2 * g + pp
                    x2b = 4 + p // 4
                    first2 = (p % 4 == 0)
                    lX2[x2b] = S.t(lambda e, p=p, x2b=x2b, first2=first2:
                                   e.matmul(
                        pb[x2b][:, (p % 4) * 128:(p % 4) * 128 + 128],
                        qs(p), skbp(p),
                        start=first2, stop=False),
                        deps=[m_qk, st_dep()] +
                             (pbfree[x2b] if first2 else []))
                    for j2 in range(2):
                        h = 2 * p + j2
                        co = h * 128
                        lXU[g] = S.t(lambda e, co=co, h=h, g=g: e.matmul(
                            pb[g][:, (h % 4) * 128:(h % 4) * 128 + 128],
                            bts_a[:, co:co + 128], ats_a[:, co:co + 128],
                            start=(h % 4 == 0), stop=False),
                            deps=[mBT, mATs] +
                                 (pbfree[g] if h % 4 == 0 else []))
                        lX2[x2b] = S.t(lambda e, co=co, h=h, p=p, x2b=x2b,
                                       j2=j2: e.matmul(
                            pb[x2b][:, (p % 4) * 128 + j2 * DK:
                                    (p % 4) * 128 + (j2 + 1) * DK],
                            bts_a[:, co:co + 128], kc(h),
                            start=False, stop=True), deps=[mBT, m_kv])
                    lXU[g] = S.t(lambda e, p=p, pp=pp, g=g: e.matmul(
                        pb[g][:, pp * 256: pp * 256 + 256],
                        rts_a[:, p * 128:(p + 1) * 128], qpd(p),
                        start=False, stop=True), deps=[mRT2[4 + p // 4]])
            evu = {}
            for g in range(4):
                g0 = g * 512
                mUTs = S.v(lambda e, g=g, g0=g0: e.tensor_tensor(
                    uts_a[:, g0:g0 + 512], pb[g][:], trium[:], ALU.mult),
                    deps=[lXU[g]])
                pbfree[g] = [mUTs]
                evu[g] = mUTs
            for bb in (4, 5):
                co = (bb - 4) * 512
                mX2s = S.s(lambda e, bb=bb, co=co: e.copy(
                    x2s_a[:, co:co + 512], pb[bb][:]), deps=[lX2[bb]])
                pbfree[bb] = [mX2s]
                evu[bb] = mX2s

            lO = {}
            lKD = {}
            for p in range(8):
                g = p // 2
                ob = 6 + p // 4
                kb = p // 4
                db_ = 2 + p // 4
                oco = (p % 4) * 128
                for j2 in range(2):
                    h = 2 * p + j2
                    co = h * 128
                    firsto = (p % 4 == 0 and j2 == 0)
                    lO[ob] = S.t(lambda e, co=co, h=h, ob=ob, oco=oco, j2=j2,
                                 firsto=firsto: e.matmul(
                        pb[ob][:, oco + j2 * DK: oco + (j2 + 1) * DK],
                        uts_a[:, co:co + 128], vc(h),
                        start=firsto, stop=False),
                        deps=[evu[g], m_kv] + (pbfree[ob] if firsto else []))
                lO[ob] = S.t(lambda e, p=p, ob=ob, oco=oco: e.matmul(
                    pb[ob][:, oco: oco + 128], qs(p), dbp(p),
                    start=False, stop=True), deps=[st_dep()])
                lKD[kb] = S.t(lambda e, p=p, kb=kb: e.matmul(
                    pb[kb][:, (p % 4) * 128:(p % 4) * 128 + 128],
                    kcp(p), kcp(p), start=(p % 4 == 0), stop=True),
                    deps=[m_kv] + (pbfree[kb] if p % 4 == 0 else []))
                lKD[db_] = S.t(lambda e, p=p, db_=db_: e.matmul(
                    pb[db_][:, (p % 4) * 128:(p % 4) * 128 + 128],
                    x2s_a[:, p * 128:(p + 1) * 128], vcp(p),
                    start=(p % 4 == 0), stop=True),
                    deps=[evu[4 + p // 4], m_kv] +
                         (pbfree[db_] if p % 4 == 0 else []))
            m_heads_T = S.mark("T")
            ot_marks = []
            pp2o = 0
            for ob in (6, 7):
                co = (ob - 6) * 512
                mo = S.s(lambda e, ob=ob, co=co, pp2o=pp2o: e.copy(
                    o_c[pp2o][:, co:co + 512], pb[ob][:]),
                    deps=[lO[ob]] + ([oc_free[pp2o]] if ob == 6 else []))
                pbfree[ob] = [mo]
                ot_marks.append(mo)
            mupd = []
            for kb in (0, 1, 2, 3):
                scr = kks if kb < 2 else dds
                stt = skf if kb < 2 else df
                so = (kb % 2) * 256
                fo = (kb % 2) * 512
                cpa = S.s(lambda e, kb=kb, scr=scr, so=so: e.copy(
                    scr[0:64, so:so + 256].rearrange(
                        "p (pr w) -> p pr w", pr=4),
                    pb[kb][0:64, :].rearrange(
                        "p (pr w) -> p pr w", pr=4)[:, :, 0:64]),
                    deps=[lKD[kb], scrfree[kb]])
                cpb = S.s(lambda e, kb=kb, scr=scr, so=so: e.copy(
                    scr[64:128, so:so + 256].rearrange(
                        "p (pr w) -> p pr w", pr=4),
                    pb[kb][64:128, :].rearrange(
                        "p (pr w) -> p pr w", pr=4)[:, :, 64:128]),
                    deps=[lKD[kb], scrfree[4 + kb]])
                pbfree[kb] = [cpb]
                dst = stt[0:64, fo:fo + 512].rearrange(
                    "p (pr w) -> p pr w", pr=4)[:, :, 0:64]
                src = scr[0:64, so:so + 256].rearrange(
                    "p (pr w) -> p pr w", pr=4)
                mUa = S.v(lambda e, dst=dst, src=src: e.tensor_add(
                    dst, dst, src), deps=[cpa])
                dst2 = stt[64:128, fo:fo + 512].rearrange(
                    "p (pr w) -> p pr w", pr=4)[:, :, 64:128]
                src2 = scr[64:128, so:so + 256].rearrange(
                    "p (pr w) -> p pr w", pr=4)
                mUb = S.v(lambda e, dst2=dst2, src2=src2: e.tensor_add(
                    dst2, dst2, src2), deps=[cpb])
                scrfree[kb] = mUa
                scrfree[4 + kb] = mUb
                mupd += [mUa, mUb]
            mskb = S.g(lambda e: e.tensor_copy(skb[:], skf[:]),
                       deps=mupd + [m_heads_T])
            mdbb = S.g(lambda e: e.tensor_copy(db[:], df[:]),
                       deps=[mskb])
            st_cast[0] = mdbb

            pp2 = 0
            motd = None
            lastmm = None
            for rnd in range(2):
                bank = 6 + rnd
                mm = None
                for k in range(4):
                    blk = rnd * 4 + k
                    mm = S.t(lambda e, blk=blk, k=k, bank=bank, pp2=pp2:
                             e.matmul(pb[bank][:, k * 128:(k + 1) * 128],
                                      o_c[pp2][:, blk * 128:(blk + 1) * 128],
                                      ident[:], start=(k == 0), stop=True),
                             deps=[ot_marks[blk // 4], m_aux2] +
                                  (pbfree[bank] if k == 0 else []))
                motd = S.s(lambda e, rnd=rnd, bank=bank, pp2=pp2: e.copy(
                    ot_c[pp2][:, rnd * 512:(rnd + 1) * 512], pb[bank][:]),
                    deps=[mm, ot_free[pp2]])
                pbfree[bank] = [motd]
                lastmm = mm
            oc_free[pp2] = lastmm
            hstate[ci] = (motd, mxs, half)

        def emit_wo_ln(ci, m_qk, m_kv):
            nonlocal s1_free
            motd, mxs, half = hstate[ci]
            bi4 = ci % 5
            pp2 = 0
            r0 = ci * C
            mWop = {}
            for dhalf in range(2):
                bi = 4 + dhalf
                for ic in range(8):
                    first = ic == 0
                    mWop[bi] = S.t(lambda e, ic=ic, dhalf=dhalf, bi=bi,
                                   first=first, pp2=pp2: e.matmul(
                        pb[bi][:], ot_c[pp2][:, ic * 128:(ic + 1) * 128],
                        wo_sb[:, ic * D + dhalf * 512:
                              ic * D + dhalf * 512 + 512],
                        start=first, stop=(ic == 7)),
                        deps=[motd, m_wo] + (pbfree[bi] if first else []))
            ot_free[pp2] = S.mark("T")
            mr1a = S.v(lambda e, bi4=bi4: e.tensor_add(
                x_c[bi4][:, 0:512], x_c[bi4][:, 0:512], pb[4][:]),
                deps=[mWop[4], mxs[half]])
            pbfree[4] = [mr1a]
            mr1b = S.v(lambda e, bi4=bi4: e.tensor_add(
                x_c[bi4][:, 512:1024], x_c[bi4][:, 512:1024], pb[5][:]),
                deps=[mWop[5], mxs[half]])
            pbfree[5] = [mr1b]
            mb1 = S.v(lambda e, bi4=bi4: e.bn_stats(bn6[:, 0:6],
                                                    x_c[bi4][:, 0:512]),
                      deps=[mr1a])
            mb2 = S.v(lambda e, bi4=bi4: e.bn_stats(bn6[:, 6:12],
                                                    x_c[bi4][:, 512:1024]),
                      deps=[mr1b])
            mb3 = S.v(lambda e: e.bn_aggr(bnmv[:], bn6[:]),
                      deps=[mb1, mb2, ln_free[0]])
            msq = S.s(lambda e: e.activation(rstd[:], bnmv[:, 1:2], AF.Sqrt,
                                             bias=CEPS, scale=1.0),
                      deps=[mb3, mk_cons])
            mrc = S.v(lambda e: e.reciprocal(rstd[:], rstd[:]), deps=[msq])
            mnm = S.v(lambda e: e.scalar_tensor_tensor(
                nmr[:], bnmv[:, 0:1], -1.0, rstd[:], ALU.mult, ALU.mult),
                deps=[mrc, ln_free[1]])
            mact = S.s(lambda e, bi4=bi4: e.activation(
                x_c[bi4][:], x_c[bi4][:], AF.Identity, bias=nmr[:],
                scale=rstd[:]), deps=[mnm])
            ln_free[0] = msq
            ln_free[1] = mact
            mst = S.dma("SY", st_s1[ci % 2], lambda e, r0=r0, bi4=bi4:
                        e.dma_start(
                out=s1_hbm[r0:r0 + C, :], in_=x_c[bi4][:]), deps=[mact])
            xuse[bi4] = mst
            m_s1_stores.append(mst)
            if debug and ci == 0:
                dv = S.mark("V")
                ds = S.mark("S")
                for nm, tile in [("d_qt", qs_all), ("d_kt", ks_all),
                                 ("d_kc", k_c2), ("d_vc", v_c2),
                                 ("d_ot", ot_c[0]), ("d_uts", uts_a),
                                 ("d_bts", bts_a), ("d_ats", ats_a),
                                 ("d_rts", rts_a), ("d_x2", x2s_a),
                                 ("d_skf", skf), ("d_df", df)]:
                    S.dma("GP", st_aux, lambda e, nm=nm, tile=tile: e.dma_start(
                        out=dbg[nm], in_=tile[:]), deps=[dv, ds])

        itA = iter(p0A_units)
        itB = iter(p0B_units)
        pump_state = {"dA": 0, "dB": 0, "scales": False}

        def do_pump(n):
            for _ in range(n):
                u = next(itA, None)
                if u is not None:
                    emit_p0A(u)
                    pump_state["dA"] += 1
                    continue
                if not pump_state["scales"]:
                    emit_p0_scales()
                    pump_state["scales"] = True
                    return
                u = next(itB, None)
                if u is None:
                    return
                emit_p0B(u)
                pump_state["dB"] += 1
        p0_pump[0] = do_pump

        pend_T = {0: emit_xpipe(0)}
        for qi in range(nch // 4):
            emit_quad(qi, pend_T.pop(qi), nch // 4)
        for u in itA:
            emit_p0A(u)
        if not pump_state["scales"]:
            emit_p0_scales()
        for u in itB:
            emit_p0B(u)

        BAR = [S.mark("T"), S.mark("V"), S.mark("S"),
               (st_p0s.sem, st_p0s.count), m_s1_stores[-1],
               (st_s1[0].sem, st_s1[0].count), (st_s1[1].sem, st_s1[1].count),
               (st_xt[0].sem, st_xt[0].count), (st_xt[1].sem, st_xt[1].count),
               (st_x[0].sem, st_x[0].count), (st_x[1].sem, st_x[1].count),
               (st_xb[0].sem, st_xb[0].count), (st_xb[1].sem, st_xb[1].count),
               (st_p0t[0].sem, st_p0t[0].count), (st_p0t[1].sem, st_p0t[1].count),
               (st_p0in[0].sem, st_p0in[0].count),
               (st_p0in[1].sem, st_p0in[1].count),
               (st_aux.sem, st_aux.count), (st_red.sem, st_red.count)]
        p01.close()

        p2 = ExitStack()
        E2 = p2.enter_context
        tile_ch = 4
        ntile = nch // tile_ch if nch >= tile_ch else 1
        s1e = [E2(nc.sbuf_tensor(f"s1e{i}", [C, D], F32)) for i in range(2)]
        s1x = [E2(nc.sbuf_tensor(f"s1x{i}", [C, D], F32)) for i in range(2)]
        xq = [E2(nc.sbuf_tensor(f"xq{i}", [C, D], BF16)) for i in range(4)]
        xqT = [E2(nc.sbuf_tensor(f"xqT{i}", [128, tile_ch * D], BF16))
               for i in range(2)]
        ht = E2(nc.sbuf_tensor("ht", [C, tile_ch * INNER], F32))
        wgub = [E2(nc.sbuf_tensor(f"wgub{i}", [128, 8192], BF16)) for i in range(2)]
        hq = [E2(nc.sbuf_tensor(f"hq{i}", [C, INNER], BF16)) for i in range(2)]
        hqT = E2(nc.sbuf_tensor("hqT", [128, tile_ch * INNER], BF16))
        wdpb = [E2(nc.sbuf_tensor(f"wdpb{i}", [128, 2 * D], BF16)) for i in range(2)]
        qzp = E2(nc.sbuf_tensor("qzp", [128, 512], F32))
        qcp = [E2(nc.sbuf_tensor(f"qcp{i}", [128, 512], I32)) for i in range(2)]
        am_parts = E2(nc.sbuf_tensor("am_parts", [C, 32], F32))
        r2t = [E2(nc.sbuf_tensor(f"r2t{i}", [C, D], F32)) for i in range(2)]
        am2 = E2(nc.sbuf_tensor("am2", [C, 4], F32))
        sv2 = E2(nc.sbuf_tensor("sv2", [C, 2], F32))
        dscl = E2(nc.sbuf_tensor("dscl", [C, 16], F32))
        bn6b = E2(nc.sbuf_tensor("bn6b", [C, 12], F32))
        bnmvb = E2(nc.sbuf_tensor("bnmvb", [C, 2], F32))
        rstd2 = E2(nc.sbuf_tensor("rstd2", [C, 1], F32))
        nmr2 = E2(nc.sbuf_tensor("nmr2", [C, 1], F32))
        pb2 = [E2(nc.psum_tensor(f"pb2_{i}", [128, 512], F32)) for i in range(8)]

        BAR1 = BAR[0]
        wgu_free = [BAR1, BAR1]
        wdp_free = [BAR1, BAR1]
        s1x_free = [BAR1, BAR1]
        s1e_free = [BAR1, BAR1]
        pb2free = [[BAR[1]]] * 8
        hq_free = [BAR1, BAR1]
        hqT_free = BAR1
        xq_free = [BAR1, BAR1, BAR1, BAR1]
        xqT_free = [BAR1, BAR1]
        r2_free = [BAR[2], BAR[2]]
        y_free = [(st_y[0].sem, 0), (st_y[1].sem, 0)]
        qtmp_free = BAR[1]
        p2first = list(BAR)
        wg_done = [p0_wdone.get(0), p0_wdone.get(1), p0_wdone.get(2)]
        s1_all = [BAR[5], BAR[6]]
        ln2_free = [BAR[2], BAR[2]]

        qsub = [0]
        qcp_free = [BAR[1], BAR[1]]

        def quant_to(src_ap, dst_ap, svcol, deps, width=512):
            nonlocal qtmp_free
            par = qsub[0]; qsub[0] ^= 1
            mzp = S.v(lambda e, src_ap=src_ap, svcol=svcol, width=width:
                      e.tensor_scalar(qzp[:, 0:width], src_ap, svcol,
                                      MRND - 0.5, ALU.mult, ALU.add),
                      deps=list(deps) + [qtmp_free])
            mcp = S.s(lambda e, src_ap=src_ap, svcol=svcol, width=width,
                      par=par:
                      e.activation(qcp[par][:, 0:width], src_ap, AF.Identity,
                                   bias=CHI, scale=svcol),
                      deps=list(deps) + [qcp_free[par]])
            mt1 = S.v(lambda e, width=width, par=par: e.tensor_tensor(
                qzp[:, 0:width], qzp[:, 0:width], qcp[par][:, 0:width],
                ALU.is_gt), deps=[mzp, mcp])
            mr = S.v(lambda e, dst_ap=dst_ap, width=width, par=par:
                     e.scalar_tensor_tensor(
                dst_ap, qcp[par][:, 0:width], -MRND, qzp[:, 0:width],
                ALU.add, ALU.add), deps=[mt1])
            qcp_free[par] = mr
            qtmp_free = mr
            return mr

        m_y_fin = []
        tile_state = {}

        def emit_xq_prep_q(ti, lo, hi):
            c0 = ti * tile_ch
            qm = {}
            for cc in range(lo, hi):
                ci = c0 + cc
                sb = cc % 2
                r0 = ci * C
                ms1 = S.dma("SY", st_s1l[sb], lambda e, r0=r0, sb=sb: e.dma_start(
                    out=s1x[sb][:], in_=s1_hbm[r0:r0 + C, :]),
                    deps=[s1x_free[sb]] + s1_all + p2first)
                src2 = s1x[sb]
                mam = S.v(lambda e, src2=src2: e.reduce_max(
                    am2[:, 0:1], src2[:], axis=AX.X, apply_absolute_value=True),
                    deps=[ms1] + p2first)
                mamc = S.v(lambda e: e.tensor_scalar(
                    am2[:, 1:2], am2[:, 0:1], 1e-5, None, ALU.max), deps=[mam])
                mrc = S.v(lambda e: e.reciprocal(sv2[:, 0:1], am2[:, 1:2]),
                          deps=[mamc])
                msc = S.v(lambda e: e.tensor_scalar(
                    sv2[:, 0:1], sv2[:, 0:1], 127.0, None, ALU.mult), deps=[mrc])
                S.v(lambda e, cc=cc: e.scalar_tensor_tensor(
                    dscl[:, cc:cc + 1], am2[:, 1:2], 1.0 / 127.0, swv[:, 4:5],
                    ALU.mult, ALU.mult), deps=[mamc] + sw_ready)
                S.v(lambda e, cc=cc: e.scalar_tensor_tensor(
                    dscl[:, 4 + cc:5 + cc], am2[:, 1:2], 1.0 / 127.0,
                    swv[:, 5:6], ALU.mult, ALU.mult), deps=[mamc])
                mq1 = quant_to(src2[:, 0:512], xq[cc % 4][:, 0:512],
                               sv2[:, 0:1], [msc, xq_free[cc % 4]])
                mq2 = quant_to(src2[:, 512:1024], xq[cc % 4][:, 512:1024],
                               sv2[:, 0:1], [msc])
                s1x_free[sb] = mq2
                qm[cc] = (mq1, mq2)
            return qm

        def emit_xq_prep_T(ti, lo, hi, qm):
            tp2 = ti % 2
            mxqt = None
            for cc in range(lo, hi):
                mq1, mq2 = qm[cc]
                lastmm = None
                for rnd in range(2):
                    bank = 4 + rnd
                    mm = None
                    for k in range(4):
                        blk = rnd * 4 + k
                        mm = S.t(lambda e, blk=blk, k=k, bank=bank, cc=cc:
                                 e.matmul(pb2[bank][:, k * 128:(k + 1) * 128],
                                          xq[cc % 4][:, blk * 128:(blk + 1) * 128],
                                          ident[:], start=(k == 0), stop=True),
                                 deps=[mq1, mq2, m_aux2] +
                                      (pb2free[bank] if k == 0 else []))
                    mxqt = S.s(lambda e, cc=cc, rnd=rnd, bank=bank, tp2=tp2:
                               e.copy(
                        xqT[tp2][:, cc * D + rnd * 512: cc * D + (rnd + 1) * 512],
                        pb2[bank][:]), deps=[mm, xqT_free[tp2]])
                    pb2free[bank] = [mxqt]
                    lastmm = mm
                xq_free[cc % 4] = lastmm
            return mxqt

        def emit_stageA(ti, m_xqT):
            tp2 = ti % 2
            h_marks = [None] * tile_ch
            for pcs in range(8):
                pbuf = pcs % 2
                mldg = S.dma("GP", st_wgu[pbuf],
                             lambda e, pcs=pcs, pbuf=pbuf: e.dma_start(
                                 out=wgub[pbuf].ap()[:, 0:4096].rearrange(
                                     "p (ic n) -> p ic n", ic=8),
                                 in_=wgT_hbm[:, :, pcs * 512:(pcs + 1) * 512]
                                 .transpose([1, 0, 2])),
                             deps=[wgu_free[pbuf], wg_done[0]] + p2first)
                mldu = S.dma("GP", st_wgu[pbuf],
                             lambda e, pcs=pcs, pbuf=pbuf: e.dma_start(
                                 out=wgub[pbuf].ap()[:, 4096:8192].rearrange(
                                     "p (ic n) -> p ic n", ic=8),
                                 in_=wuT_hbm[:, :, pcs * 512:(pcs + 1) * 512]
                                 .transpose([1, 0, 2])),
                             deps=[wg_done[1]])
                for cc in range(tile_ch):
                    bg = 4 + (cc % 2) * 2
                    bu = bg + 1
                    lg = lu = None
                    for ic in range(8):
                        first = ic == 0
                        lg = S.t(lambda e, ic=ic, cc=cc, pbuf=pbuf, bg=bg,
                                 first=first, tp2=tp2:
                                 e.matmul(pb2[bg][:],
                                          xqT[tp2][:, cc * D + ic * 128:
                                                   cc * D + (ic + 1) * 128],
                                          wgub[pbuf][:, ic * 512:(ic + 1) * 512],
                                          start=first, stop=(ic == 7)),
                                 deps=[m_xqT, mldu] +
                                      (pb2free[bg] if first else []))
                        lu = S.t(lambda e, ic=ic, cc=cc, pbuf=pbuf, bu=bu,
                                 first=first, tp2=tp2:
                                 e.matmul(pb2[bu][:],
                                          xqT[tp2][:, cc * D + ic * 128:
                                                   cc * D + (ic + 1) * 128],
                                          wgub[pbuf][:, 4096 + ic * 512:
                                                     4096 + (ic + 1) * 512],
                                          start=first, stop=(ic == 7)),
                                 deps=[mldu] + (pb2free[bu] if first else []))
                    hsl = slice(cc * INNER + pcs * 512,
                                cc * INNER + (pcs + 1) * 512)
                    pq = (tile_state[ti - 1]["qmarks"][cc][pcs]
                          if ti > 0 else BAR[1])
                    msil = S.s(lambda e, cc=cc, bg=bg, hsl=hsl, pq=pq:
                               e.activation(
                        ht[:, hsl], pb2[bg][:], AF.Silu, bias=CZERO,
                        scale=dscl[:, cc:cc + 1]), deps=[lg, pq])
                    pb2free[bg] = [msil]
                    mh = S.v(lambda e, cc=cc, pcs=pcs, bu=bu, hsl=hsl:
                             e.scalar_tensor_tensor(
                        ht[:, hsl],
                        pb2[bu][:], dscl[:, 4 + cc:5 + cc], ht[:, hsl],
                        ALU.mult, ALU.mult), deps=[msil, lu])
                    pb2free[bu] = [mh]
                    mam = S.v(lambda e, cc=cc, pcs=pcs: e.reduce_max(
                        am_parts[:, cc * 8 + pcs: cc * 8 + pcs + 1],
                        ht[:, cc * INNER + pcs * 512:
                           cc * INNER + (pcs + 1) * 512],
                        axis=AX.X, apply_absolute_value=True), deps=[mh])
                    h_marks[cc] = mam
                wgu_free[pbuf] = S.mark("T")
            tile_state[ti] = {"h_marks": h_marks}

        def emit_quant(ti, lo, hi):
            nonlocal hq_free
            h_marks = tile_state[ti]["h_marks"]
            qmarks = tile_state[ti].setdefault("qmarks", {})
            for cc in range(lo, hi):
                mam = S.v(lambda e, cc=cc: e.reduce_max(
                    am2[:, 2:3], am_parts[:, cc * 8:(cc + 1) * 8], axis=AX.X),
                    deps=[h_marks[cc]])
                mamc = S.v(lambda e: e.tensor_scalar(
                    am2[:, 3:4], am2[:, 2:3], 1e-5, None, ALU.max), deps=[mam])
                mrc = S.v(lambda e: e.reciprocal(sv2[:, 1:2], am2[:, 3:4]),
                          deps=[mamc])
                msc = S.v(lambda e: e.tensor_scalar(
                    sv2[:, 1:2], sv2[:, 1:2], 127.0, None, ALU.mult), deps=[mrc])
                dco = 8 + (ti % 2) * 4 + cc
                S.v(lambda e, dco=dco: e.scalar_tensor_tensor(
                    dscl[:, dco:dco + 1], am2[:, 3:4], 1.0 / 127.0,
                    swv[:, 6:7], ALU.mult, ALU.mult), deps=[mamc] + sw_ready)
                subm = []
                for sub in range(8):
                    subm.append(quant_to(
                        ht[:, cc * INNER + sub * 512:
                           cc * INNER + (sub + 1) * 512],
                        hq[cc % 2][:, sub * 512:(sub + 1) * 512], sv2[:, 1:2],
                        [msc, hq_free[cc % 2]]))
                qmarks[cc] = subm

        def emit_hqT(ti, lo, hi):
            qmarks = tile_state[ti]["qmarks"]
            hqt_marks = tile_state[ti].setdefault("hqt_marks", {})
            for cc in range(lo, hi):
                subm = qmarks[cc]
                lastmm = None
                mev = None
                for rnd in range(8):
                    bank = rnd % 2
                    mm = None
                    for k in range(4):
                        blk = rnd * 4 + k
                        mm = S.t(lambda e, blk=blk, k=k, bank=bank, cc=cc:
                                 e.matmul(pb2[bank][:, k * 128:(k + 1) * 128],
                                          hq[cc % 2][:, blk * 128:(blk + 1) * 128],
                                          ident[:], start=(k == 0), stop=True),
                                 deps=[subm[rnd], m_aux2] +
                                      (pb2free[bank] if k == 0 else []))
                    mev = S.s(lambda e, cc=cc, rnd=rnd, bank=bank: e.copy(
                        hqT[:, cc * INNER + rnd * 512:
                            cc * INNER + (rnd + 1) * 512],
                        pb2[bank][:]), deps=[mm, hqT_free])
                    pb2free[bank] = [mev]
                    lastmm = mm
                hq_free[cc % 2] = lastmm
                hqt_marks[cc] = mev

        def emit_down_ep(ti):
            nonlocal hqT_free
            c0 = ti * tile_ch
            hqt_marks = tile_state[ti]["hqt_marks"]
            lastd = {}
            for pcs in range(16):
                pbuf = pcs % 2
                mldd = S.dma("GP", st_wdp[pbuf],
                             lambda e, pcs=pcs, pbuf=pbuf: e.dma_start(
                                 out=wdpb[pbuf].ap().rearrange(
                                     "p (i2 o) -> p i2 o", i2=2),
                                 in_=wdT_hbm[pcs * 2:(pcs + 1) * 2, :, :]
                                 .transpose([1, 0, 2])),
                             deps=[wdp_free[pbuf], wg_done[2]] + p2first)
                for cc in range(tile_ch):
                    for i2l in range(2):
                        i2 = pcs * 2 + i2l
                        for dh in range(2):
                            bi = cc * 2 + dh
                            first = pcs == 0 and i2l == 0
                            lastd[bi] = S.t(
                                lambda e, cc=cc, i2=i2, i2l=i2l, dh=dh,
                                bi=bi, pbuf=pbuf, first=first:
                                e.matmul(pb2[bi][:],
                                         hqT[:, cc * INNER + i2 * 128:
                                             cc * INNER + (i2 + 1) * 128],
                                         wdpb[pbuf][:, i2l * D + dh * 512:
                                                    i2l * D + dh * 512 + 512],
                                         start=first,
                                         stop=(pcs == 15 and i2l == 1)),
                                deps=[hqt_marks[cc], mldd] +
                                     (pb2free[bi] if first else []))
                wdp_free[pbuf] = S.mark("T")
            for cc in range(tile_ch):
                ci = c0 + cc
                r0 = ci * C
                eb = ci % 2
                yb = ci % 2
                ms1 = S.dma("SY", st_s1e[eb], lambda e, r0=r0, eb=eb:
                            e.dma_start(out=s1e[eb][:],
                                        in_=s1_hbm[r0:r0 + C, :]),
                            deps=[s1e_free[eb]] + s1_all)
                dco = 8 + (ti % 2) * 4 + cc
                md0 = S.s(lambda e, cc=cc, eb=eb, dco=dco: e.activation(
                    r2t[eb][:, 0:512], pb2[cc * 2][:], AF.Identity,
                    bias=CZERO, scale=dscl[:, dco:dco + 1]),
                    deps=[lastd[cc * 2], r2_free[eb]])
                pb2free[cc * 2] = [md0]
                md1 = S.s(lambda e, cc=cc, eb=eb, dco=dco: e.activation(
                    r2t[eb][:, 512:1024], pb2[cc * 2 + 1][:], AF.Identity,
                    bias=CZERO, scale=dscl[:, dco:dco + 1]),
                    deps=[lastd[cc * 2 + 1]])
                pb2free[cc * 2 + 1] = [md1]
                mr2 = S.v(lambda e, eb=eb: e.tensor_add(
                    r2t[eb][:], r2t[eb][:], s1e[eb][:]),
                    deps=[md0, md1, ms1])
                s1e_free[eb] = mr2
                mb1 = S.v(lambda e, eb=eb: e.bn_stats(
                    bn6b[:, 0:6], r2t[eb][:, 0:512]), deps=[mr2])
                mb2 = S.v(lambda e, eb=eb: e.bn_stats(
                    bn6b[:, 6:12], r2t[eb][:, 512:1024]), deps=[mr2])
                mb3 = S.v(lambda e: e.bn_aggr(bnmvb[:], bn6b[:]),
                          deps=[mb1, mb2, ln2_free[0]])
                msq = S.s(lambda e: e.activation(
                    rstd2[:], bnmvb[:, 1:2], AF.Sqrt, bias=CEPS, scale=1.0),
                    deps=[mb3])
                mrc2 = S.v(lambda e: e.reciprocal(rstd2[:], rstd2[:]),
                           deps=[msq])
                mnm2 = S.v(lambda e: e.scalar_tensor_tensor(
                    nmr2[:], bnmvb[:, 0:1], -1.0, rstd2[:],
                    ALU.mult, ALU.mult), deps=[mrc2, ln2_free[1]])
                mact = S.s(lambda e, eb=eb: e.activation(
                    r2t[eb][:], r2t[eb][:], AF.Identity, bias=nmr2[:],
                    scale=rstd2[:]), deps=[mnm2, y_free[yb]])
                ln2_free[0] = msq
                ln2_free[1] = mact
                mys = S.dma("SY", st_y[yb], lambda e, r0=r0, eb=eb:
                            e.dma_start(out=y_d[r0:r0 + C, :],
                                        in_=r2t[eb][:]), deps=[mact])
                y_free[yb] = mys
                r2_free[eb] = mys
                m_y_fin.append(mys)
            hqT_free = S.mark("T")
            xqT_free[ti % 2] = S.mark("T")

        prev_qm = None
        for ti in range(ntile):
            qm = prev_qm if prev_qm is not None else emit_xq_prep_q(ti, 0, 4)
            mx = emit_xq_prep_T(ti, 0, 4, qm)
            emit_stageA(ti, mx)
            emit_quant(ti, 0, 2)
            if ti > 0:
                emit_down_ep(ti - 1)
            emit_hqT(ti, 0, 2)
            emit_quant(ti, 2, 4)
            if ti + 1 < ntile:
                prev_qm = emit_xq_prep_q(ti + 1, 0, 4)
            emit_hqT(ti, 2, 4)
            p2first = []
        emit_down_ep(ntile - 1)

        def fin(e):
            return e.nop()
        S.ops["SY"].append((S._waits("SY", m_y_fin[-2:] +
                                     [(st_y[0].sem, st_y[0].count),
                                      (st_y[1].sem, st_y[1].count)]),
                            fin, None))

        blk = E(nc.Block())

        def runner(key):
            def go(eng):
                for waits, fn, inc in S.ops[key]:
                    for (sem, val) in waits:
                        eng.wait_ge(sem, val)
                    ins = fn(eng)
                    if inc is not None:
                        ins.then_inc(inc[0], inc[1])
            return go

        blk.tensor(runner("T"))
        blk.vector(runner("V"))
        blk.scalar(runner("S"))
        blk.sync(runner("SY"))
        blk.gpsimd(runner("GP"))

        p2.close()
    return nc


_NC_CACHE = {}


def _get_nc():
    if "nc" not in _NC_CACHE:
        _NC_CACHE["nc"] = build_program()
    return _NC_CACHE["nc"]


def kernel(L0_attn_x, L0_attn_W_q, L0_attn_W_k, L0_attn_W_v, L0_attn_W_o,
           L0_ffn_w_gate, L0_ffn_w_up, L0_ffn_w_down):
    x = np.ascontiguousarray(np.asarray(L0_attn_x, dtype=np.float32))
    wq = np.ascontiguousarray(np.asarray(L0_attn_W_q, dtype=np.float32))
    wk = np.ascontiguousarray(np.asarray(L0_attn_W_k, dtype=np.float32))
    wv = np.ascontiguousarray(np.asarray(L0_attn_W_v, dtype=np.float32))
    wo = np.ascontiguousarray(np.asarray(L0_attn_W_o, dtype=np.float32))
    wg = np.ascontiguousarray(np.asarray(L0_ffn_w_gate, dtype=np.float32))
    wu = np.ascontiguousarray(np.asarray(L0_ffn_w_up, dtype=np.float32))
    wd = np.ascontiguousarray(np.asarray(L0_ffn_w_down, dtype=np.float32))

    tril = np.tril(np.ones((C, C), np.float32))
    trium = np.tile(tril.T, (1, 4)).astype(ml_dtypes.bfloat16)
    ident = np.eye(C, dtype=np.float32).astype(ml_dtypes.bfloat16)

    nc = _get_nc()
    in_maps = []
    for b in range(NCORE):
        in_maps.append(dict(x=x[b], wq=wq, wk=wk, wv=wv, wo=wo,
                            wg=wg, wu=wu, wd=wd, trium=trium, ident=ident))
    res = run_bass_kernel_spmd(nc, in_maps, core_ids=list(range(NCORE)))
    return np.stack([res.results[b]["y"] for b in range(NCORE)], axis=0)


# version 22
# speedup vs baseline: 1.0410x; 1.0410x over previous
import numpy as np
import ml_dtypes
from contextlib import ExitStack

import concourse.bass as bass
import concourse.mybir as mybir
from concourse.bass_utils import run_bass_kernel_spmd

F32 = mybir.dt.float32
BF16 = mybir.dt.bfloat16
I32 = mybir.dt.int32
AF = mybir.ActivationFunctionType
ALU = mybir.AluOpType
AX = mybir.AxisListType

D = 1024
T = 4096
C = 128
H = 16
DK = 64
INNER = 4096
MRND = 256.0
NCORE = 8


class Sched:

    def __init__(self, nc):
        self.nc = nc
        self.ops = {k: [] for k in ("T", "V", "S", "SY", "GP")}
        self.cnt = {"T": 0, "V": 0, "S": 0, "G": 0}
        self.lastwait = {}
        self.sems = {}

    def set_engine_sems(self, semT, semV, semS, semG=None):
        self.sems = {"T": semT, "V": semV, "S": semS, "G": semG}

    def _waits(self, key, deps):
        out = []
        for dp in deps:
            if dp is None:
                continue
            sem, val = dp
            if val <= 0:
                continue
            k = (key, id(sem))
            if self.lastwait.get(k, -1) < val:
                self.lastwait[k] = val
                out.append((sem, val))
        return out

    def _emit(self, key, fn, deps, inc):
        self.ops[key].append((self._waits(key, deps), fn, inc))

    def t(self, fn, deps=()):
        self.cnt["T"] += 1
        self._emit("T", fn, deps, (self.sems["T"], 1))
        return (self.sems["T"], self.cnt["T"])

    def v(self, fn, deps=()):
        self.cnt["V"] += 1
        self._emit("V", fn, deps, (self.sems["V"], 1))
        return (self.sems["V"], self.cnt["V"])

    def s(self, fn, deps=()):
        self.cnt["S"] += 1
        self._emit("S", fn, deps, (self.sems["S"], 1))
        return (self.sems["S"], self.cnt["S"])

    def g(self, fn, deps=()):
        self.cnt["G"] += 1
        self._emit("GP", fn, deps, (self.sems["G"], 1))
        return (self.sems["G"], self.cnt["G"])

    def dma(self, q, stream, fn, deps=()):
        stream.count += 16
        self._emit(q, fn, deps, (stream.sem, 16))
        return (stream.sem, stream.count)

    def mark(self, key):
        return (self.sems[key], self.cnt[key])


class Stream:
    def __init__(self, sem):
        self.sem = sem
        self.count = 0


def build_program(nch=T // C, ntile=8, tile_ch=4, debug=False):
    nc = bass.Bass("TRN2", target_bir_lowering=False)

    x_d = nc.dram_tensor("x", [T, D], F32, kind="ExternalInput").ap()
    wq_d = nc.dram_tensor("wq", [D, D], F32, kind="ExternalInput").ap()
    wk_d = nc.dram_tensor("wk", [D, D], F32, kind="ExternalInput").ap()
    wv_d = nc.dram_tensor("wv", [D, D], F32, kind="ExternalInput").ap()
    wo_d = nc.dram_tensor("wo", [D, D], F32, kind="ExternalInput").ap()
    wg_d = nc.dram_tensor("wg", [INNER, D], F32, kind="ExternalInput").ap()
    wu_d = nc.dram_tensor("wu", [INNER, D], F32, kind="ExternalInput").ap()
    wd_d = nc.dram_tensor("wd", [D, INNER], F32, kind="ExternalInput").ap()
    trium_d = nc.dram_tensor("trium", [C, 4 * C], BF16, kind="ExternalInput").ap()
    ident_d = nc.dram_tensor("ident", [C, C], BF16, kind="ExternalInput").ap()
    y_d = nc.dram_tensor("y", [T, D], F32, kind="ExternalOutput").ap()

    s1_hbm = nc.dram_tensor("s1_hbm", [T, D], F32, kind="ExternalOutput" if debug else "Internal").ap()
    wgT_hbm = nc.dram_tensor("wgT_hbm", [8, 128, INNER], BF16, kind="Internal").ap()
    wuT_hbm = nc.dram_tensor("wuT_hbm", [8, 128, INNER], BF16, kind="Internal").ap()
    wdT_hbm = nc.dram_tensor("wdT_hbm", [32, 128, D], BF16, kind="Internal").ap()
    red_hbm = nc.dram_tensor("red_hbm", [128, 4], F32, kind="Internal").ap()
    dbg = {}
    if debug:
        for nm, shp in [("d_qt", [128, 8 * 4 * C]), ("d_kt", [128, 8 * 4 * C]),
                        ("d_kc", [C, 4 * D]), ("d_vc", [C, 4 * D]),
                        ("d_ot", [128, 8 * C]),
                        ("d_uts", [C, 16 * C]), ("d_bts", [C, 16 * C]),
                        ("d_ats", [C, 16 * C]), ("d_rts", [128, 8 * C]),
                        ("d_x2", [C, H * DK]), ("d_skf", [128, H * DK]),
                        ("d_df", [128, H * DK]), ("d_attn", [C, D])]:
            dbg[nm] = nc.dram_tensor(nm, shp, F32, kind="ExternalOutput").ap()
        for nm, shp in [("d_h", [C, INNER]), ("d_hq", [C, INNER]),
                        ("d_am", [C, 4]), ("d_xq", [C, D]), ("d_dsc", [C, 12])]:
            dbg[nm] = nc.dram_tensor(nm, shp, F32, kind="ExternalOutput").ap()
    sw_hbm = nc.dram_tensor("sw_hbm", [8, 1], F32, kind="Internal").ap()

    S = Sched(nc)
    NG = 4

    with ExitStack() as st:
        E = st.enter_context
        semT = E(nc.semaphore("semT"))
        semV = E(nc.semaphore("semV"))
        semS = E(nc.semaphore("semS"))
        semG = E(nc.semaphore("semG"))
        S.set_engine_sems(semT, semV, semS, semG)

        def mkstream(name):
            return Stream(E(nc.semaphore(name)))
        st_aux = mkstream("st_aux")
        st_x = [mkstream("st_x0"), mkstream("st_x1")]
        st_xb = [mkstream("st_xb0"), mkstream("st_xb1")]
        st_xt = [mkstream("st_xt0"), mkstream("st_xt1")]
        st_s1 = [mkstream("st_s10"), mkstream("st_s11")]
        st_p0in = [mkstream("st_p0a"), mkstream("st_p0b")]
        st_p0t = [mkstream("st_p0t0"), mkstream("st_p0t1")]
        st_p0s = mkstream("st_p0s")
        st_red = mkstream("st_red")
        st_s1l = [mkstream("st_s1l0"), mkstream("st_s1l1")]
        st_xqt = [mkstream("st_xqt0"), mkstream("st_xqt1")]
        st_hqt = [mkstream(f"st_hqt{i}") for i in range(4)]
        st_wgu = [mkstream("st_wgu0"), mkstream("st_wgu1")]
        st_wdp = [mkstream("st_wdp0"), mkstream("st_wdp1")]
        st_s1e = [mkstream("st_s1e0"), mkstream("st_s1e1")]
        st_y = [mkstream("st_y0"), mkstream("st_y1")]
        st_ot = [mkstream("st_ot0"), mkstream("st_ot1")]

        trium = E(nc.sbuf_tensor("trium_sb", [C, 4 * C], BF16))
        ident = E(nc.sbuf_tensor("ident_sb", [C, C], BF16))
        cons = E(nc.sbuf_tensor("cons", [128, 4], F32))
        swv = E(nc.sbuf_tensor("swv", [128, 8], F32))
        CEPS = cons[:, 0:1]
        CLO = cons[:, 1:2]
        CHI = cons[:, 2:3]
        CZERO = cons[:, 3:4]
        CNHALF = swv[:, 7:8]

        m_aux1 = S.dma("SY", st_aux, lambda e: e.dma_start(out=trium[:], in_=trium_d))
        m_aux2 = S.dma("SY", st_aux, lambda e: e.dma_start(out=ident[:], in_=ident_d))
        S.v(lambda e: e.memset(cons[:, 0:1], 1e-5))
        S.v(lambda e: e.memset(cons[:, 1:2], MRND - 0.5))
        S.v(lambda e: e.memset(cons[:, 2:3], MRND))
        S.v(lambda e: e.memset(cons[:, 3:4], 0.0))
        mk_cons = S.v(lambda e: e.memset(swv[:, 7:8], -0.5))

        p01 = ExitStack()
        E1 = p01.enter_context
        p0in = [E1(nc.sbuf_tensor(f"p0in{i}", [128, D], F32)) for i in range(2)]
        p0bf = [E1(nc.sbuf_tensor(f"p0bf{i}", [128, D], BF16)) for i in range(2)]
        p0t1 = E1(nc.sbuf_tensor("p0t1", [128, 512], F32))
        p0tr = [E1(nc.sbuf_tensor(f"p0tr{i}", [128, 1024], BF16)) for i in range(1)]
        p0acc = E1(nc.sbuf_tensor("p0acc", [128, 4], F32))
        p0red = E1(nc.sbuf_tensor("p0red", [1, 128], F32))
        p0sw = E1(nc.sbuf_tensor("p0sw", [1, 8], F32))

        wq_sb = E1(nc.sbuf_tensor("wq_sb", [128, 8 * D], BF16))
        wk_sb = E1(nc.sbuf_tensor("wk_sb", [128, 8 * D], BF16))
        wv_sb = E1(nc.sbuf_tensor("wv_sb", [128, 8 * D], BF16))
        wo_sb = E1(nc.sbuf_tensor("wo_sb", [128, 8 * D], BF16))
        x_c = [E1(nc.sbuf_tensor(f"x_c{i}", [C, D], F32)) for i in range(5)]
        xb_c = [E1(nc.sbuf_tensor(f"xb_c{i}", [C, D], BF16)) for i in range(2)]
        xT2 = [E1(nc.sbuf_tensor(f"xT2_{i}", [128, 4 * D], BF16)) for i in range(2)]
        qs_all = E1(nc.sbuf_tensor("qs_all", [128, 8 * 4 * C], BF16))
        ks_all = E1(nc.sbuf_tensor("ks_all", [128, 8 * 4 * C], BF16))
        qpd_all = E1(nc.sbuf_tensor("qpd_all", [128, 8 * 4 * 2 * C], BF16))
        k_c2 = E1(nc.sbuf_tensor("k_c2", [C, 4 * D], BF16))
        v_c2 = E1(nc.sbuf_tensor("v_c2", [C, 4 * D], BF16))
        bts_a = E1(nc.sbuf_tensor("bts_a", [C, 16 * C], BF16))
        ats_a = E1(nc.sbuf_tensor("ats_a", [C, 16 * C], BF16))
        uts_a = E1(nc.sbuf_tensor("uts_a", [C, 16 * C], BF16))
        rts_a = E1(nc.sbuf_tensor("rts_a", [128, 8 * C], BF16))
        x2s_a = E1(nc.sbuf_tensor("x2s_a", [C, H * DK], BF16))
        o_c = [E1(nc.sbuf_tensor(f"o_c{i}", [C, D], BF16)) for i in range(1)]
        ot_c = [E1(nc.sbuf_tensor(f"ot_c{i}", [128, 8 * C], BF16)) for i in range(1)]
        skf = E1(nc.sbuf_tensor("skf", [128, H * DK], F32))
        df = E1(nc.sbuf_tensor("df", [128, H * DK], F32))
        kks = E1(nc.sbuf_tensor("kks", [128, 512], F32))
        dds = E1(nc.sbuf_tensor("dds", [128, 512], F32))
        skb = E1(nc.sbuf_tensor("skb", [128, H * DK], BF16))
        db = E1(nc.sbuf_tensor("db", [128, H * DK], BF16))
        bn6 = E1(nc.sbuf_tensor("bn6", [C, 12], F32))
        bnmv = E1(nc.sbuf_tensor("bnmv", [C, 2], F32))
        rstd = E1(nc.sbuf_tensor("rstd", [C, 1], F32))
        nmr = E1(nc.sbuf_tensor("nmr", [C, 1], F32))
        pb = [E1(nc.psum_tensor(f"pb{i}", [128, 512], F32)) for i in range(8)]

        def ld_w(sb, dr):
            return S.dma("GP", st_aux, lambda e, sb=sb, dr=dr: e.dma_start(
                out=sb.ap().rearrange("p (ic c) -> p ic c", ic=8),
                in_=dr.rearrange("(ic p) c -> p ic c", p=128)))
        ld_w(wq_sb, wq_d)
        ld_w(wk_sb, wk_d)
        ld_w(wv_sb, wv_d)
        ld_w(wo_sb, wo_d)
        m_aux_all = (st_aux.sem, st_aux.count)

        m_st0 = [S.v(lambda e, t=t_: e.memset(t[:], 0.0))
                 for t_ in (skf, df, skb, db, qpd_all)]
        w_dep = [m_aux_all, m_aux1, mk_cons] + m_st0
        m_wo = m_aux_all

        W3 = [(wg_d, wgT_hbm), (wu_d, wuT_hbm), (wd_d, wdT_hbm)]
        p0A_units = []
        p0B_units = []
        for wi, (wdr, _) in enumerate(W3):
            rows, cols = wdr.shape[0], wdr.shape[1]
            for b in range(rows // 128):
                for ss in range(cols // D):
                    p0A_units.append((wi, b, ss))
                    p0B_units.append((wi, b, ss))

        p0st = {
            "ldbuf": 0, "use": [(semV, 0), (semV, 0)],
            "accfirst": [True, True, True], "Adone": (semV, 0),
            "trbuf": 0, "tr_free": [(st_p0s.sem, 0), (st_p0s.sem, 0)],
            "bank": 0,
            "bfbuf": 0, "bf_free": [[(semT, 0)], [(semT, 0)]],
            "qfree": (semV, 0),
        }
        sw_ready = []
        p0_wdone = {}

        def emit_p0A(u):
            wi, b, ss = u
            buf = p0st["ldbuf"]; p0st["ldbuf"] ^= 1
            wdr = W3[wi][0]
            ldm = S.dma("GP", st_p0in[buf],
                        lambda e, wdr=wdr, b=b, ss=ss, buf=buf: e.dma_start(
                            out=p0in[buf][:],
                            in_=wdr[b * 128:(b + 1) * 128, ss * D:(ss + 1) * D]),
                        deps=[p0st["use"][buf]])
            m1 = S.v(lambda e, buf=buf: e.reduce_sum(
                p0acc[:, 3:4], p0in[buf][:], axis=AX.X, apply_absolute_value=True),
                deps=[ldm])
            if p0st["accfirst"][wi]:
                p0st["accfirst"][wi] = False
                m2 = S.v(lambda e, wi=wi: e.tensor_copy(
                    p0acc[:, wi:wi + 1], p0acc[:, 3:4]), deps=[m1])
            else:
                m2 = S.v(lambda e, wi=wi: e.tensor_add(
                    p0acc[:, wi:wi + 1], p0acc[:, wi:wi + 1], p0acc[:, 3:4]), deps=[m1])
            p0st["use"][buf] = m2
            p0st["Adone"] = m2

        def emit_p0_scales():
            mst = S.dma("SY", st_red, lambda e: e.dma_start(
                out=red_hbm[:, 0:3], in_=p0acc[:, 0:3]), deps=[p0st["Adone"]])
            for wi in range(3):
                nel = float(W3[wi][0].shape[0] * W3[wi][0].shape[1])
                mld = S.dma("SY", st_red, lambda e, wi=wi: e.dma_start(
                    out=p0red[:], in_=red_hbm[:, wi:wi + 1].transpose([1, 0]),
                    allow_slow_non_contiguous=True),
                    deps=[mst, (semV, S.cnt["V"])])
                mr = S.v(lambda e: e.reduce_sum(p0sw[0:1, 6:7], p0red[:], axis=AX.X),
                         deps=[mld])
                mm = S.v(lambda e, nel=nel, wi=wi: e.tensor_scalar(
                    p0sw[0:1, 4 + (wi % 2) * 0 + 0:5] if False else p0sw[0:1, 6:7],
                    p0sw[0:1, 6:7], 1.0 / nel, 1e-5, ALU.mult, ALU.max), deps=[mr])
                mcpy = S.v(lambda e, wi=wi: e.tensor_copy(
                    p0sw[0:1, 3 + wi:4 + wi], p0sw[0:1, 6:7]), deps=[mm])
                mrc = S.v(lambda e, wi=wi: e.reciprocal(
                    p0sw[0:1, wi:wi + 1], p0sw[0:1, 6:7]), deps=[mcpy])
                mstw = S.dma("SY", st_red, lambda e, wi=wi: e.dma_start(
                    out=sw_hbm[wi:wi + 1, :], in_=p0sw[0:1, wi:wi + 1]), deps=[mrc])
                mstm = S.dma("SY", st_red, lambda e, wi=wi: e.dma_start(
                    out=sw_hbm[4 + wi:5 + wi, :], in_=p0sw[0:1, 3 + wi:4 + wi]),
                    deps=[mrc])
                mb1 = S.dma("SY", st_red, lambda e, wi=wi: e.dma_start(
                    out=swv[:, wi:wi + 1],
                    in_=sw_hbm[wi:wi + 1, 0:1].partition_broadcast(128)), deps=[mstw])
                mb2 = S.dma("SY", st_red, lambda e, wi=wi: e.dma_start(
                    out=swv[:, 4 + wi:5 + wi],
                    in_=sw_hbm[4 + wi:5 + wi, 0:1].partition_broadcast(128)),
                    deps=[mstm])
                sw_ready.append(mb1)
                sw_ready.append(mb2)

        def emit_p0B(u):
            wi, b, ss = u
            buf = p0st["ldbuf"]; p0st["ldbuf"] ^= 1
            wdr, wtd = W3[wi]
            ldm = S.dma("GP", st_p0in[buf],
                        lambda e, wdr=wdr, b=b, ss=ss, buf=buf: e.dma_start(
                            out=p0in[buf][:],
                            in_=wdr[b * 128:(b + 1) * 128, ss * D:(ss + 1) * D]),
                        deps=[p0st["use"][buf]])
            deps = [ldm, p0st["qfree"]] + sw_ready
            bfb = p0st["bfbuf"]; p0st["bfbuf"] ^= 1
            mq = None
            for sp in range(2):
                sl = slice(sp * 512, (sp + 1) * 512)
                mt1 = S.v(lambda e, wi=wi, buf=buf, sl=sl, bfb=bfb:
                          e.tensor_scalar(
                    p0bf[bfb][:, sl], p0in[buf][:, sl], swv[:, wi:wi + 1], 0.5,
                    ALU.mult, ALU.is_gt),
                    deps=deps + [p0st["qfree"]] + p0st["bf_free"][bfb])
                mt2 = S.v(lambda e, wi=wi, buf=buf, sl=sl: e.tensor_scalar(
                    p0t1[:], p0in[buf][:, sl], swv[:, wi:wi + 1], -0.5,
                    ALU.mult, ALU.is_lt), deps=deps)
                mq = S.v(lambda e, bfb=bfb, sl=sl: e.tensor_sub(
                    p0bf[bfb][:, sl], p0bf[bfb][:, sl], p0t1[:]),
                    deps=[mt1, mt2])
                p0st["qfree"] = mq
            p0st["use"][buf] = mq
            stg = 0
            ev_marks = []
            lastmm = None
            for rnd in range(2):
                bank = p0st["bank"]; p0st["bank"] = (p0st["bank"] + 1) % 8
                mm = None
                for k in range(4):
                    blk = rnd * 4 + k
                    mm = S.t(lambda e, bfb=bfb, blk=blk, k=k, bank=bank:
                             e.matmul(pb[bank][:, k * 128:(k + 1) * 128],
                                      p0bf[bfb][:, blk * 128:(blk + 1) * 128],
                                      ident[:], start=(k == 0), stop=True),
                             deps=[mq, m_aux2] +
                                  (pbfree[bank] if k == 0 else []))
                mev = S.s(lambda e, stg=stg, rnd=rnd, bank=bank: e.copy(
                    p0tr[stg][:, rnd * 512:(rnd + 1) * 512], pb[bank][:]),
                    deps=[mm, p0st["tr_free"][stg]])
                pbfree[bank] = [mev]
                ev_marks.append(mev)
                lastmm = mm
            if wi < 2:
                dst = wtd[0:8, :, b * 128:(b + 1) * 128]
            else:
                dst = wtd[ss * 8:ss * 8 + 8, :, b * 128:(b + 1) * 128]
            mstb = S.dma("GP", st_p0s,
                         lambda e, dst=dst, stg=stg: e.dma_start(
                             out=dst.transpose([1, 0, 2]),
                             in_=p0tr[stg].ap().rearrange(
                                 "p (ic q) -> p ic q", ic=8)),
                         deps=ev_marks)
            p0st["tr_free"][stg] = (st_p0s.sem, st_p0s.count)
            p0st["bf_free"][bfb] = [lastmm]
            p0_wdone[wi] = mstb

        xuse = [(semV, 0)] * 5
        xbuse = [(st_xt[0].sem, 0), (st_xt[1].sem, 0),
                 (st_xt[0].sem, 0), (st_xt[1].sem, 0)]
        xtuse = [(semT, 0), (semT, 0)]
        pbfree = [[(semV, 0)]] * 8
        st_cast = [(semV, 0)]
        scrfree = [(semV, 0)] * 8
        ln_free = [(semS, 0), (semS, 0)]
        ot_free = [(semT, 0), (semT, 0)]
        oc_free = [(st_ot[0].sem, 0), (st_ot[1].sem, 0)]
        s1_free = (st_s1[0].sem, 0)
        m_s1_stores = []

        def st_dep():
            return st_cast[0]

        p0_pump = [lambda n: None]

        def emit_xpipe(qi):
            qp = qi % 2
            mxts = []
            for half in range(4):
                ci = 4 * qi + half
                bx = ci % 2
                r0 = ci * C
                mxb = S.dma("GP", st_xb[ci % 2],
                            lambda e, r0=r0, bx=bx: e.dma_start(
                                out=xb_c[bx][:], in_=x_d[r0:r0 + C, :]),
                            deps=[xbuse[bx]])
                mxt = None
                for ic in range(8):
                    mxt = S.dma("SY", st_xt[ci % 2],
                                lambda e, ic=ic, bx=bx, qp=qp, half=half:
                                e.dma_start(
                                    out=xT2[qp][:, ic * 512 + half * 128:
                                                ic * 512 + half * 128 + 128],
                                    in_=xb_c[bx][:, ic * 128:(ic + 1) * 128],
                                    transpose=True),
                                deps=[mxb, xtuse[qp]])
                xbuse[bx] = mxt
                mxts.append(mxt)
            return mxts

        def emit_quad(qi, mxts, nq):
            qp = qi % 2
            mxs = []
            for half in range(4):
                ci = 4 * qi + half
                bi8 = ci % 5
                r0 = ci * C
                mxs.append(S.dma("SY", st_x[ci % 2],
                                 lambda e, r0=r0, bi8=bi8: e.dma_start(
                                     out=x_c[bi8][:], in_=x_d[r0:r0 + C, :]),
                                 deps=[xuse[bi8]]))
            dep0 = mxts + w_dep

            for r in range(4):
                lastb = {}
                for qsel in (True, False):
                    wsb = wq_sb if qsel else wk_sb
                    for pp2 in range(2):
                        p = r * 2 + pp2
                        bi = (0 if qsel else 4) + (r % 2) * 2 + pp2
                        for ic in range(8):
                            first = ic == 0
                            lastb[bi] = S.t(
                                lambda e, wsb=wsb, ic=ic, p=p, bi=bi,
                                first=first, qp=qp:
                                e.matmul(pb[bi][:, :],
                                         wsb[:, ic * D + p * 128:
                                             ic * D + p * 128 + 128],
                                         xT2[qp][:, ic * 512:(ic + 1) * 512],
                                         start=first, stop=(ic == 7)),
                                deps=dep0 + (pbfree[bi] if first else []))
                for pp2 in range(2):
                    p = r * 2 + pp2
                    bi = (r % 2) * 2 + pp2
                    mq1 = S.s(lambda e, p=p, bi=bi: e.copy(
                        qs_all[:, p * 512:p * 512 + 512], pb[bi][:]),
                        deps=[lastb[bi]])
                    mq2 = S.s(lambda e, p=p, bi=bi: e.copy(
                        qpd_all[0:64, p * 1024:(p + 1) * 1024].rearrange(
                            "p (c w) -> p c w", c=4)[:, :, 0:128],
                        pb[bi][0:64, :].rearrange("p (c w) -> p c w", c=4)),
                        deps=[lastb[bi]])
                    mq3 = S.s(lambda e, p=p, bi=bi: e.copy(
                        qpd_all[64:128, p * 1024:(p + 1) * 1024].rearrange(
                            "p (c w) -> p c w", c=4)[:, :, 128:256],
                        pb[bi][64:128, :].rearrange("p (c w) -> p c w", c=4)),
                        deps=[lastb[bi]])
                    pbfree[bi] = [mq3]
                    bk = 4 + (r % 2) * 2 + pp2
                    mk1 = S.s(lambda e, p=p, bk=bk: e.copy(
                        ks_all[:, p * 512:p * 512 + 512], pb[bk][:]),
                        deps=[lastb[bk]])
                    pbfree[bk] = [mk1]
            m_qk = S.mark("S")
            p0_pump[0](2)

            for sub in range(2):
                lastkv = {}
                for h2 in range(2):
                    half = sub * 2 + h2
                    for vsel in (False, True):
                        wsb = wv_sb if vsel else wk_sb
                        boff = h2 * 4 + (2 if vsel else 0)
                        for dhalf in range(2):
                            bi = boff + dhalf
                            for ic in range(8):
                                first = ic == 0
                                lastkv[bi] = S.t(
                                    lambda e, wsb=wsb, ic=ic, dhalf=dhalf,
                                    bi=bi, qp=qp, half=half, first=first:
                                    e.matmul(pb[bi][:, :],
                                             xT2[qp][:, ic * 512 + half * 128:
                                                     ic * 512 + half * 128 + 128],
                                             wsb[:, ic * D + dhalf * 512:
                                                 ic * D + dhalf * 512 + 512],
                                             start=first, stop=(ic == 7)),
                                    deps=dep0 + (pbfree[bi] if first else []))
                for bi in range(8):
                    h2 = bi // 4
                    half = sub * 2 + h2
                    vsel = (bi % 4) >= 2
                    dhalf = bi % 2
                    dstt = v_c2 if vsel else k_c2
                    co = half * D + dhalf * 512
                    pbfree[bi] = [S.s(lambda e, dstt=dstt, co=co, bi=bi:
                                      e.copy(dstt[:, co:co + 512], pb[bi][:]),
                                      deps=[lastkv[bi]])]
            xtuse[qp] = S.mark("T")
            m_kv = S.mark("S")
            if qi + 1 < nq:
                pend_T[qi + 1] = emit_xpipe(qi + 1)
            p0_pump[0](2)
            for half in range(4):
                emit_heads(4 * qi + half, half, m_qk, m_kv, mxs)
                p0_pump[0](2)
                emit_wo_ln(4 * qi + half, m_qk, m_kv)
                p0_pump[0](3)

        hstate = {}

        def emit_heads(ci, half, m_qk, m_kv, mxs):
            bi4 = ci % 5
            r0 = ci * C
            koff = half * D
            qoff = half * 128

            def qs(p):
                return qs_all[:, p * 512 + qoff: p * 512 + qoff + 128]

            def ks(p):
                return ks_all[:, p * 512 + qoff: p * 512 + qoff + 128]

            def qpd(p):
                return qpd_all[:, p * 1024 + half * 256:
                               p * 1024 + half * 256 + 256]

            def kc(h):
                return k_c2[:, koff + h * DK: koff + (h + 1) * DK]

            def vc(h):
                return v_c2[:, koff + h * DK: koff + (h + 1) * DK]

            def kcp(p):
                return k_c2[:, koff + p * 128: koff + p * 128 + 128]

            def vcp(p):
                return v_c2[:, koff + p * 128: koff + p * 128 + 128]

            def skbp(p):
                return skb[:, p * 128:(p + 1) * 128]

            def dbp(p):
                return db[:, p * 128:(p + 1) * 128]

            lA = {}
            lR = {}
            for g in range(4):
                for pp in range(2):
                    p = 2 * g + pp
                    rb = 4 + p // 4
                    lA[g] = S.t(lambda e, p=p, pp=pp, g=g: e.matmul(
                        pb[g][:, pp * 256: pp * 256 + 256], ks(p), qpd(p),
                        start=(pp == 0), stop=True),
                        deps=[m_qk] + (pbfree[g] if pp == 0 else []))
                    lR[rb] = S.t(lambda e, p=p, rb=rb: e.matmul(
                        pb[rb][:, (p % 4) * 128:(p % 4) * 128 + 128],
                        skbp(p), qs(p),
                        start=(p % 4 == 0), stop=True),
                        deps=[m_qk, st_dep()] +
                             (pbfree[rb] if p % 4 == 0 else []))
            evb = {}
            for g in range(4):
                g0 = g * 512
                mBT = S.v(lambda e, g=g, g0=g0: e.tensor_tensor(
                    bts_a[:, g0:g0 + 512], pb[g][:], trium[:], ALU.mult),
                    deps=[lA[g]])
                mATs = S.s(lambda e, g=g, g0=g0: e.copy(
                    ats_a[:, g0:g0 + 512], pb[g][:]), deps=[lA[g], mBT])
                pbfree[g] = [mATs]
                evb[g] = (mBT, mATs)
            mRT2 = {}
            for rb in (4, 5):
                mRT2[rb] = S.s(lambda e, rb=rb: e.copy(
                    rts_a[:, (rb - 4) * 512:(rb - 4) * 512 + 512],
                    pb[rb][:]), deps=[lR[rb]])
                pbfree[rb] = [mRT2[rb]]

            lXU = {}
            lX2 = {}
            for g in range(4):
                mBT, mATs = evb[g]
                for pp in range(2):
                    p = 2 * g + pp
                    x2b = 4 + p // 4
                    first2 = (p % 4 == 0)
                    lX2[x2b] = S.t(lambda e, p=p, x2b=x2b, first2=first2:
                                   e.matmul(
                        pb[x2b][:, (p % 4) * 128:(p % 4) * 128 + 128],
                        qs(p), skbp(p),
                        start=first2, stop=False),
                        deps=[m_qk, st_dep()] +
                             (pbfree[x2b] if first2 else []))
                    for j2 in range(2):
                        h = 2 * p + j2
                        co = h * 128
                        lXU[g] = S.t(lambda e, co=co, h=h, g=g: e.matmul(
                            pb[g][:, (h % 4) * 128:(h % 4) * 128 + 128],
                            bts_a[:, co:co + 128], ats_a[:, co:co + 128],
                            start=(h % 4 == 0), stop=False),
                            deps=[mBT, mATs] +
                                 (pbfree[g] if h % 4 == 0 else []))
                        lX2[x2b] = S.t(lambda e, co=co, h=h, p=p, x2b=x2b,
                                       j2=j2: e.matmul(
                            pb[x2b][:, (p % 4) * 128 + j2 * DK:
                                    (p % 4) * 128 + (j2 + 1) * DK],
                            bts_a[:, co:co + 128], kc(h),
                            start=False, stop=True), deps=[mBT, m_kv])
                    lXU[g] = S.t(lambda e, p=p, pp=pp, g=g: e.matmul(
                        pb[g][:, pp * 256: pp * 256 + 256],
                        rts_a[:, p * 128:(p + 1) * 128], qpd(p),
                        start=False, stop=True), deps=[mRT2[4 + p // 4]])
            evu = {}
            for g in range(4):
                g0 = g * 512
                mUTs = S.v(lambda e, g=g, g0=g0: e.tensor_tensor(
                    uts_a[:, g0:g0 + 512], pb[g][:], trium[:], ALU.mult),
                    deps=[lXU[g]])
                pbfree[g] = [mUTs]
                evu[g] = mUTs
            for bb in (4, 5):
                co = (bb - 4) * 512
                mX2s = S.s(lambda e, bb=bb, co=co: e.copy(
                    x2s_a[:, co:co + 512], pb[bb][:]), deps=[lX2[bb]])
                pbfree[bb] = [mX2s]
                evu[bb] = mX2s

            lO = {}
            lKD = {}
            for p in range(8):
                g = p // 2
                ob = 6 + p // 4
                kb = p // 4
                db_ = 2 + p // 4
                oco = (p % 4) * 128
                for j2 in range(2):
                    h = 2 * p + j2
                    co = h * 128
                    firsto = (p % 4 == 0 and j2 == 0)
                    lO[ob] = S.t(lambda e, co=co, h=h, ob=ob, oco=oco, j2=j2,
                                 firsto=firsto: e.matmul(
                        pb[ob][:, oco + j2 * DK: oco + (j2 + 1) * DK],
                        uts_a[:, co:co + 128], vc(h),
                        start=firsto, stop=False),
                        deps=[evu[g], m_kv] + (pbfree[ob] if firsto else []))
                lO[ob] = S.t(lambda e, p=p, ob=ob, oco=oco: e.matmul(
                    pb[ob][:, oco: oco + 128], qs(p), dbp(p),
                    start=False, stop=True), deps=[st_dep()])
                lKD[kb] = S.t(lambda e, p=p, kb=kb: e.matmul(
                    pb[kb][:, (p % 4) * 128:(p % 4) * 128 + 128],
                    kcp(p), kcp(p), start=(p % 4 == 0), stop=True),
                    deps=[m_kv] + (pbfree[kb] if p % 4 == 0 else []))
                lKD[db_] = S.t(lambda e, p=p, db_=db_: e.matmul(
                    pb[db_][:, (p % 4) * 128:(p % 4) * 128 + 128],
                    x2s_a[:, p * 128:(p + 1) * 128], vcp(p),
                    start=(p % 4 == 0), stop=True),
                    deps=[evu[4 + p // 4], m_kv] +
                         (pbfree[db_] if p % 4 == 0 else []))
            m_heads_T = S.mark("T")
            ot_marks = []
            pp2o = 0
            for ob in (6, 7):
                co = (ob - 6) * 512
                mo = S.s(lambda e, ob=ob, co=co, pp2o=pp2o: e.copy(
                    o_c[pp2o][:, co:co + 512], pb[ob][:]),
                    deps=[lO[ob]] + ([oc_free[pp2o]] if ob == 6 else []))
                pbfree[ob] = [mo]
                ot_marks.append(mo)
            mupd = []
            for kb in (0, 1, 2, 3):
                scr = kks if kb < 2 else dds
                stt = skf if kb < 2 else df
                so = (kb % 2) * 256
                fo = (kb % 2) * 512
                cpa = S.s(lambda e, kb=kb, scr=scr, so=so: e.copy(
                    scr[0:64, so:so + 256].rearrange(
                        "p (pr w) -> p pr w", pr=4),
                    pb[kb][0:64, :].rearrange(
                        "p (pr w) -> p pr w", pr=4)[:, :, 0:64]),
                    deps=[lKD[kb], scrfree[kb]])
                cpb = S.s(lambda e, kb=kb, scr=scr, so=so: e.copy(
                    scr[64:128, so:so + 256].rearrange(
                        "p (pr w) -> p pr w", pr=4),
                    pb[kb][64:128, :].rearrange(
                        "p (pr w) -> p pr w", pr=4)[:, :, 64:128]),
                    deps=[lKD[kb], scrfree[4 + kb]])
                pbfree[kb] = [cpb]
                dst = stt[0:64, fo:fo + 512].rearrange(
                    "p (pr w) -> p pr w", pr=4)[:, :, 0:64]
                src = scr[0:64, so:so + 256].rearrange(
                    "p (pr w) -> p pr w", pr=4)
                mUa = S.v(lambda e, dst=dst, src=src: e.tensor_add(
                    dst, dst, src), deps=[cpa])
                dst2 = stt[64:128, fo:fo + 512].rearrange(
                    "p (pr w) -> p pr w", pr=4)[:, :, 64:128]
                src2 = scr[64:128, so:so + 256].rearrange(
                    "p (pr w) -> p pr w", pr=4)
                mUb = S.v(lambda e, dst2=dst2, src2=src2: e.tensor_add(
                    dst2, dst2, src2), deps=[cpb])
                scrfree[kb] = mUa
                scrfree[4 + kb] = mUb
                mupd += [mUa, mUb]
            mskb = S.v(lambda e: e.tensor_copy(skb[:], skf[:]),
                       deps=mupd + [m_heads_T])
            mdbb = S.v(lambda e: e.tensor_copy(db[:], df[:]),
                       deps=[mskb])
            st_cast[0] = mdbb

            pp2 = 0
            motd = None
            lastmm = None
            for rnd in range(2):
                bank = 6 + rnd
                mm = None
                for k in range(4):
                    blk = rnd * 4 + k
                    mm = S.t(lambda e, blk=blk, k=k, bank=bank, pp2=pp2:
                             e.matmul(pb[bank][:, k * 128:(k + 1) * 128],
                                      o_c[pp2][:, blk * 128:(blk + 1) * 128],
                                      ident[:], start=(k == 0), stop=True),
                             deps=[ot_marks[blk // 4], m_aux2] +
                                  (pbfree[bank] if k == 0 else []))
                motd = S.s(lambda e, rnd=rnd, bank=bank, pp2=pp2: e.copy(
                    ot_c[pp2][:, rnd * 512:(rnd + 1) * 512], pb[bank][:]),
                    deps=[mm, ot_free[pp2]])
                pbfree[bank] = [motd]
                lastmm = mm
            oc_free[pp2] = lastmm
            hstate[ci] = (motd, mxs, half)

        def emit_wo_ln(ci, m_qk, m_kv):
            nonlocal s1_free
            motd, mxs, half = hstate[ci]
            bi4 = ci % 5
            pp2 = 0
            r0 = ci * C
            mWop = {}
            for dhalf in range(2):
                bi = 4 + dhalf
                for ic in range(8):
                    first = ic == 0
                    mWop[bi] = S.t(lambda e, ic=ic, dhalf=dhalf, bi=bi,
                                   first=first, pp2=pp2: e.matmul(
                        pb[bi][:], ot_c[pp2][:, ic * 128:(ic + 1) * 128],
                        wo_sb[:, ic * D + dhalf * 512:
                              ic * D + dhalf * 512 + 512],
                        start=first, stop=(ic == 7)),
                        deps=[motd, m_wo] + (pbfree[bi] if first else []))
            ot_free[pp2] = S.mark("T")
            mr1a = S.v(lambda e, bi4=bi4: e.tensor_add(
                x_c[bi4][:, 0:512], x_c[bi4][:, 0:512], pb[4][:]),
                deps=[mWop[4], mxs[half]])
            pbfree[4] = [mr1a]
            mr1b = S.v(lambda e, bi4=bi4: e.tensor_add(
                x_c[bi4][:, 512:1024], x_c[bi4][:, 512:1024], pb[5][:]),
                deps=[mWop[5], mxs[half]])
            pbfree[5] = [mr1b]
            mb1 = S.v(lambda e, bi4=bi4: e.bn_stats(bn6[:, 0:6],
                                                    x_c[bi4][:, 0:512]),
                      deps=[mr1a])
            mb2 = S.v(lambda e, bi4=bi4: e.bn_stats(bn6[:, 6:12],
                                                    x_c[bi4][:, 512:1024]),
                      deps=[mr1b])
            mb3 = S.v(lambda e: e.bn_aggr(bnmv[:], bn6[:]),
                      deps=[mb1, mb2, ln_free[0]])
            msq = S.s(lambda e: e.activation(rstd[:], bnmv[:, 1:2], AF.Sqrt,
                                             bias=CEPS, scale=1.0),
                      deps=[mb3, mk_cons])
            mrc = S.v(lambda e: e.reciprocal(rstd[:], rstd[:]), deps=[msq])
            mnm = S.v(lambda e: e.scalar_tensor_tensor(
                nmr[:], bnmv[:, 0:1], -1.0, rstd[:], ALU.mult, ALU.mult),
                deps=[mrc, ln_free[1]])
            mact = S.s(lambda e, bi4=bi4: e.activation(
                x_c[bi4][:], x_c[bi4][:], AF.Identity, bias=nmr[:],
                scale=rstd[:]), deps=[mnm])
            ln_free[0] = msq
            ln_free[1] = mact
            mst = S.dma("SY", st_s1[ci % 2], lambda e, r0=r0, bi4=bi4:
                        e.dma_start(
                out=s1_hbm[r0:r0 + C, :], in_=x_c[bi4][:]), deps=[mact])
            xuse[bi4] = mst
            m_s1_stores.append(mst)
            if debug and ci == 0:
                dv = S.mark("V")
                ds = S.mark("S")
                for nm, tile in [("d_qt", qs_all), ("d_kt", ks_all),
                                 ("d_kc", k_c2), ("d_vc", v_c2),
                                 ("d_ot", ot_c[0]), ("d_uts", uts_a),
                                 ("d_bts", bts_a), ("d_ats", ats_a),
                                 ("d_rts", rts_a), ("d_x2", x2s_a),
                                 ("d_skf", skf), ("d_df", df)]:
                    S.dma("GP", st_aux, lambda e, nm=nm, tile=tile: e.dma_start(
                        out=dbg[nm], in_=tile[:]), deps=[dv, ds])

        itA = iter(p0A_units)
        itB = iter(p0B_units)
        pump_state = {"dA": 0, "dB": 0, "scales": False}

        def do_pump(n):
            for _ in range(n):
                u = next(itA, None)
                if u is not None:
                    emit_p0A(u)
                    pump_state["dA"] += 1
                    continue
                if not pump_state["scales"]:
                    emit_p0_scales()
                    pump_state["scales"] = True
                    return
                u = next(itB, None)
                if u is None:
                    return
                emit_p0B(u)
                pump_state["dB"] += 1
        p0_pump[0] = do_pump

        pend_T = {0: emit_xpipe(0)}
        for qi in range(nch // 4):
            emit_quad(qi, pend_T.pop(qi), nch // 4)
        for u in itA:
            emit_p0A(u)
        if not pump_state["scales"]:
            emit_p0_scales()
        for u in itB:
            emit_p0B(u)

        BAR = [S.mark("T"), S.mark("V"), S.mark("S"),
               (st_p0s.sem, st_p0s.count), m_s1_stores[-1],
               (st_s1[0].sem, st_s1[0].count), (st_s1[1].sem, st_s1[1].count),
               (st_xt[0].sem, st_xt[0].count), (st_xt[1].sem, st_xt[1].count),
               (st_x[0].sem, st_x[0].count), (st_x[1].sem, st_x[1].count),
               (st_xb[0].sem, st_xb[0].count), (st_xb[1].sem, st_xb[1].count),
               (st_p0t[0].sem, st_p0t[0].count), (st_p0t[1].sem, st_p0t[1].count),
               (st_p0in[0].sem, st_p0in[0].count),
               (st_p0in[1].sem, st_p0in[1].count),
               (st_aux.sem, st_aux.count), (st_red.sem, st_red.count)]
        p01.close()

        p2 = ExitStack()
        E2 = p2.enter_context
        tile_ch = 4
        ntile = nch // tile_ch if nch >= tile_ch else 1
        s1e = [E2(nc.sbuf_tensor(f"s1e{i}", [C, D], F32)) for i in range(2)]
        s1x = [E2(nc.sbuf_tensor(f"s1x{i}", [C, D], F32)) for i in range(2)]
        xq = [E2(nc.sbuf_tensor(f"xq{i}", [C, D], BF16)) for i in range(4)]
        xqT = [E2(nc.sbuf_tensor(f"xqT{i}", [128, tile_ch * D], BF16))
               for i in range(2)]
        ht = E2(nc.sbuf_tensor("ht", [C, tile_ch * INNER], F32))
        wgub = [E2(nc.sbuf_tensor(f"wgub{i}", [128, 8192], BF16)) for i in range(2)]
        hq = [E2(nc.sbuf_tensor(f"hq{i}", [C, INNER], BF16)) for i in range(2)]
        hqT = E2(nc.sbuf_tensor("hqT", [128, tile_ch * INNER], BF16))
        wdpb = [E2(nc.sbuf_tensor(f"wdpb{i}", [128, 2 * D], BF16)) for i in range(2)]
        qzp = E2(nc.sbuf_tensor("qzp", [128, 512], F32))
        qcp = [E2(nc.sbuf_tensor(f"qcp{i}", [128, 512], I32)) for i in range(2)]
        am_parts = E2(nc.sbuf_tensor("am_parts", [C, 32], F32))
        r2t = [E2(nc.sbuf_tensor(f"r2t{i}", [C, D], F32)) for i in range(2)]
        am2 = E2(nc.sbuf_tensor("am2", [C, 4], F32))
        sv2 = E2(nc.sbuf_tensor("sv2", [C, 2], F32))
        dscl = E2(nc.sbuf_tensor("dscl", [C, 16], F32))
        bn6b = E2(nc.sbuf_tensor("bn6b", [C, 12], F32))
        bnmvb = E2(nc.sbuf_tensor("bnmvb", [C, 2], F32))
        rstd2 = E2(nc.sbuf_tensor("rstd2", [C, 1], F32))
        nmr2 = E2(nc.sbuf_tensor("nmr2", [C, 1], F32))
        pb2 = [E2(nc.psum_tensor(f"pb2_{i}", [128, 512], F32)) for i in range(8)]

        BAR1 = BAR[0]
        wgu_free = [BAR1, BAR1]
        wdp_free = [BAR1, BAR1]
        s1x_free = [BAR1, BAR1]
        s1e_free = [BAR1, BAR1]
        pb2free = [[BAR[1]]] * 8
        hq_free = [BAR1, BAR1]
        hqT_free = BAR1
        xq_free = [BAR1, BAR1, BAR1, BAR1]
        xqT_free = [BAR1, BAR1]
        r2_free = [BAR[2], BAR[2]]
        y_free = [(st_y[0].sem, 0), (st_y[1].sem, 0)]
        qtmp_free = BAR[1]
        p2first = list(BAR)
        wg_done = [p0_wdone.get(0), p0_wdone.get(1), p0_wdone.get(2)]
        s1_all = [BAR[5], BAR[6]]
        ln2_free = [BAR[2], BAR[2]]

        qsub = [0]
        qcp_free = [BAR[1], BAR[1]]

        def quant_to(src_ap, dst_ap, svcol, deps, width=512):
            nonlocal qtmp_free
            par = qsub[0]; qsub[0] ^= 1
            mzp = S.v(lambda e, src_ap=src_ap, svcol=svcol, width=width:
                      e.tensor_scalar(qzp[:, 0:width], src_ap, svcol,
                                      MRND - 0.5, ALU.mult, ALU.add),
                      deps=list(deps) + [qtmp_free])
            mcp = S.s(lambda e, src_ap=src_ap, svcol=svcol, width=width,
                      par=par:
                      e.activation(qcp[par][:, 0:width], src_ap, AF.Identity,
                                   bias=CHI, scale=svcol),
                      deps=list(deps) + [qcp_free[par]])
            mt1 = S.v(lambda e, width=width, par=par: e.tensor_tensor(
                qzp[:, 0:width], qzp[:, 0:width], qcp[par][:, 0:width],
                ALU.is_gt), deps=[mzp, mcp])
            mr = S.v(lambda e, dst_ap=dst_ap, width=width, par=par:
                     e.scalar_tensor_tensor(
                dst_ap, qcp[par][:, 0:width], -MRND, qzp[:, 0:width],
                ALU.add, ALU.add), deps=[mt1])
            qcp_free[par] = mr
            qtmp_free = mr
            return mr

        m_y_fin = []
        tile_state = {}

        def emit_xq_prep_q(ti, lo, hi):
            c0 = ti * tile_ch
            qm = {}
            for cc in range(lo, hi):
                ci = c0 + cc
                sb = cc % 2
                r0 = ci * C
                ms1 = S.dma("SY", st_s1l[sb], lambda e, r0=r0, sb=sb: e.dma_start(
                    out=s1x[sb][:], in_=s1_hbm[r0:r0 + C, :]),
                    deps=[s1x_free[sb]] + s1_all + p2first)
                src2 = s1x[sb]
                mam = S.v(lambda e, src2=src2: e.reduce_max(
                    am2[:, 0:1], src2[:], axis=AX.X, apply_absolute_value=True),
                    deps=[ms1] + p2first)
                mamc = S.v(lambda e: e.tensor_scalar(
                    am2[:, 1:2], am2[:, 0:1], 1e-5, None, ALU.max), deps=[mam])
                mrc = S.v(lambda e: e.reciprocal(sv2[:, 0:1], am2[:, 1:2]),
                          deps=[mamc])
                msc = S.v(lambda e: e.tensor_scalar(
                    sv2[:, 0:1], sv2[:, 0:1], 127.0, None, ALU.mult), deps=[mrc])
                S.v(lambda e, cc=cc: e.scalar_tensor_tensor(
                    dscl[:, cc:cc + 1], am2[:, 1:2], 1.0 / 127.0, swv[:, 4:5],
                    ALU.mult, ALU.mult), deps=[mamc] + sw_ready)
                S.v(lambda e, cc=cc: e.scalar_tensor_tensor(
                    dscl[:, 4 + cc:5 + cc], am2[:, 1:2], 1.0 / 127.0,
                    swv[:, 5:6], ALU.mult, ALU.mult), deps=[mamc])
                mq1 = quant_to(src2[:, 0:512], xq[cc % 4][:, 0:512],
                               sv2[:, 0:1], [msc, xq_free[cc % 4]])
                mq2 = quant_to(src2[:, 512:1024], xq[cc % 4][:, 512:1024],
                               sv2[:, 0:1], [msc])
                s1x_free[sb] = mq2
                qm[cc] = (mq1, mq2)
            return qm

        def emit_xq_prep_T(ti, lo, hi, qm):
            tp2 = ti % 2
            mxqt = None
            for cc in range(lo, hi):
                mq1, mq2 = qm[cc]
                lastmm = None
                for rnd in range(2):
                    bank = 4 + rnd
                    mm = None
                    for k in range(4):
                        blk = rnd * 4 + k
                        mm = S.t(lambda e, blk=blk, k=k, bank=bank, cc=cc:
                                 e.matmul(pb2[bank][:, k * 128:(k + 1) * 128],
                                          xq[cc % 4][:, blk * 128:(blk + 1) * 128],
                                          ident[:], start=(k == 0), stop=True),
                                 deps=[mq1, mq2, m_aux2] +
                                      (pb2free[bank] if k == 0 else []))
                    mxqt = S.s(lambda e, cc=cc, rnd=rnd, bank=bank, tp2=tp2:
                               e.copy(
                        xqT[tp2][:, cc * D + rnd * 512: cc * D + (rnd + 1) * 512],
                        pb2[bank][:]), deps=[mm, xqT_free[tp2]])
                    pb2free[bank] = [mxqt]
                    lastmm = mm
                xq_free[cc % 4] = lastmm
            return mxqt

        def emit_stageA(ti, m_xqT):
            tp2 = ti % 2
            h_marks = [None] * tile_ch
            for pcs in range(8):
                pbuf = pcs % 2
                mldg = S.dma("GP", st_wgu[pbuf],
                             lambda e, pcs=pcs, pbuf=pbuf: e.dma_start(
                                 out=wgub[pbuf].ap()[:, 0:4096].rearrange(
                                     "p (ic n) -> p ic n", ic=8),
                                 in_=wgT_hbm[:, :, pcs * 512:(pcs + 1) * 512]
                                 .transpose([1, 0, 2])),
                             deps=[wgu_free[pbuf], wg_done[0]] + p2first)
                mldu = S.dma("GP", st_wgu[pbuf],
                             lambda e, pcs=pcs, pbuf=pbuf: e.dma_start(
                                 out=wgub[pbuf].ap()[:, 4096:8192].rearrange(
                                     "p (ic n) -> p ic n", ic=8),
                                 in_=wuT_hbm[:, :, pcs * 512:(pcs + 1) * 512]
                                 .transpose([1, 0, 2])),
                             deps=[wg_done[1]])
                for cc in range(tile_ch):
                    bg = 4 + (cc % 2) * 2
                    bu = bg + 1
                    lg = lu = None
                    for ic in range(8):
                        first = ic == 0
                        lg = S.t(lambda e, ic=ic, cc=cc, pbuf=pbuf, bg=bg,
                                 first=first, tp2=tp2:
                                 e.matmul(pb2[bg][:],
                                          xqT[tp2][:, cc * D + ic * 128:
                                                   cc * D + (ic + 1) * 128],
                                          wgub[pbuf][:, ic * 512:(ic + 1) * 512],
                                          start=first, stop=(ic == 7)),
                                 deps=[m_xqT, mldu] +
                                      (pb2free[bg] if first else []))
                        lu = S.t(lambda e, ic=ic, cc=cc, pbuf=pbuf, bu=bu,
                                 first=first, tp2=tp2:
                                 e.matmul(pb2[bu][:],
                                          xqT[tp2][:, cc * D + ic * 128:
                                                   cc * D + (ic + 1) * 128],
                                          wgub[pbuf][:, 4096 + ic * 512:
                                                     4096 + (ic + 1) * 512],
                                          start=first, stop=(ic == 7)),
                                 deps=[mldu] + (pb2free[bu] if first else []))
                    hsl = slice(cc * INNER + pcs * 512,
                                cc * INNER + (pcs + 1) * 512)
                    pq = (tile_state[ti - 1]["qmarks"][cc][pcs]
                          if ti > 0 else BAR[1])
                    msil = S.s(lambda e, cc=cc, bg=bg, hsl=hsl, pq=pq:
                               e.activation(
                        ht[:, hsl], pb2[bg][:], AF.Silu, bias=CZERO,
                        scale=dscl[:, cc:cc + 1]), deps=[lg, pq])
                    pb2free[bg] = [msil]
                    mh = S.v(lambda e, cc=cc, pcs=pcs, bu=bu, hsl=hsl:
                             e.scalar_tensor_tensor(
                        ht[:, hsl],
                        pb2[bu][:], dscl[:, 4 + cc:5 + cc], ht[:, hsl],
                        ALU.mult, ALU.mult), deps=[msil, lu])
                    pb2free[bu] = [mh]
                    mam = S.v(lambda e, cc=cc, pcs=pcs: e.reduce_max(
                        am_parts[:, cc * 8 + pcs: cc * 8 + pcs + 1],
                        ht[:, cc * INNER + pcs * 512:
                           cc * INNER + (pcs + 1) * 512],
                        axis=AX.X, apply_absolute_value=True), deps=[mh])
                    h_marks[cc] = mam
                wgu_free[pbuf] = S.mark("T")
            tile_state[ti] = {"h_marks": h_marks}

        def emit_quant(ti, lo, hi):
            nonlocal hq_free
            h_marks = tile_state[ti]["h_marks"]
            qmarks = tile_state[ti].setdefault("qmarks", {})
            for cc in range(lo, hi):
                mam = S.v(lambda e, cc=cc: e.reduce_max(
                    am2[:, 2:3], am_parts[:, cc * 8:(cc + 1) * 8], axis=AX.X),
                    deps=[h_marks[cc]])
                mamc = S.v(lambda e: e.tensor_scalar(
                    am2[:, 3:4], am2[:, 2:3], 1e-5, None, ALU.max), deps=[mam])
                mrc = S.v(lambda e: e.reciprocal(sv2[:, 1:2], am2[:, 3:4]),
                          deps=[mamc])
                msc = S.v(lambda e: e.tensor_scalar(
                    sv2[:, 1:2], sv2[:, 1:2], 127.0, None, ALU.mult), deps=[mrc])
                dco = 8 + (ti % 2) * 4 + cc
                S.v(lambda e, dco=dco: e.scalar_tensor_tensor(
                    dscl[:, dco:dco + 1], am2[:, 3:4], 1.0 / 127.0,
                    swv[:, 6:7], ALU.mult, ALU.mult), deps=[mamc] + sw_ready)
                subm = []
                for sub in range(8):
                    subm.append(quant_to(
                        ht[:, cc * INNER + sub * 512:
                           cc * INNER + (sub + 1) * 512],
                        hq[cc % 2][:, sub * 512:(sub + 1) * 512], sv2[:, 1:2],
                        [msc, hq_free[cc % 2]]))
                qmarks[cc] = subm

        def emit_hqT(ti, lo, hi):
            qmarks = tile_state[ti]["qmarks"]
            hqt_marks = tile_state[ti].setdefault("hqt_marks", {})
            for cc in range(lo, hi):
                subm = qmarks[cc]
                lastmm = None
                mev = None
                for rnd in range(8):
                    bank = rnd % 2
                    mm = None
                    for k in range(4):
                        blk = rnd * 4 + k
                        mm = S.t(lambda e, blk=blk, k=k, bank=bank, cc=cc:
                                 e.matmul(pb2[bank][:, k * 128:(k + 1) * 128],
                                          hq[cc % 2][:, blk * 128:(blk + 1) * 128],
                                          ident[:], start=(k == 0), stop=True),
                                 deps=[subm[rnd], m_aux2] +
                                      (pb2free[bank] if k == 0 else []))
                    mev = S.s(lambda e, cc=cc, rnd=rnd, bank=bank: e.copy(
                        hqT[:, cc * INNER + rnd * 512:
                            cc * INNER + (rnd + 1) * 512],
                        pb2[bank][:]), deps=[mm, hqT_free])
                    pb2free[bank] = [mev]
                    lastmm = mm
                hq_free[cc % 2] = lastmm
                hqt_marks[cc] = mev

        def emit_down_ep(ti):
            nonlocal hqT_free
            c0 = ti * tile_ch
            hqt_marks = tile_state[ti]["hqt_marks"]
            lastd = {}
            for pcs in range(16):
                pbuf = pcs % 2
                mldd = S.dma("GP", st_wdp[pbuf],
                             lambda e, pcs=pcs, pbuf=pbuf: e.dma_start(
                                 out=wdpb[pbuf].ap().rearrange(
                                     "p (i2 o) -> p i2 o", i2=2),
                                 in_=wdT_hbm[pcs * 2:(pcs + 1) * 2, :, :]
                                 .transpose([1, 0, 2])),
                             deps=[wdp_free[pbuf], wg_done[2]] + p2first)
                for cc in range(tile_ch):
                    for i2l in range(2):
                        i2 = pcs * 2 + i2l
                        for dh in range(2):
                            bi = cc * 2 + dh
                            first = pcs == 0 and i2l == 0
                            lastd[bi] = S.t(
                                lambda e, cc=cc, i2=i2, i2l=i2l, dh=dh,
                                bi=bi, pbuf=pbuf, first=first:
                                e.matmul(pb2[bi][:],
                                         hqT[:, cc * INNER + i2 * 128:
                                             cc * INNER + (i2 + 1) * 128],
                                         wdpb[pbuf][:, i2l * D + dh * 512:
                                                    i2l * D + dh * 512 + 512],
                                         start=first,
                                         stop=(pcs == 15 and i2l == 1)),
                                deps=[hqt_marks[cc], mldd] +
                                     (pb2free[bi] if first else []))
                wdp_free[pbuf] = S.mark("T")
            for cc in range(tile_ch):
                ci = c0 + cc
                r0 = ci * C
                eb = ci % 2
                yb = ci % 2
                ms1 = S.dma("SY", st_s1e[eb], lambda e, r0=r0, eb=eb:
                            e.dma_start(out=s1e[eb][:],
                                        in_=s1_hbm[r0:r0 + C, :]),
                            deps=[s1e_free[eb]] + s1_all)
                dco = 8 + (ti % 2) * 4 + cc
                md0 = S.s(lambda e, cc=cc, eb=eb, dco=dco: e.activation(
                    r2t[eb][:, 0:512], pb2[cc * 2][:], AF.Identity,
                    bias=CZERO, scale=dscl[:, dco:dco + 1]),
                    deps=[lastd[cc * 2], r2_free[eb]])
                pb2free[cc * 2] = [md0]
                md1 = S.s(lambda e, cc=cc, eb=eb, dco=dco: e.activation(
                    r2t[eb][:, 512:1024], pb2[cc * 2 + 1][:], AF.Identity,
                    bias=CZERO, scale=dscl[:, dco:dco + 1]),
                    deps=[lastd[cc * 2 + 1]])
                pb2free[cc * 2 + 1] = [md1]
                mr2 = S.v(lambda e, eb=eb: e.tensor_add(
                    r2t[eb][:], r2t[eb][:], s1e[eb][:]),
                    deps=[md0, md1, ms1])
                s1e_free[eb] = mr2
                mb1 = S.v(lambda e, eb=eb: e.bn_stats(
                    bn6b[:, 0:6], r2t[eb][:, 0:512]), deps=[mr2])
                mb2 = S.v(lambda e, eb=eb: e.bn_stats(
                    bn6b[:, 6:12], r2t[eb][:, 512:1024]), deps=[mr2])
                mb3 = S.v(lambda e: e.bn_aggr(bnmvb[:], bn6b[:]),
                          deps=[mb1, mb2, ln2_free[0]])
                msq = S.s(lambda e: e.activation(
                    rstd2[:], bnmvb[:, 1:2], AF.Sqrt, bias=CEPS, scale=1.0),
                    deps=[mb3])
                mrc2 = S.v(lambda e: e.reciprocal(rstd2[:], rstd2[:]),
                           deps=[msq])
                mnm2 = S.v(lambda e: e.scalar_tensor_tensor(
                    nmr2[:], bnmvb[:, 0:1], -1.0, rstd2[:],
                    ALU.mult, ALU.mult), deps=[mrc2, ln2_free[1]])
                mact = S.s(lambda e, eb=eb: e.activation(
                    r2t[eb][:], r2t[eb][:], AF.Identity, bias=nmr2[:],
                    scale=rstd2[:]), deps=[mnm2, y_free[yb]])
                ln2_free[0] = msq
                ln2_free[1] = mact
                mys = S.dma("SY", st_y[yb], lambda e, r0=r0, eb=eb:
                            e.dma_start(out=y_d[r0:r0 + C, :],
                                        in_=r2t[eb][:]), deps=[mact])
                y_free[yb] = mys
                r2_free[eb] = mys
                m_y_fin.append(mys)
            hqT_free = S.mark("T")
            xqT_free[ti % 2] = S.mark("T")

        prev_qm = None
        for ti in range(ntile):
            qm = prev_qm if prev_qm is not None else emit_xq_prep_q(ti, 0, 4)
            mx = emit_xq_prep_T(ti, 0, 4, qm)
            emit_stageA(ti, mx)
            emit_quant(ti, 0, 2)
            if ti > 0:
                emit_down_ep(ti - 1)
            emit_hqT(ti, 0, 2)
            emit_quant(ti, 2, 4)
            if ti + 1 < ntile:
                prev_qm = emit_xq_prep_q(ti + 1, 0, 4)
            emit_hqT(ti, 2, 4)
            p2first = []
        emit_down_ep(ntile - 1)

        def fin(e):
            return e.nop()
        S.ops["SY"].append((S._waits("SY", m_y_fin[-2:] +
                                     [(st_y[0].sem, st_y[0].count),
                                      (st_y[1].sem, st_y[1].count)]),
                            fin, None))

        blk = E(nc.Block())

        def runner(key):
            def go(eng):
                for waits, fn, inc in S.ops[key]:
                    for (sem, val) in waits:
                        eng.wait_ge(sem, val)
                    ins = fn(eng)
                    if inc is not None:
                        ins.then_inc(inc[0], inc[1])
            return go

        blk.tensor(runner("T"))
        blk.vector(runner("V"))
        blk.scalar(runner("S"))
        blk.sync(runner("SY"))
        blk.gpsimd(runner("GP"))

        p2.close()
    return nc


_NC_CACHE = {}


def _get_nc():
    if "nc" not in _NC_CACHE:
        _NC_CACHE["nc"] = build_program()
    return _NC_CACHE["nc"]


def kernel(L0_attn_x, L0_attn_W_q, L0_attn_W_k, L0_attn_W_v, L0_attn_W_o,
           L0_ffn_w_gate, L0_ffn_w_up, L0_ffn_w_down):
    x = np.ascontiguousarray(np.asarray(L0_attn_x, dtype=np.float32))
    wq = np.ascontiguousarray(np.asarray(L0_attn_W_q, dtype=np.float32))
    wk = np.ascontiguousarray(np.asarray(L0_attn_W_k, dtype=np.float32))
    wv = np.ascontiguousarray(np.asarray(L0_attn_W_v, dtype=np.float32))
    wo = np.ascontiguousarray(np.asarray(L0_attn_W_o, dtype=np.float32))
    wg = np.ascontiguousarray(np.asarray(L0_ffn_w_gate, dtype=np.float32))
    wu = np.ascontiguousarray(np.asarray(L0_ffn_w_up, dtype=np.float32))
    wd = np.ascontiguousarray(np.asarray(L0_ffn_w_down, dtype=np.float32))

    tril = np.tril(np.ones((C, C), np.float32))
    trium = np.tile(tril.T, (1, 4)).astype(ml_dtypes.bfloat16)
    ident = np.eye(C, dtype=np.float32).astype(ml_dtypes.bfloat16)

    nc = _get_nc()
    in_maps = []
    for b in range(NCORE):
        in_maps.append(dict(x=x[b], wq=wq, wk=wk, wv=wv, wo=wo,
                            wg=wg, wu=wu, wd=wd, trium=trium, ident=ident))
    res = run_bass_kernel_spmd(nc, in_maps, core_ids=list(range(NCORE)))
    return np.stack([res.results[b]["y"] for b in range(NCORE)], axis=0)


# version 23
# speedup vs baseline: 1.0742x; 1.0318x over previous
import numpy as np
import ml_dtypes
from contextlib import ExitStack

import concourse.bass as bass
import concourse.mybir as mybir
from concourse.bass_utils import run_bass_kernel_spmd

F32 = mybir.dt.float32
BF16 = mybir.dt.bfloat16
I32 = mybir.dt.int32
AF = mybir.ActivationFunctionType
ALU = mybir.AluOpType
AX = mybir.AxisListType

D = 1024
T = 4096
C = 128
H = 16
DK = 64
INNER = 4096
MRND = 256.0
NCORE = 8


class Sched:

    def __init__(self, nc):
        self.nc = nc
        self.ops = {k: [] for k in ("T", "V", "S", "SY", "GP")}
        self.cnt = {"T": 0, "V": 0, "S": 0}
        self.lastwait = {}
        self.sems = {}

    def set_engine_sems(self, semT, semV, semS):
        self.sems = {"T": semT, "V": semV, "S": semS}

    def _waits(self, key, deps):
        out = []
        for dp in deps:
            if dp is None:
                continue
            sem, val = dp
            if val <= 0:
                continue
            k = (key, id(sem))
            if self.lastwait.get(k, -1) < val:
                self.lastwait[k] = val
                out.append((sem, val))
        return out

    def _emit(self, key, fn, deps, inc):
        self.ops[key].append((self._waits(key, deps), fn, inc))

    def t(self, fn, deps=()):
        self.cnt["T"] += 1
        self._emit("T", fn, deps, (self.sems["T"], 1))
        return (self.sems["T"], self.cnt["T"])

    def v(self, fn, deps=()):
        self.cnt["V"] += 1
        self._emit("V", fn, deps, (self.sems["V"], 1))
        return (self.sems["V"], self.cnt["V"])

    def s(self, fn, deps=()):
        self.cnt["S"] += 1
        self._emit("S", fn, deps, (self.sems["S"], 1))
        return (self.sems["S"], self.cnt["S"])

    def dma(self, q, stream, fn, deps=()):
        stream.count += 16
        self._emit(q, fn, deps, (stream.sem, 16))
        return (stream.sem, stream.count)

    def mark(self, key):
        return (self.sems[key], self.cnt[key])


class Stream:
    def __init__(self, sem):
        self.sem = sem
        self.count = 0


def build_program(nch=T // C, ntile=8, tile_ch=4, debug=False):
    nc = bass.Bass("TRN2", target_bir_lowering=False)

    x_d = nc.dram_tensor("x", [T, D], F32, kind="ExternalInput").ap()
    wq_d = nc.dram_tensor("wq", [D, D], F32, kind="ExternalInput").ap()
    wk_d = nc.dram_tensor("wk", [D, D], F32, kind="ExternalInput").ap()
    wv_d = nc.dram_tensor("wv", [D, D], F32, kind="ExternalInput").ap()
    wo_d = nc.dram_tensor("wo", [D, D], F32, kind="ExternalInput").ap()
    wg_d = nc.dram_tensor("wg", [INNER, D], F32, kind="ExternalInput").ap()
    wu_d = nc.dram_tensor("wu", [INNER, D], F32, kind="ExternalInput").ap()
    wd_d = nc.dram_tensor("wd", [D, INNER], F32, kind="ExternalInput").ap()
    trium_d = nc.dram_tensor("trium", [C, 4 * C], BF16, kind="ExternalInput").ap()
    ident_d = nc.dram_tensor("ident", [C, C], BF16, kind="ExternalInput").ap()
    y_d = nc.dram_tensor("y", [T, D], F32, kind="ExternalOutput").ap()

    s1_hbm = nc.dram_tensor("s1_hbm", [T, D], F32, kind="ExternalOutput" if debug else "Internal").ap()
    wgT_hbm = nc.dram_tensor("wgT_hbm", [8, 128, INNER], BF16, kind="Internal").ap()
    wuT_hbm = nc.dram_tensor("wuT_hbm", [8, 128, INNER], BF16, kind="Internal").ap()
    wdT_hbm = nc.dram_tensor("wdT_hbm", [32, 128, D], BF16, kind="Internal").ap()
    red_hbm = nc.dram_tensor("red_hbm", [128, 4], F32, kind="Internal").ap()
    dbg = {}
    if debug:
        for nm, shp in [("d_qt", [128, 8 * 4 * C]), ("d_kt", [128, 8 * 4 * C]),
                        ("d_kc", [C, 4 * D]), ("d_vc", [C, 4 * D]),
                        ("d_ot", [128, 8 * C]),
                        ("d_uts", [C, 16 * C]), ("d_bts", [C, 16 * C]),
                        ("d_ats", [C, 16 * C]), ("d_rts", [128, 8 * C]),
                        ("d_x2", [C, H * DK]), ("d_skf", [128, H * DK]),
                        ("d_df", [128, H * DK]), ("d_attn", [C, D])]:
            dbg[nm] = nc.dram_tensor(nm, shp, F32, kind="ExternalOutput").ap()
        for nm, shp in [("d_h", [C, INNER]), ("d_hq", [C, INNER]),
                        ("d_am", [C, 4]), ("d_xq", [C, D]), ("d_dsc", [C, 12])]:
            dbg[nm] = nc.dram_tensor(nm, shp, F32, kind="ExternalOutput").ap()
    sw_hbm = nc.dram_tensor("sw_hbm", [8, 1], F32, kind="Internal").ap()

    S = Sched(nc)
    NG = 4

    with ExitStack() as st:
        E = st.enter_context
        semT = E(nc.semaphore("semT"))
        semV = E(nc.semaphore("semV"))
        semS = E(nc.semaphore("semS"))
        S.set_engine_sems(semT, semV, semS)

        def mkstream(name):
            return Stream(E(nc.semaphore(name)))
        st_aux = mkstream("st_aux")
        st_auxg = mkstream("st_auxg")
        st_x = [mkstream("st_x0"), mkstream("st_x1")]
        st_xb = [mkstream("st_xb0"), mkstream("st_xb1")]
        st_xt = [mkstream("st_xt0"), mkstream("st_xt1")]
        st_s1 = [mkstream("st_s10"), mkstream("st_s11")]
        st_p0in = [mkstream("st_p0a"), mkstream("st_p0b")]
        st_p0t = [mkstream("st_p0t0"), mkstream("st_p0t1")]
        st_p0s = mkstream("st_p0s")
        st_red = mkstream("st_red")
        st_s1l = [mkstream("st_s1l0"), mkstream("st_s1l1")]
        st_xqt = [mkstream("st_xqt0"), mkstream("st_xqt1")]
        st_hqt = [mkstream(f"st_hqt{i}") for i in range(4)]
        st_wgu = [mkstream("st_wgu0"), mkstream("st_wgu1")]
        st_wdp = [mkstream("st_wdp0"), mkstream("st_wdp1")]
        st_s1e = [mkstream("st_s1e0"), mkstream("st_s1e1")]
        st_y = [mkstream("st_y0"), mkstream("st_y1")]
        st_ot = [mkstream("st_ot0"), mkstream("st_ot1")]

        trium = E(nc.sbuf_tensor("trium_sb", [C, 4 * C], BF16))
        ident = E(nc.sbuf_tensor("ident_sb", [C, C], BF16))
        cons = E(nc.sbuf_tensor("cons", [128, 4], F32))
        swv = E(nc.sbuf_tensor("swv", [128, 8], F32))
        CEPS = cons[:, 0:1]
        CLO = cons[:, 1:2]
        CHI = cons[:, 2:3]
        CZERO = cons[:, 3:4]
        CNHALF = swv[:, 7:8]

        m_aux1 = S.dma("SY", st_aux, lambda e: e.dma_start(out=trium[:], in_=trium_d))
        m_aux2 = S.dma("SY", st_aux, lambda e: e.dma_start(out=ident[:], in_=ident_d))
        S.v(lambda e: e.memset(cons[:, 0:1], 1e-5))
        S.v(lambda e: e.memset(cons[:, 1:2], MRND - 0.5))
        S.v(lambda e: e.memset(cons[:, 2:3], MRND))
        S.v(lambda e: e.memset(cons[:, 3:4], 0.0))
        mk_cons = S.v(lambda e: e.memset(swv[:, 7:8], -0.5))

        p01 = ExitStack()
        E1 = p01.enter_context
        p0in = [E1(nc.sbuf_tensor(f"p0in{i}", [128, D], F32)) for i in range(2)]
        p0bf = [E1(nc.sbuf_tensor(f"p0bf{i}", [128, D], BF16)) for i in range(2)]
        p0t1 = E1(nc.sbuf_tensor("p0t1", [128, 512], F32))
        p0t2 = E1(nc.sbuf_tensor("p0t2", [128, 512], F32))
        p0tr = [E1(nc.sbuf_tensor(f"p0tr{i}", [128, 1024], BF16)) for i in range(1)]
        p0acc = E1(nc.sbuf_tensor("p0acc", [128, 4], F32))
        p0red = E1(nc.sbuf_tensor("p0red", [1, 128], F32))
        p0sw = E1(nc.sbuf_tensor("p0sw", [1, 8], F32))

        wq_sb = E1(nc.sbuf_tensor("wq_sb", [128, 8 * D], BF16))
        wk_sb = E1(nc.sbuf_tensor("wk_sb", [128, 8 * D], BF16))
        wv_sb = E1(nc.sbuf_tensor("wv_sb", [128, 8 * D], BF16))
        wo_sb = E1(nc.sbuf_tensor("wo_sb", [128, 8 * D], BF16))
        x_c = [E1(nc.sbuf_tensor(f"x_c{i}", [C, D], F32)) for i in range(5)]
        xb_c = [E1(nc.sbuf_tensor(f"xb_c{i}", [C, D], BF16)) for i in range(2)]
        xT2 = [E1(nc.sbuf_tensor(f"xT2_{i}", [128, 4 * D], BF16)) for i in range(2)]
        qs_all = E1(nc.sbuf_tensor("qs_all", [128, 8 * 4 * C], BF16))
        ks_all = E1(nc.sbuf_tensor("ks_all", [128, 8 * 4 * C], BF16))
        qpd_all = E1(nc.sbuf_tensor("qpd_all", [128, 8 * 4 * 2 * C], BF16))
        k_c2 = E1(nc.sbuf_tensor("k_c2", [C, 4 * D], BF16))
        v_c2 = E1(nc.sbuf_tensor("v_c2", [C, 4 * D], BF16))
        bts_a = E1(nc.sbuf_tensor("bts_a", [C, 16 * C], BF16))
        ats_a = E1(nc.sbuf_tensor("ats_a", [C, 16 * C], BF16))
        uts_a = E1(nc.sbuf_tensor("uts_a", [C, 16 * C], BF16))
        rts_a = E1(nc.sbuf_tensor("rts_a", [128, 8 * C], BF16))
        x2s_a = E1(nc.sbuf_tensor("x2s_a", [C, H * DK], BF16))
        o_c = [E1(nc.sbuf_tensor(f"o_c{i}", [C, D], BF16)) for i in range(1)]
        ot_c = [E1(nc.sbuf_tensor(f"ot_c{i}", [128, 8 * C], BF16)) for i in range(1)]
        skf = E1(nc.sbuf_tensor("skf", [128, H * DK], F32))
        df = E1(nc.sbuf_tensor("df", [128, H * DK], F32))
        skb = E1(nc.sbuf_tensor("skb", [128, H * DK], BF16))
        db = E1(nc.sbuf_tensor("db", [128, H * DK], BF16))
        bn6 = E1(nc.sbuf_tensor("bn6", [C, 12], F32))
        bnmv = E1(nc.sbuf_tensor("bnmv", [C, 2], F32))
        rstd = E1(nc.sbuf_tensor("rstd", [C, 1], F32))
        nmr = E1(nc.sbuf_tensor("nmr", [C, 1], F32))
        pb = [E1(nc.psum_tensor(f"pb{i}", [128, 512], F32)) for i in range(8)]

        def ld_w(sb, dr):
            return S.dma("GP", st_auxg, lambda e, sb=sb, dr=dr: e.dma_start(
                out=sb.ap().rearrange("p (ic c) -> p ic c", ic=8),
                in_=dr.rearrange("(ic p) c -> p ic c", p=128)))
        ld_w(wq_sb, wq_d)
        ld_w(wk_sb, wk_d)
        ld_w(wv_sb, wv_d)
        ld_w(wo_sb, wo_d)
        m_aux_all = (st_aux.sem, st_aux.count)
        m_auxg_all = (st_auxg.sem, st_auxg.count)

        m_st0 = [S.v(lambda e, t=t_: e.memset(t[:], 0.0))
                 for t_ in (skf, df, skb, db, qpd_all)]
        w_dep = [m_aux_all, m_auxg_all, m_aux1, mk_cons] + m_st0
        m_wo = m_auxg_all

        W3 = [(wg_d, wgT_hbm), (wu_d, wuT_hbm), (wd_d, wdT_hbm)]
        p0A_units = []
        p0B_units = []
        for wi, (wdr, _) in enumerate(W3):
            rows, cols = wdr.shape[0], wdr.shape[1]
            for b in range(rows // 128):
                for ss in range(cols // D):
                    p0A_units.append((wi, b, ss))
                    p0B_units.append((wi, b, ss))

        p0st = {
            "ldbuf": 0, "use": [(semV, 0), (semV, 0)],
            "accfirst": [True, True, True], "Adone": (semV, 0),
            "trbuf": 0, "tr_free": [(st_p0s.sem, 0), (st_p0s.sem, 0)],
            "bank": 0,
            "bfbuf": 0, "bf_free": [[(semT, 0)], [(semT, 0)]],
            "qfree": (semV, 0),
        }
        sw_ready = []
        p0_wdone = {}

        def emit_p0A(u):
            wi, b, ss = u
            buf = p0st["ldbuf"]; p0st["ldbuf"] ^= 1
            wdr = W3[wi][0]
            ldm = S.dma("GP", st_p0in[buf],
                        lambda e, wdr=wdr, b=b, ss=ss, buf=buf: e.dma_start(
                            out=p0in[buf][:],
                            in_=wdr[b * 128:(b + 1) * 128, ss * D:(ss + 1) * D]),
                        deps=[p0st["use"][buf]])
            m1 = S.v(lambda e, buf=buf: e.reduce_sum(
                p0acc[:, 3:4], p0in[buf][:], axis=AX.X, apply_absolute_value=True),
                deps=[ldm])
            if p0st["accfirst"][wi]:
                p0st["accfirst"][wi] = False
                m2 = S.v(lambda e, wi=wi: e.tensor_copy(
                    p0acc[:, wi:wi + 1], p0acc[:, 3:4]), deps=[m1])
            else:
                m2 = S.v(lambda e, wi=wi: e.tensor_add(
                    p0acc[:, wi:wi + 1], p0acc[:, wi:wi + 1], p0acc[:, 3:4]), deps=[m1])
            p0st["use"][buf] = m2
            p0st["Adone"] = m2

        def emit_p0_scales():
            mst = S.dma("SY", st_red, lambda e: e.dma_start(
                out=red_hbm[:, 0:3], in_=p0acc[:, 0:3]), deps=[p0st["Adone"]])
            for wi in range(3):
                nel = float(W3[wi][0].shape[0] * W3[wi][0].shape[1])
                mld = S.dma("SY", st_red, lambda e, wi=wi: e.dma_start(
                    out=p0red[:], in_=red_hbm[:, wi:wi + 1].transpose([1, 0]),
                    allow_slow_non_contiguous=True),
                    deps=[mst, (semV, S.cnt["V"])])
                mr = S.v(lambda e: e.reduce_sum(p0sw[0:1, 6:7], p0red[:], axis=AX.X),
                         deps=[mld])
                mm = S.v(lambda e, nel=nel, wi=wi: e.tensor_scalar(
                    p0sw[0:1, 4 + (wi % 2) * 0 + 0:5] if False else p0sw[0:1, 6:7],
                    p0sw[0:1, 6:7], 1.0 / nel, 1e-5, ALU.mult, ALU.max), deps=[mr])
                mcpy = S.v(lambda e, wi=wi: e.tensor_copy(
                    p0sw[0:1, 3 + wi:4 + wi], p0sw[0:1, 6:7]), deps=[mm])
                mrc = S.v(lambda e, wi=wi: e.reciprocal(
                    p0sw[0:1, wi:wi + 1], p0sw[0:1, 6:7]), deps=[mcpy])
                mstw = S.dma("SY", st_red, lambda e, wi=wi: e.dma_start(
                    out=sw_hbm[wi:wi + 1, :], in_=p0sw[0:1, wi:wi + 1]), deps=[mrc])
                mstm = S.dma("SY", st_red, lambda e, wi=wi: e.dma_start(
                    out=sw_hbm[4 + wi:5 + wi, :], in_=p0sw[0:1, 3 + wi:4 + wi]),
                    deps=[mrc])
                mb1 = S.dma("SY", st_red, lambda e, wi=wi: e.dma_start(
                    out=swv[:, wi:wi + 1],
                    in_=sw_hbm[wi:wi + 1, 0:1].partition_broadcast(128)), deps=[mstw])
                mb2 = S.dma("SY", st_red, lambda e, wi=wi: e.dma_start(
                    out=swv[:, 4 + wi:5 + wi],
                    in_=sw_hbm[4 + wi:5 + wi, 0:1].partition_broadcast(128)),
                    deps=[mstm])
                sw_ready.append(mb1)
                sw_ready.append(mb2)

        def emit_p0B(u):
            wi, b, ss = u
            buf = p0st["ldbuf"]; p0st["ldbuf"] ^= 1
            wdr, wtd = W3[wi]
            ldm = S.dma("GP", st_p0in[buf],
                        lambda e, wdr=wdr, b=b, ss=ss, buf=buf: e.dma_start(
                            out=p0in[buf][:],
                            in_=wdr[b * 128:(b + 1) * 128, ss * D:(ss + 1) * D]),
                        deps=[p0st["use"][buf]])
            deps = [ldm, p0st["qfree"]] + sw_ready
            bfb = p0st["bfbuf"]; p0st["bfbuf"] ^= 1
            mq = None
            for sp in range(2):
                sl = slice(sp * 512, (sp + 1) * 512)
                mt1 = S.v(lambda e, wi=wi, buf=buf, sl=sl: e.tensor_scalar(
                    p0t1[:], p0in[buf][:, sl], swv[:, wi:wi + 1], 0.5,
                    ALU.mult, ALU.is_gt), deps=deps + [p0st["qfree"]])
                mt2 = S.v(lambda e, wi=wi, buf=buf, sl=sl: e.tensor_scalar(
                    p0t2[:], p0in[buf][:, sl], swv[:, wi:wi + 1], -0.5,
                    ALU.mult, ALU.is_lt), deps=deps)
                mq = S.v(lambda e, bfb=bfb, sl=sl: e.tensor_sub(
                    p0bf[bfb][:, sl], p0t1[:], p0t2[:]),
                    deps=[mt1, mt2] + p0st["bf_free"][bfb])
                p0st["qfree"] = mq
            p0st["use"][buf] = mq
            stg = 0
            ev_marks = []
            lastmm = None
            for rnd in range(2):
                bank = p0st["bank"]; p0st["bank"] = (p0st["bank"] + 1) % 8
                mm = None
                for k in range(4):
                    blk = rnd * 4 + k
                    mm = S.t(lambda e, bfb=bfb, blk=blk, k=k, bank=bank:
                             e.matmul(pb[bank][:, k * 128:(k + 1) * 128],
                                      p0bf[bfb][:, blk * 128:(blk + 1) * 128],
                                      ident[:], start=(k == 0), stop=True),
                             deps=[mq, m_aux2] +
                                  (pbfree[bank] if k == 0 else []))
                mev = S.s(lambda e, stg=stg, rnd=rnd, bank=bank: e.copy(
                    p0tr[stg][:, rnd * 512:(rnd + 1) * 512], pb[bank][:]),
                    deps=[mm, p0st["tr_free"][stg]])
                pbfree[bank] = [mev]
                ev_marks.append(mev)
                lastmm = mm
            if wi < 2:
                dst = wtd[0:8, :, b * 128:(b + 1) * 128]
            else:
                dst = wtd[ss * 8:ss * 8 + 8, :, b * 128:(b + 1) * 128]
            mstb = S.dma("GP", st_p0s,
                         lambda e, dst=dst, stg=stg: e.dma_start(
                             out=dst.transpose([1, 0, 2]),
                             in_=p0tr[stg].ap().rearrange(
                                 "p (ic q) -> p ic q", ic=8)),
                         deps=ev_marks)
            p0st["tr_free"][stg] = (st_p0s.sem, st_p0s.count)
            p0st["bf_free"][bfb] = [lastmm]
            p0_wdone[wi] = mstb

        xuse = [(semV, 0)] * 5
        xbuse = [(st_xt[0].sem, 0), (st_xt[1].sem, 0),
                 (st_xt[0].sem, 0), (st_xt[1].sem, 0)]
        xtuse = [(semT, 0), (semT, 0)]
        pbfree = [[(semV, 0)]] * 8
        st_cast = [(semV, 0)]
        ln_free = [(semS, 0), (semS, 0)]
        ot_free = [(semT, 0), (semT, 0)]
        oc_free = [(st_ot[0].sem, 0), (st_ot[1].sem, 0)]
        s1_free = (st_s1[0].sem, 0)
        m_s1_stores = []

        def st_dep():
            return st_cast[0]

        p0_pump = [lambda n: None]

        def emit_xpipe(qi):
            qp = qi % 2
            mxts = []
            for half in range(4):
                ci = 4 * qi + half
                bx = ci % 2
                r0 = ci * C
                mxb = S.dma("GP", st_xb[ci % 2],
                            lambda e, r0=r0, bx=bx: e.dma_start(
                                out=xb_c[bx][:], in_=x_d[r0:r0 + C, :]),
                            deps=[xbuse[bx]])
                mxt = None
                for ic in range(8):
                    mxt = S.dma("SY", st_xt[ci % 2],
                                lambda e, ic=ic, bx=bx, qp=qp, half=half:
                                e.dma_start(
                                    out=xT2[qp][:, ic * 512 + half * 128:
                                                ic * 512 + half * 128 + 128],
                                    in_=xb_c[bx][:, ic * 128:(ic + 1) * 128],
                                    transpose=True),
                                deps=[mxb, xtuse[qp]])
                xbuse[bx] = mxt
                mxts.append(mxt)
            return mxts

        def emit_quad(qi, mxts, nq):
            qp = qi % 2
            mxs = []
            for half in range(4):
                ci = 4 * qi + half
                bi8 = ci % 5
                r0 = ci * C
                mxs.append(S.dma("SY", st_x[ci % 2],
                                 lambda e, r0=r0, bi8=bi8: e.dma_start(
                                     out=x_c[bi8][:], in_=x_d[r0:r0 + C, :]),
                                 deps=[xuse[bi8]]))
            dep0 = mxts + w_dep

            for r in range(4):
                lastb = {}
                for qsel in (True, False):
                    wsb = wq_sb if qsel else wk_sb
                    for pp2 in range(2):
                        p = r * 2 + pp2
                        bi = (0 if qsel else 4) + (r % 2) * 2 + pp2
                        for ic in range(8):
                            first = ic == 0
                            lastb[bi] = S.t(
                                lambda e, wsb=wsb, ic=ic, p=p, bi=bi,
                                first=first, qp=qp:
                                e.matmul(pb[bi][:, :],
                                         wsb[:, ic * D + p * 128:
                                             ic * D + p * 128 + 128],
                                         xT2[qp][:, ic * 512:(ic + 1) * 512],
                                         start=first, stop=(ic == 7)),
                                deps=dep0 + (pbfree[bi] if first else []))
                for pp2 in range(2):
                    p = r * 2 + pp2
                    bi = (r % 2) * 2 + pp2
                    mq1 = S.s(lambda e, p=p, bi=bi: e.copy(
                        qs_all[:, p * 512:p * 512 + 512], pb[bi][:]),
                        deps=[lastb[bi]])
                    mq2 = S.s(lambda e, p=p, bi=bi: e.copy(
                        qpd_all[0:64, p * 1024:(p + 1) * 1024].rearrange(
                            "p (c w) -> p c w", c=4)[:, :, 0:128],
                        pb[bi][0:64, :].rearrange("p (c w) -> p c w", c=4)),
                        deps=[lastb[bi]])
                    mq3 = S.s(lambda e, p=p, bi=bi: e.copy(
                        qpd_all[64:128, p * 1024:(p + 1) * 1024].rearrange(
                            "p (c w) -> p c w", c=4)[:, :, 128:256],
                        pb[bi][64:128, :].rearrange("p (c w) -> p c w", c=4)),
                        deps=[lastb[bi]])
                    pbfree[bi] = [mq3]
                    bk = 4 + (r % 2) * 2 + pp2
                    mk1 = S.s(lambda e, p=p, bk=bk: e.copy(
                        ks_all[:, p * 512:p * 512 + 512], pb[bk][:]),
                        deps=[lastb[bk]])
                    pbfree[bk] = [mk1]
            m_qk = S.mark("S")
            p0_pump[0](2)

            for sub in range(2):
                lastkv = {}
                for h2 in range(2):
                    half = sub * 2 + h2
                    for vsel in (False, True):
                        wsb = wv_sb if vsel else wk_sb
                        boff = h2 * 4 + (2 if vsel else 0)
                        for dhalf in range(2):
                            bi = boff + dhalf
                            for ic in range(8):
                                first = ic == 0
                                lastkv[bi] = S.t(
                                    lambda e, wsb=wsb, ic=ic, dhalf=dhalf,
                                    bi=bi, qp=qp, half=half, first=first:
                                    e.matmul(pb[bi][:, :],
                                             xT2[qp][:, ic * 512 + half * 128:
                                                     ic * 512 + half * 128 + 128],
                                             wsb[:, ic * D + dhalf * 512:
                                                 ic * D + dhalf * 512 + 512],
                                             start=first, stop=(ic == 7)),
                                    deps=dep0 + (pbfree[bi] if first else []))
                for bi in range(8):
                    h2 = bi // 4
                    half = sub * 2 + h2
                    vsel = (bi % 4) >= 2
                    dhalf = bi % 2
                    dstt = v_c2 if vsel else k_c2
                    co = half * D + dhalf * 512
                    pbfree[bi] = [S.s(lambda e, dstt=dstt, co=co, bi=bi:
                                      e.copy(dstt[:, co:co + 512], pb[bi][:]),
                                      deps=[lastkv[bi]])]
            xtuse[qp] = S.mark("T")
            m_kv = S.mark("S")
            if qi + 1 < nq:
                pend_T[qi + 1] = emit_xpipe(qi + 1)
            p0_pump[0](2)
            for half in range(4):
                emit_heads(4 * qi + half, half, m_qk, m_kv, mxs)
                p0_pump[0](2)
                emit_wo_ln(4 * qi + half, m_qk, m_kv)
                p0_pump[0](3)

        hstate = {}

        def emit_heads(ci, half, m_qk, m_kv, mxs):
            bi4 = ci % 5
            r0 = ci * C
            koff = half * D
            qoff = half * 128

            def qs(p):
                return qs_all[:, p * 512 + qoff: p * 512 + qoff + 128]

            def ks(p):
                return ks_all[:, p * 512 + qoff: p * 512 + qoff + 128]

            def qpd(p):
                return qpd_all[:, p * 1024 + half * 256:
                               p * 1024 + half * 256 + 256]

            def kc(h):
                return k_c2[:, koff + h * DK: koff + (h + 1) * DK]

            def vc(h):
                return v_c2[:, koff + h * DK: koff + (h + 1) * DK]

            def kcp(p):
                return k_c2[:, koff + p * 128: koff + p * 128 + 128]

            def vcp(p):
                return v_c2[:, koff + p * 128: koff + p * 128 + 128]

            def skbp(p):
                return skb[:, p * 128:(p + 1) * 128]

            def dbp(p):
                return db[:, p * 128:(p + 1) * 128]

            lA = {}
            lR = {}
            for g in range(4):
                for pp in range(2):
                    p = 2 * g + pp
                    rb = 4 + p // 4
                    lA[g] = S.t(lambda e, p=p, pp=pp, g=g: e.matmul(
                        pb[g][:, pp * 256: pp * 256 + 256], ks(p), qpd(p),
                        start=(pp == 0), stop=True),
                        deps=[m_qk] + (pbfree[g] if pp == 0 else []))
                    lR[rb] = S.t(lambda e, p=p, rb=rb: e.matmul(
                        pb[rb][:, (p % 4) * 128:(p % 4) * 128 + 128],
                        skbp(p), qs(p),
                        start=(p % 4 == 0), stop=True),
                        deps=[m_qk, st_dep()] +
                             (pbfree[rb] if p % 4 == 0 else []))
            evb = {}
            for g in range(4):
                g0 = g * 512
                mBT = S.v(lambda e, g=g, g0=g0: e.tensor_tensor(
                    bts_a[:, g0:g0 + 512], pb[g][:], trium[:], ALU.mult),
                    deps=[lA[g]])
                mATs = S.s(lambda e, g=g, g0=g0: e.copy(
                    ats_a[:, g0:g0 + 512], pb[g][:]), deps=[lA[g], mBT])
                pbfree[g] = [mATs]
                evb[g] = (mBT, mATs)
            mRT2 = {}
            for rb in (4, 5):
                mRT2[rb] = S.s(lambda e, rb=rb: e.copy(
                    rts_a[:, (rb - 4) * 512:(rb - 4) * 512 + 512],
                    pb[rb][:]), deps=[lR[rb]])
                pbfree[rb] = [mRT2[rb]]

            lXU = {}
            lX2 = {}
            for g in range(4):
                mBT, mATs = evb[g]
                for pp in range(2):
                    p = 2 * g + pp
                    x2b = 4 + p // 4
                    first2 = (p % 4 == 0)
                    lX2[x2b] = S.t(lambda e, p=p, x2b=x2b, first2=first2:
                                   e.matmul(
                        pb[x2b][:, (p % 4) * 128:(p % 4) * 128 + 128],
                        qs(p), skbp(p),
                        start=first2, stop=False),
                        deps=[m_qk, st_dep()] +
                             (pbfree[x2b] if first2 else []))
                    for j2 in range(2):
                        h = 2 * p + j2
                        co = h * 128
                        lXU[g] = S.t(lambda e, co=co, h=h, g=g: e.matmul(
                            pb[g][:, (h % 4) * 128:(h % 4) * 128 + 128],
                            bts_a[:, co:co + 128], ats_a[:, co:co + 128],
                            start=(h % 4 == 0), stop=False),
                            deps=[mBT, mATs] +
                                 (pbfree[g] if h % 4 == 0 else []))
                        lX2[x2b] = S.t(lambda e, co=co, h=h, p=p, x2b=x2b,
                                       j2=j2: e.matmul(
                            pb[x2b][:, (p % 4) * 128 + j2 * DK:
                                    (p % 4) * 128 + (j2 + 1) * DK],
                            bts_a[:, co:co + 128], kc(h),
                            start=False, stop=True), deps=[mBT, m_kv])
                    lXU[g] = S.t(lambda e, p=p, pp=pp, g=g: e.matmul(
                        pb[g][:, pp * 256: pp * 256 + 256],
                        rts_a[:, p * 128:(p + 1) * 128], qpd(p),
                        start=False, stop=True), deps=[mRT2[4 + p // 4]])
            evu = {}
            for g in range(4):
                g0 = g * 512
                mUTs = S.v(lambda e, g=g, g0=g0: e.tensor_tensor(
                    uts_a[:, g0:g0 + 512], pb[g][:], trium[:], ALU.mult),
                    deps=[lXU[g]])
                pbfree[g] = [mUTs]
                evu[g] = mUTs
            for bb in (4, 5):
                co = (bb - 4) * 512
                mX2s = S.s(lambda e, bb=bb, co=co: e.copy(
                    x2s_a[:, co:co + 512], pb[bb][:]), deps=[lX2[bb]])
                pbfree[bb] = [mX2s]
                evu[bb] = mX2s

            lO = {}
            lKD = {}
            for p in range(8):
                g = p // 2
                ob = 6 + p // 4
                kb = p // 4
                db_ = 2 + p // 4
                oco = (p % 4) * 128
                for j2 in range(2):
                    h = 2 * p + j2
                    co = h * 128
                    firsto = (p % 4 == 0 and j2 == 0)
                    lO[ob] = S.t(lambda e, co=co, h=h, ob=ob, oco=oco, j2=j2,
                                 firsto=firsto: e.matmul(
                        pb[ob][:, oco + j2 * DK: oco + (j2 + 1) * DK],
                        uts_a[:, co:co + 128], vc(h),
                        start=firsto, stop=False),
                        deps=[evu[g], m_kv] + (pbfree[ob] if firsto else []))
                lO[ob] = S.t(lambda e, p=p, ob=ob, oco=oco: e.matmul(
                    pb[ob][:, oco: oco + 128], qs(p), dbp(p),
                    start=False, stop=True), deps=[st_dep()])
                lKD[kb] = S.t(lambda e, p=p, kb=kb: e.matmul(
                    pb[kb][:, (p % 4) * 128:(p % 4) * 128 + 128],
                    kcp(p), kcp(p), start=(p % 4 == 0), stop=True),
                    deps=[m_kv] + (pbfree[kb] if p % 4 == 0 else []))
                lKD[db_] = S.t(lambda e, p=p, db_=db_: e.matmul(
                    pb[db_][:, (p % 4) * 128:(p % 4) * 128 + 128],
                    x2s_a[:, p * 128:(p + 1) * 128], vcp(p),
                    start=(p % 4 == 0), stop=True),
                    deps=[evu[4 + p // 4], m_kv] +
                         (pbfree[db_] if p % 4 == 0 else []))
            m_heads_T = S.mark("T")
            ot_marks = []
            pp2o = 0
            for ob in (6, 7):
                co = (ob - 6) * 512
                mo = S.s(lambda e, ob=ob, co=co, pp2o=pp2o: e.copy(
                    o_c[pp2o][:, co:co + 512], pb[ob][:]),
                    deps=[lO[ob]] + ([oc_free[pp2o]] if ob == 6 else []))
                pbfree[ob] = [mo]
                ot_marks.append(mo)
            mupd = []
            for kb in (0, 1):
                dst = skf[0:64, kb * 512:(kb + 1) * 512].rearrange(
                    "p (pr w) -> p pr w", pr=4)[:, :, 0:64]
                src = pb[kb][0:64, :].rearrange(
                    "p (pr w) -> p pr w", pr=4)[:, :, 0:64]
                mSKa = S.v(lambda e, dst=dst, src=src, kb=kb: e.tensor_add(
                    dst, dst, src), deps=[lKD[kb]])
                dst2 = skf[64:128, kb * 512:(kb + 1) * 512].rearrange(
                    "p (pr w) -> p pr w", pr=4)[:, :, 64:128]
                src2 = pb[kb][64:128, :].rearrange(
                    "p (pr w) -> p pr w", pr=4)[:, :, 64:128]
                mSKb = S.v(lambda e, dst2=dst2, src2=src2, kb=kb: e.tensor_add(
                    dst2, dst2, src2), deps=[lKD[kb]])
                pbfree[kb] = [mSKb]
                mupd += [mSKa, mSKb]
            for db2 in (2, 3):
                dst = df[0:64, (db2 - 2) * 512:(db2 - 1) * 512].rearrange(
                    "p (pr w) -> p pr w", pr=4)[:, :, 0:64]
                src = pb[db2][0:64, :].rearrange(
                    "p (pr w) -> p pr w", pr=4)[:, :, 0:64]
                mDFa = S.v(lambda e, dst=dst, src=src: e.tensor_add(
                    dst, dst, src), deps=[lKD[db2]])
                dst2 = df[64:128, (db2 - 2) * 512:(db2 - 1) * 512].rearrange(
                    "p (pr w) -> p pr w", pr=4)[:, :, 64:128]
                src2 = pb[db2][64:128, :].rearrange(
                    "p (pr w) -> p pr w", pr=4)[:, :, 64:128]
                mDFb = S.v(lambda e, dst2=dst2, src2=src2: e.tensor_add(
                    dst2, dst2, src2), deps=[lKD[db2]])
                pbfree[db2] = [mDFb]
                mupd += [mDFa, mDFb]
            mskb = S.v(lambda e: e.tensor_copy(skb[:], skf[:]),
                       deps=mupd + [m_heads_T])
            mdbb = S.v(lambda e: e.tensor_copy(db[:], df[:]),
                       deps=[mskb])
            st_cast[0] = mdbb

            pp2 = 0
            motd = None
            lastmm = None
            for rnd in range(2):
                bank = 6 + rnd
                mm = None
                for k in range(4):
                    blk = rnd * 4 + k
                    mm = S.t(lambda e, blk=blk, k=k, bank=bank, pp2=pp2:
                             e.matmul(pb[bank][:, k * 128:(k + 1) * 128],
                                      o_c[pp2][:, blk * 128:(blk + 1) * 128],
                                      ident[:], start=(k == 0), stop=True),
                             deps=[ot_marks[blk // 4], m_aux2] +
                                  (pbfree[bank] if k == 0 else []))
                motd = S.s(lambda e, rnd=rnd, bank=bank, pp2=pp2: e.copy(
                    ot_c[pp2][:, rnd * 512:(rnd + 1) * 512], pb[bank][:]),
                    deps=[mm, ot_free[pp2]])
                pbfree[bank] = [motd]
                lastmm = mm
            oc_free[pp2] = lastmm
            hstate[ci] = (motd, mxs, half)

        def emit_wo_ln(ci, m_qk, m_kv):
            nonlocal s1_free
            motd, mxs, half = hstate[ci]
            bi4 = ci % 5
            pp2 = 0
            r0 = ci * C
            mWop = {}
            for dhalf in range(2):
                bi = 4 + dhalf
                for ic in range(8):
                    first = ic == 0
                    mWop[bi] = S.t(lambda e, ic=ic, dhalf=dhalf, bi=bi,
                                   first=first, pp2=pp2: e.matmul(
                        pb[bi][:], ot_c[pp2][:, ic * 128:(ic + 1) * 128],
                        wo_sb[:, ic * D + dhalf * 512:
                              ic * D + dhalf * 512 + 512],
                        start=first, stop=(ic == 7)),
                        deps=[motd, m_wo] + (pbfree[bi] if first else []))
            ot_free[pp2] = S.mark("T")
            mr1a = S.v(lambda e, bi4=bi4: e.tensor_add(
                x_c[bi4][:, 0:512], x_c[bi4][:, 0:512], pb[4][:]),
                deps=[mWop[4], mxs[half]])
            pbfree[4] = [mr1a]
            mr1b = S.v(lambda e, bi4=bi4: e.tensor_add(
                x_c[bi4][:, 512:1024], x_c[bi4][:, 512:1024], pb[5][:]),
                deps=[mWop[5], mxs[half]])
            pbfree[5] = [mr1b]
            mb1 = S.v(lambda e, bi4=bi4: e.bn_stats(bn6[:, 0:6],
                                                    x_c[bi4][:, 0:512]),
                      deps=[mr1a])
            mb2 = S.v(lambda e, bi4=bi4: e.bn_stats(bn6[:, 6:12],
                                                    x_c[bi4][:, 512:1024]),
                      deps=[mr1b])
            mb3 = S.v(lambda e: e.bn_aggr(bnmv[:], bn6[:]),
                      deps=[mb1, mb2, ln_free[0]])
            msq = S.s(lambda e: e.activation(rstd[:], bnmv[:, 1:2], AF.Sqrt,
                                             bias=CEPS, scale=1.0),
                      deps=[mb3, mk_cons])
            mrc = S.v(lambda e: e.reciprocal(rstd[:], rstd[:]), deps=[msq])
            mnm = S.v(lambda e: e.scalar_tensor_tensor(
                nmr[:], bnmv[:, 0:1], -1.0, rstd[:], ALU.mult, ALU.mult),
                deps=[mrc, ln_free[1]])
            mact = S.s(lambda e, bi4=bi4: e.activation(
                x_c[bi4][:], x_c[bi4][:], AF.Identity, bias=nmr[:],
                scale=rstd[:]), deps=[mnm])
            ln_free[0] = msq
            ln_free[1] = mact
            mst = S.dma("SY", st_s1[ci % 2], lambda e, r0=r0, bi4=bi4:
                        e.dma_start(
                out=s1_hbm[r0:r0 + C, :], in_=x_c[bi4][:]), deps=[mact])
            xuse[bi4] = mst
            m_s1_stores.append(mst)
            if debug and ci == 0:
                dv = S.mark("V")
                ds = S.mark("S")
                for nm, tile in [("d_qt", qs_all), ("d_kt", ks_all),
                                 ("d_kc", k_c2), ("d_vc", v_c2),
                                 ("d_ot", ot_c[0]), ("d_uts", uts_a),
                                 ("d_bts", bts_a), ("d_ats", ats_a),
                                 ("d_rts", rts_a), ("d_x2", x2s_a),
                                 ("d_skf", skf), ("d_df", df)]:
                    S.dma("GP", st_auxg, lambda e, nm=nm, tile=tile: e.dma_start(
                        out=dbg[nm], in_=tile[:]), deps=[dv, ds])

        itA = iter(p0A_units)
        itB = iter(p0B_units)
        pump_state = {"dA": 0, "dB": 0, "scales": False}

        def do_pump(n):
            for _ in range(n):
                u = next(itA, None)
                if u is not None:
                    emit_p0A(u)
                    pump_state["dA"] += 1
                    continue
                if not pump_state["scales"]:
                    emit_p0_scales()
                    pump_state["scales"] = True
                    return
                u = next(itB, None)
                if u is None:
                    return
                emit_p0B(u)
                pump_state["dB"] += 1
        p0_pump[0] = do_pump

        pend_T = {0: emit_xpipe(0)}
        for qi in range(nch // 4):
            emit_quad(qi, pend_T.pop(qi), nch // 4)
        for u in itA:
            emit_p0A(u)
        if not pump_state["scales"]:
            emit_p0_scales()
        for u in itB:
            emit_p0B(u)

        BAR = [S.mark("T"), S.mark("V"), S.mark("S"),
               (st_p0s.sem, st_p0s.count), m_s1_stores[-1],
               (st_s1[0].sem, st_s1[0].count), (st_s1[1].sem, st_s1[1].count),
               (st_xt[0].sem, st_xt[0].count), (st_xt[1].sem, st_xt[1].count),
               (st_x[0].sem, st_x[0].count), (st_x[1].sem, st_x[1].count),
               (st_xb[0].sem, st_xb[0].count), (st_xb[1].sem, st_xb[1].count),
               (st_p0t[0].sem, st_p0t[0].count), (st_p0t[1].sem, st_p0t[1].count),
               (st_p0in[0].sem, st_p0in[0].count),
               (st_p0in[1].sem, st_p0in[1].count),
               (st_aux.sem, st_aux.count), (st_auxg.sem, st_auxg.count),
               (st_red.sem, st_red.count)]
        p01.close()

        p2 = ExitStack()
        E2 = p2.enter_context
        tile_ch = 4
        ntile = nch // tile_ch if nch >= tile_ch else 1
        s1e = [E2(nc.sbuf_tensor(f"s1e{i}", [C, D], F32)) for i in range(2)]
        s1x = [E2(nc.sbuf_tensor(f"s1x{i}", [C, D], F32)) for i in range(2)]
        xq = [E2(nc.sbuf_tensor(f"xq{i}", [C, D], BF16)) for i in range(4)]
        xqT = [E2(nc.sbuf_tensor(f"xqT{i}", [128, tile_ch * D], BF16))
               for i in range(2)]
        ht = E2(nc.sbuf_tensor("ht", [C, tile_ch * INNER], F32))
        wgub = [E2(nc.sbuf_tensor(f"wgub{i}", [128, 8192], BF16)) for i in range(2)]
        hq = [E2(nc.sbuf_tensor(f"hq{i}", [C, INNER], BF16)) for i in range(2)]
        hqT = E2(nc.sbuf_tensor("hqT", [128, tile_ch * INNER], BF16))
        wdpb = [E2(nc.sbuf_tensor(f"wdpb{i}", [128, 2 * D], BF16)) for i in range(2)]
        qzp = E2(nc.sbuf_tensor("qzp", [128, 512], F32))
        qcp = [E2(nc.sbuf_tensor(f"qcp{i}", [128, 512], I32)) for i in range(2)]
        am_parts = E2(nc.sbuf_tensor("am_parts", [C, 32], F32))
        r2t = [E2(nc.sbuf_tensor(f"r2t{i}", [C, D], F32)) for i in range(2)]
        am2 = E2(nc.sbuf_tensor("am2", [C, 4], F32))
        sv2 = E2(nc.sbuf_tensor("sv2", [C, 2], F32))
        dscl = E2(nc.sbuf_tensor("dscl", [C, 16], F32))
        bn6b = E2(nc.sbuf_tensor("bn6b", [C, 12], F32))
        bnmvb = E2(nc.sbuf_tensor("bnmvb", [C, 2], F32))
        rstd2 = E2(nc.sbuf_tensor("rstd2", [C, 1], F32))
        nmr2 = E2(nc.sbuf_tensor("nmr2", [C, 1], F32))
        pb2 = [E2(nc.psum_tensor(f"pb2_{i}", [128, 512], F32)) for i in range(8)]

        BAR1 = BAR[0]
        wgu_free = [BAR1, BAR1]
        wdp_free = [BAR1, BAR1]
        s1x_free = [BAR1, BAR1]
        s1e_free = [BAR1, BAR1]
        pb2free = [[BAR[1]]] * 8
        hq_free = [BAR1, BAR1]
        hqT_free = BAR1
        xq_free = [BAR1, BAR1, BAR1, BAR1]
        xqT_free = [BAR1, BAR1]
        r2_free = [BAR[2], BAR[2]]
        y_free = [(st_y[0].sem, 0), (st_y[1].sem, 0)]
        qtmp_free = BAR[1]
        p2first = list(BAR)
        wg_done = [p0_wdone.get(0), p0_wdone.get(1), p0_wdone.get(2)]
        s1_all = [BAR[5], BAR[6]]
        ln2_free = [BAR[2], BAR[2]]

        qsub = [0]
        qcp_free = [BAR[1], BAR[1]]

        def quant_to(src_ap, dst_ap, svcol, deps, width=512):
            nonlocal qtmp_free
            par = qsub[0]; qsub[0] ^= 1
            mzp = S.v(lambda e, src_ap=src_ap, svcol=svcol, width=width:
                      e.tensor_scalar(qzp[:, 0:width], src_ap, svcol,
                                      MRND - 0.5, ALU.mult, ALU.add),
                      deps=list(deps) + [qtmp_free])
            mcp = S.s(lambda e, src_ap=src_ap, svcol=svcol, width=width,
                      par=par:
                      e.activation(qcp[par][:, 0:width], src_ap, AF.Identity,
                                   bias=CHI, scale=svcol),
                      deps=list(deps) + [qcp_free[par]])
            mt1 = S.v(lambda e, width=width, par=par: e.tensor_tensor(
                qzp[:, 0:width], qzp[:, 0:width], qcp[par][:, 0:width],
                ALU.is_gt), deps=[mzp, mcp])
            mr = S.v(lambda e, dst_ap=dst_ap, width=width, par=par:
                     e.scalar_tensor_tensor(
                dst_ap, qcp[par][:, 0:width], -MRND, qzp[:, 0:width],
                ALU.add, ALU.add), deps=[mt1])
            qcp_free[par] = mr
            qtmp_free = mr
            return mr

        m_y_fin = []
        tile_state = {}

        def emit_xq_prep_q(ti, lo, hi):
            c0 = ti * tile_ch
            qm = {}
            for cc in range(lo, hi):
                ci = c0 + cc
                sb = cc % 2
                r0 = ci * C
                ms1 = S.dma("SY", st_s1l[sb], lambda e, r0=r0, sb=sb: e.dma_start(
                    out=s1x[sb][:], in_=s1_hbm[r0:r0 + C, :]),
                    deps=[s1x_free[sb]] + s1_all + p2first)
                src2 = s1x[sb]
                mam = S.v(lambda e, src2=src2: e.reduce_max(
                    am2[:, 0:1], src2[:], axis=AX.X, apply_absolute_value=True),
                    deps=[ms1] + p2first)
                mamc = S.v(lambda e: e.tensor_scalar(
                    am2[:, 1:2], am2[:, 0:1], 1e-5, None, ALU.max), deps=[mam])
                mrc = S.v(lambda e: e.reciprocal(sv2[:, 0:1], am2[:, 1:2]),
                          deps=[mamc])
                msc = S.v(lambda e: e.tensor_scalar(
                    sv2[:, 0:1], sv2[:, 0:1], 127.0, None, ALU.mult), deps=[mrc])
                S.v(lambda e, cc=cc: e.scalar_tensor_tensor(
                    dscl[:, cc:cc + 1], am2[:, 1:2], 1.0 / 127.0, swv[:, 4:5],
                    ALU.mult, ALU.mult), deps=[mamc] + sw_ready)
                S.v(lambda e, cc=cc: e.scalar_tensor_tensor(
                    dscl[:, 4 + cc:5 + cc], am2[:, 1:2], 1.0 / 127.0,
                    swv[:, 5:6], ALU.mult, ALU.mult), deps=[mamc])
                mq1 = quant_to(src2[:, 0:512], xq[cc % 4][:, 0:512],
                               sv2[:, 0:1], [msc, xq_free[cc % 4]])
                mq2 = quant_to(src2[:, 512:1024], xq[cc % 4][:, 512:1024],
                               sv2[:, 0:1], [msc])
                s1x_free[sb] = mq2
                qm[cc] = (mq1, mq2)
            return qm

        def emit_xq_prep_T(ti, lo, hi, qm):
            tp2 = ti % 2
            mxqt = None
            for cc in range(lo, hi):
                mq1, mq2 = qm[cc]
                lastmm = None
                for rnd in range(2):
                    bank = 4 + rnd
                    mm = None
                    for k in range(4):
                        blk = rnd * 4 + k
                        mm = S.t(lambda e, blk=blk, k=k, bank=bank, cc=cc:
                                 e.matmul(pb2[bank][:, k * 128:(k + 1) * 128],
                                          xq[cc % 4][:, blk * 128:(blk + 1) * 128],
                                          ident[:], start=(k == 0), stop=True),
                                 deps=[mq1, mq2, m_aux2] +
                                      (pb2free[bank] if k == 0 else []))
                    mxqt = S.s(lambda e, cc=cc, rnd=rnd, bank=bank, tp2=tp2:
                               e.copy(
                        xqT[tp2][:, cc * D + rnd * 512: cc * D + (rnd + 1) * 512],
                        pb2[bank][:]), deps=[mm, xqT_free[tp2]])
                    pb2free[bank] = [mxqt]
                    lastmm = mm
                xq_free[cc % 4] = lastmm
            return mxqt

        def emit_stageA(ti, m_xqT):
            tp2 = ti % 2
            h_marks = [None] * tile_ch
            for pcs in range(8):
                pbuf = pcs % 2
                mldg = S.dma("GP", st_wgu[pbuf],
                             lambda e, pcs=pcs, pbuf=pbuf: e.dma_start(
                                 out=wgub[pbuf].ap()[:, 0:4096].rearrange(
                                     "p (ic n) -> p ic n", ic=8),
                                 in_=wgT_hbm[:, :, pcs * 512:(pcs + 1) * 512]
                                 .transpose([1, 0, 2])),
                             deps=[wgu_free[pbuf], wg_done[0]] + p2first)
                mldu = S.dma("GP", st_wgu[pbuf],
                             lambda e, pcs=pcs, pbuf=pbuf: e.dma_start(
                                 out=wgub[pbuf].ap()[:, 4096:8192].rearrange(
                                     "p (ic n) -> p ic n", ic=8),
                                 in_=wuT_hbm[:, :, pcs * 512:(pcs + 1) * 512]
                                 .transpose([1, 0, 2])),
                             deps=[wg_done[1]])
                for cc in range(tile_ch):
                    bg = 4 + (cc % 2) * 2
                    bu = bg + 1
                    lg = lu = None
                    for ic in range(8):
                        first = ic == 0
                        lg = S.t(lambda e, ic=ic, cc=cc, pbuf=pbuf, bg=bg,
                                 first=first, tp2=tp2:
                                 e.matmul(pb2[bg][:],
                                          xqT[tp2][:, cc * D + ic * 128:
                                                   cc * D + (ic + 1) * 128],
                                          wgub[pbuf][:, ic * 512:(ic + 1) * 512],
                                          start=first, stop=(ic == 7)),
                                 deps=[m_xqT, mldu] +
                                      (pb2free[bg] if first else []))
                        lu = S.t(lambda e, ic=ic, cc=cc, pbuf=pbuf, bu=bu,
                                 first=first, tp2=tp2:
                                 e.matmul(pb2[bu][:],
                                          xqT[tp2][:, cc * D + ic * 128:
                                                   cc * D + (ic + 1) * 128],
                                          wgub[pbuf][:, 4096 + ic * 512:
                                                     4096 + (ic + 1) * 512],
                                          start=first, stop=(ic == 7)),
                                 deps=[mldu] + (pb2free[bu] if first else []))
                    hsl = slice(cc * INNER + pcs * 512,
                                cc * INNER + (pcs + 1) * 512)
                    pq = (tile_state[ti - 1]["qmarks"][cc][pcs]
                          if ti > 0 else BAR[1])
                    msil = S.s(lambda e, cc=cc, bg=bg, hsl=hsl, pq=pq:
                               e.activation(
                        ht[:, hsl], pb2[bg][:], AF.Silu, bias=CZERO,
                        scale=dscl[:, cc:cc + 1]), deps=[lg, pq])
                    pb2free[bg] = [msil]
                    mh = S.v(lambda e, cc=cc, pcs=pcs, bu=bu, hsl=hsl:
                             e.scalar_tensor_tensor(
                        ht[:, hsl],
                        pb2[bu][:], dscl[:, 4 + cc:5 + cc], ht[:, hsl],
                        ALU.mult, ALU.mult), deps=[msil, lu])
                    pb2free[bu] = [mh]
                    mam = S.v(lambda e, cc=cc, pcs=pcs: e.reduce_max(
                        am_parts[:, cc * 8 + pcs: cc * 8 + pcs + 1],
                        ht[:, cc * INNER + pcs * 512:
                           cc * INNER + (pcs + 1) * 512],
                        axis=AX.X, apply_absolute_value=True), deps=[mh])
                    h_marks[cc] = mam
                wgu_free[pbuf] = S.mark("T")
            tile_state[ti] = {"h_marks": h_marks}

        def emit_quant(ti, lo, hi):
            nonlocal hq_free
            h_marks = tile_state[ti]["h_marks"]
            qmarks = tile_state[ti].setdefault("qmarks", {})
            for cc in range(lo, hi):
                mam = S.v(lambda e, cc=cc: e.reduce_max(
                    am2[:, 2:3], am_parts[:, cc * 8:(cc + 1) * 8], axis=AX.X),
                    deps=[h_marks[cc]])
                mamc = S.v(lambda e: e.tensor_scalar(
                    am2[:, 3:4], am2[:, 2:3], 1e-5, None, ALU.max), deps=[mam])
                mrc = S.v(lambda e: e.reciprocal(sv2[:, 1:2], am2[:, 3:4]),
                          deps=[mamc])
                msc = S.v(lambda e: e.tensor_scalar(
                    sv2[:, 1:2], sv2[:, 1:2], 127.0, None, ALU.mult), deps=[mrc])
                dco = 8 + (ti % 2) * 4 + cc
                S.v(lambda e, dco=dco: e.scalar_tensor_tensor(
                    dscl[:, dco:dco + 1], am2[:, 3:4], 1.0 / 127.0,
                    swv[:, 6:7], ALU.mult, ALU.mult), deps=[mamc] + sw_ready)
                subm = []
                for sub in range(8):
                    subm.append(quant_to(
                        ht[:, cc * INNER + sub * 512:
                           cc * INNER + (sub + 1) * 512],
                        hq[cc % 2][:, sub * 512:(sub + 1) * 512], sv2[:, 1:2],
                        [msc, hq_free[cc % 2]]))
                qmarks[cc] = subm

        def emit_hqT(ti, lo, hi):
            qmarks = tile_state[ti]["qmarks"]
            hqt_marks = tile_state[ti].setdefault("hqt_marks", {})
            for cc in range(lo, hi):
                subm = qmarks[cc]
                lastmm = None
                mev = None
                for rnd in range(8):
                    bank = rnd % 2
                    mm = None
                    for k in range(4):
                        blk = rnd * 4 + k
                        mm = S.t(lambda e, blk=blk, k=k, bank=bank, cc=cc:
                                 e.matmul(pb2[bank][:, k * 128:(k + 1) * 128],
                                          hq[cc % 2][:, blk * 128:(blk + 1) * 128],
                                          ident[:], start=(k == 0), stop=True),
                                 deps=[subm[rnd], m_aux2] +
                                      (pb2free[bank] if k == 0 else []))
                    mev = S.s(lambda e, cc=cc, rnd=rnd, bank=bank: e.copy(
                        hqT[:, cc * INNER + rnd * 512:
                            cc * INNER + (rnd + 1) * 512],
                        pb2[bank][:]), deps=[mm, hqT_free])
                    pb2free[bank] = [mev]
                    lastmm = mm
                hq_free[cc % 2] = lastmm
                hqt_marks[cc] = mev

        def emit_down_ep(ti):
            nonlocal hqT_free
            c0 = ti * tile_ch
            hqt_marks = tile_state[ti]["hqt_marks"]
            lastd = {}
            for pcs in range(16):
                pbuf = pcs % 2
                mldd = S.dma("GP", st_wdp[pbuf],
                             lambda e, pcs=pcs, pbuf=pbuf: e.dma_start(
                                 out=wdpb[pbuf].ap().rearrange(
                                     "p (i2 o) -> p i2 o", i2=2),
                                 in_=wdT_hbm[pcs * 2:(pcs + 1) * 2, :, :]
                                 .transpose([1, 0, 2])),
                             deps=[wdp_free[pbuf], wg_done[2]] + p2first)
                for cc in range(tile_ch):
                    for i2l in range(2):
                        i2 = pcs * 2 + i2l
                        for dh in range(2):
                            bi = cc * 2 + dh
                            first = pcs == 0 and i2l == 0
                            lastd[bi] = S.t(
                                lambda e, cc=cc, i2=i2, i2l=i2l, dh=dh,
                                bi=bi, pbuf=pbuf, first=first:
                                e.matmul(pb2[bi][:],
                                         hqT[:, cc * INNER + i2 * 128:
                                             cc * INNER + (i2 + 1) * 128],
                                         wdpb[pbuf][:, i2l * D + dh * 512:
                                                    i2l * D + dh * 512 + 512],
                                         start=first,
                                         stop=(pcs == 15 and i2l == 1)),
                                deps=[hqt_marks[cc], mldd] +
                                     (pb2free[bi] if first else []))
                wdp_free[pbuf] = S.mark("T")
            for cc in range(tile_ch):
                ci = c0 + cc
                r0 = ci * C
                eb = ci % 2
                yb = ci % 2
                ms1 = S.dma("SY", st_s1e[eb], lambda e, r0=r0, eb=eb:
                            e.dma_start(out=s1e[eb][:],
                                        in_=s1_hbm[r0:r0 + C, :]),
                            deps=[s1e_free[eb]] + s1_all)
                dco = 8 + (ti % 2) * 4 + cc
                md0 = S.s(lambda e, cc=cc, eb=eb, dco=dco: e.activation(
                    r2t[eb][:, 0:512], pb2[cc * 2][:], AF.Identity,
                    bias=CZERO, scale=dscl[:, dco:dco + 1]),
                    deps=[lastd[cc * 2], r2_free[eb]])
                pb2free[cc * 2] = [md0]
                md1 = S.s(lambda e, cc=cc, eb=eb, dco=dco: e.activation(
                    r2t[eb][:, 512:1024], pb2[cc * 2 + 1][:], AF.Identity,
                    bias=CZERO, scale=dscl[:, dco:dco + 1]),
                    deps=[lastd[cc * 2 + 1]])
                pb2free[cc * 2 + 1] = [md1]
                mr2 = S.v(lambda e, eb=eb: e.tensor_add(
                    r2t[eb][:], r2t[eb][:], s1e[eb][:]),
                    deps=[md0, md1, ms1])
                s1e_free[eb] = mr2
                mb1 = S.v(lambda e, eb=eb: e.bn_stats(
                    bn6b[:, 0:6], r2t[eb][:, 0:512]), deps=[mr2])
                mb2 = S.v(lambda e, eb=eb: e.bn_stats(
                    bn6b[:, 6:12], r2t[eb][:, 512:1024]), deps=[mr2])
                mb3 = S.v(lambda e: e.bn_aggr(bnmvb[:], bn6b[:]),
                          deps=[mb1, mb2, ln2_free[0]])
                msq = S.s(lambda e: e.activation(
                    rstd2[:], bnmvb[:, 1:2], AF.Sqrt, bias=CEPS, scale=1.0),
                    deps=[mb3])
                mrc2 = S.v(lambda e: e.reciprocal(rstd2[:], rstd2[:]),
                           deps=[msq])
                mnm2 = S.v(lambda e: e.scalar_tensor_tensor(
                    nmr2[:], bnmvb[:, 0:1], -1.0, rstd2[:],
                    ALU.mult, ALU.mult), deps=[mrc2, ln2_free[1]])
                mact = S.s(lambda e, eb=eb: e.activation(
                    r2t[eb][:], r2t[eb][:], AF.Identity, bias=nmr2[:],
                    scale=rstd2[:]), deps=[mnm2, y_free[yb]])
                ln2_free[0] = msq
                ln2_free[1] = mact
                mys = S.dma("SY", st_y[yb], lambda e, r0=r0, eb=eb:
                            e.dma_start(out=y_d[r0:r0 + C, :],
                                        in_=r2t[eb][:]), deps=[mact])
                y_free[yb] = mys
                r2_free[eb] = mys
                m_y_fin.append(mys)
            hqT_free = S.mark("T")
            xqT_free[ti % 2] = S.mark("T")

        prev_qm = None
        for ti in range(ntile):
            qm = prev_qm if prev_qm is not None else emit_xq_prep_q(ti, 0, 4)
            mx = emit_xq_prep_T(ti, 0, 4, qm)
            emit_stageA(ti, mx)
            emit_quant(ti, 0, 2)
            if ti > 0:
                emit_down_ep(ti - 1)
            emit_hqT(ti, 0, 2)
            emit_quant(ti, 2, 4)
            if ti + 1 < ntile:
                prev_qm = emit_xq_prep_q(ti + 1, 0, 4)
            emit_hqT(ti, 2, 4)
            p2first = []
        emit_down_ep(ntile - 1)

        def fin(e):
            return e.nop()
        S.ops["SY"].append((S._waits("SY", m_y_fin[-2:] +
                                     [(st_y[0].sem, st_y[0].count),
                                      (st_y[1].sem, st_y[1].count)]),
                            fin, None))

        blk = E(nc.Block())

        def runner(key):
            def go(eng):
                for waits, fn, inc in S.ops[key]:
                    for (sem, val) in waits:
                        eng.wait_ge(sem, val)
                    ins = fn(eng)
                    if inc is not None:
                        ins.then_inc(inc[0], inc[1])
            return go

        blk.tensor(runner("T"))
        blk.vector(runner("V"))
        blk.scalar(runner("S"))
        blk.sync(runner("SY"))
        blk.gpsimd(runner("GP"))

        p2.close()
    return nc


_NC_CACHE = {}


def _get_nc():
    if "nc" not in _NC_CACHE:
        _NC_CACHE["nc"] = build_program()
    return _NC_CACHE["nc"]


def kernel(L0_attn_x, L0_attn_W_q, L0_attn_W_k, L0_attn_W_v, L0_attn_W_o,
           L0_ffn_w_gate, L0_ffn_w_up, L0_ffn_w_down):
    x = np.ascontiguousarray(np.asarray(L0_attn_x, dtype=np.float32))
    wq = np.ascontiguousarray(np.asarray(L0_attn_W_q, dtype=np.float32))
    wk = np.ascontiguousarray(np.asarray(L0_attn_W_k, dtype=np.float32))
    wv = np.ascontiguousarray(np.asarray(L0_attn_W_v, dtype=np.float32))
    wo = np.ascontiguousarray(np.asarray(L0_attn_W_o, dtype=np.float32))
    wg = np.ascontiguousarray(np.asarray(L0_ffn_w_gate, dtype=np.float32))
    wu = np.ascontiguousarray(np.asarray(L0_ffn_w_up, dtype=np.float32))
    wd = np.ascontiguousarray(np.asarray(L0_ffn_w_down, dtype=np.float32))

    tril = np.tril(np.ones((C, C), np.float32))
    trium = np.tile(tril.T, (1, 4)).astype(ml_dtypes.bfloat16)
    ident = np.eye(C, dtype=np.float32).astype(ml_dtypes.bfloat16)

    nc = _get_nc()
    in_maps = []
    for b in range(NCORE):
        in_maps.append(dict(x=x[b], wq=wq, wk=wk, wv=wv, wo=wo,
                            wg=wg, wu=wu, wd=wd, trium=trium, ident=ident))
    res = run_bass_kernel_spmd(nc, in_maps, core_ids=list(range(NCORE)))
    return np.stack([res.results[b]["y"] for b in range(NCORE)], axis=0)


# version 24
# speedup vs baseline: 1.0786x; 1.0041x over previous
import numpy as np
import ml_dtypes
from contextlib import ExitStack

import concourse.bass as bass
import concourse.mybir as mybir
from concourse.bass_utils import run_bass_kernel_spmd

F32 = mybir.dt.float32
BF16 = mybir.dt.bfloat16
I32 = mybir.dt.int32
AF = mybir.ActivationFunctionType
ALU = mybir.AluOpType
AX = mybir.AxisListType

D = 1024
T = 4096
C = 128
H = 16
DK = 64
INNER = 4096
MRND = 256.0
NCORE = 8


class Sched:

    def __init__(self, nc):
        self.nc = nc
        self.ops = {k: [] for k in ("T", "V", "S", "SY", "GP")}
        self.cnt = {"T": 0, "V": 0, "S": 0}
        self.lastwait = {}
        self.sems = {}

    def set_engine_sems(self, semT, semV, semS):
        self.sems = {"T": semT, "V": semV, "S": semS}

    def _waits(self, key, deps):
        out = []
        for dp in deps:
            if dp is None:
                continue
            sem, val = dp
            if val <= 0:
                continue
            k = (key, id(sem))
            if self.lastwait.get(k, -1) < val:
                self.lastwait[k] = val
                out.append((sem, val))
        return out

    def _emit(self, key, fn, deps, inc):
        self.ops[key].append((self._waits(key, deps), fn, inc))

    def t(self, fn, deps=()):
        self.cnt["T"] += 1
        self._emit("T", fn, deps, (self.sems["T"], 1))
        return (self.sems["T"], self.cnt["T"])

    def v(self, fn, deps=()):
        self.cnt["V"] += 1
        self._emit("V", fn, deps, (self.sems["V"], 1))
        return (self.sems["V"], self.cnt["V"])

    def s(self, fn, deps=()):
        self.cnt["S"] += 1
        self._emit("S", fn, deps, (self.sems["S"], 1))
        return (self.sems["S"], self.cnt["S"])

    def dma(self, q, stream, fn, deps=()):
        stream.count += 16
        self._emit(q, fn, deps, (stream.sem, 16))
        return (stream.sem, stream.count)

    def mark(self, key):
        return (self.sems[key], self.cnt[key])


class Stream:
    def __init__(self, sem):
        self.sem = sem
        self.count = 0


def build_program(nch=T // C, ntile=8, tile_ch=4, debug=False):
    nc = bass.Bass("TRN2", target_bir_lowering=False)

    x_d = nc.dram_tensor("x", [T, D], F32, kind="ExternalInput").ap()
    wq_d = nc.dram_tensor("wq", [D, D], F32, kind="ExternalInput").ap()
    wk_d = nc.dram_tensor("wk", [D, D], F32, kind="ExternalInput").ap()
    wv_d = nc.dram_tensor("wv", [D, D], F32, kind="ExternalInput").ap()
    wo_d = nc.dram_tensor("wo", [D, D], F32, kind="ExternalInput").ap()
    wg_d = nc.dram_tensor("wg", [INNER, D], F32, kind="ExternalInput").ap()
    wu_d = nc.dram_tensor("wu", [INNER, D], F32, kind="ExternalInput").ap()
    wd_d = nc.dram_tensor("wd", [D, INNER], F32, kind="ExternalInput").ap()
    trium_d = nc.dram_tensor("trium", [C, 4 * C], BF16, kind="ExternalInput").ap()
    ident_d = nc.dram_tensor("ident", [C, C], BF16, kind="ExternalInput").ap()
    y_d = nc.dram_tensor("y", [T, D], F32, kind="ExternalOutput").ap()

    s1_hbm = nc.dram_tensor("s1_hbm", [T, D], F32, kind="ExternalOutput" if debug else "Internal").ap()
    wgT_hbm = nc.dram_tensor("wgT_hbm", [8, 128, INNER], BF16, kind="Internal").ap()
    wuT_hbm = nc.dram_tensor("wuT_hbm", [8, 128, INNER], BF16, kind="Internal").ap()
    wdT_hbm = nc.dram_tensor("wdT_hbm", [32, 128, D], BF16, kind="Internal").ap()
    red_hbm = nc.dram_tensor("red_hbm", [128, 4], F32, kind="Internal").ap()
    dbg = {}
    if debug:
        for nm, shp in [("d_qt", [128, 8 * 4 * C]), ("d_kt", [128, 8 * 4 * C]),
                        ("d_kc", [C, 4 * D]), ("d_vc", [C, 4 * D]),
                        ("d_ot", [128, 8 * C]),
                        ("d_uts", [C, 16 * C]), ("d_bts", [C, 16 * C]),
                        ("d_ats", [C, 16 * C]), ("d_rts", [128, 8 * C]),
                        ("d_x2", [C, H * DK]), ("d_skf", [128, H * DK]),
                        ("d_df", [128, H * DK]), ("d_attn", [C, D])]:
            dbg[nm] = nc.dram_tensor(nm, shp, F32, kind="ExternalOutput").ap()
        for nm, shp in [("d_h", [C, INNER]), ("d_hq", [C, INNER]),
                        ("d_am", [C, 4]), ("d_xq", [C, D]), ("d_dsc", [C, 12])]:
            dbg[nm] = nc.dram_tensor(nm, shp, F32, kind="ExternalOutput").ap()
    sw_hbm = nc.dram_tensor("sw_hbm", [8, 1], F32, kind="Internal").ap()

    S = Sched(nc)
    NG = 4

    with ExitStack() as st:
        E = st.enter_context
        semT = E(nc.semaphore("semT"))
        semV = E(nc.semaphore("semV"))
        semS = E(nc.semaphore("semS"))
        S.set_engine_sems(semT, semV, semS)

        def mkstream(name):
            return Stream(E(nc.semaphore(name)))
        st_aux = mkstream("st_aux")
        st_auxg = mkstream("st_auxg")
        st_x = [mkstream(f"st_x{i}") for i in range(5)]
        st_xb = [mkstream("st_xb0"), mkstream("st_xb1")]
        st_xt = [mkstream("st_xt0"), mkstream("st_xt1")]
        st_s1 = [mkstream("st_s10"), mkstream("st_s11")]
        st_p0in = [mkstream("st_p0a"), mkstream("st_p0b")]
        st_p0t = [mkstream("st_p0t0"), mkstream("st_p0t1")]
        st_p0s = mkstream("st_p0s")
        st_red = mkstream("st_red")
        st_rd2 = [mkstream(f"st_rd2_{i}") for i in range(3)]
        st_s1l = [mkstream("st_s1l0"), mkstream("st_s1l1")]
        st_xqt = [mkstream("st_xqt0"), mkstream("st_xqt1")]
        st_hqt = [mkstream(f"st_hqt{i}") for i in range(4)]
        st_wgu = [mkstream("st_wgu0"), mkstream("st_wgu1")]
        st_wdp = [mkstream("st_wdp0"), mkstream("st_wdp1")]
        st_s1e = [mkstream("st_s1e0"), mkstream("st_s1e1")]
        st_y = [mkstream("st_y0"), mkstream("st_y1")]
        st_ot = [mkstream("st_ot0"), mkstream("st_ot1")]

        trium = E(nc.sbuf_tensor("trium_sb", [C, 4 * C], BF16))
        ident = E(nc.sbuf_tensor("ident_sb", [C, C], BF16))
        cons = E(nc.sbuf_tensor("cons", [128, 4], F32))
        swv = E(nc.sbuf_tensor("swv", [128, 8], F32))
        CEPS = cons[:, 0:1]
        CLO = cons[:, 1:2]
        CHI = cons[:, 2:3]
        CZERO = cons[:, 3:4]
        CNHALF = swv[:, 7:8]

        st_aux2 = mkstream("st_aux2")
        m_aux1 = S.dma("SY", st_aux, lambda e: e.dma_start(out=trium[:], in_=trium_d))
        m_aux2 = S.dma("SY", st_aux2, lambda e: e.dma_start(out=ident[:], in_=ident_d))
        S.v(lambda e: e.memset(cons[:, 0:1], 1e-5))
        S.v(lambda e: e.memset(cons[:, 1:2], MRND - 0.5))
        S.v(lambda e: e.memset(cons[:, 2:3], MRND))
        S.v(lambda e: e.memset(cons[:, 3:4], 0.0))
        mk_cons = S.v(lambda e: e.memset(swv[:, 7:8], -0.5))

        p01 = ExitStack()
        E1 = p01.enter_context
        p0in = [E1(nc.sbuf_tensor(f"p0in{i}", [128, D], F32)) for i in range(2)]
        p0bf = [E1(nc.sbuf_tensor(f"p0bf{i}", [128, D], BF16)) for i in range(2)]
        p0t1 = E1(nc.sbuf_tensor("p0t1", [128, 512], F32))
        p0t2 = E1(nc.sbuf_tensor("p0t2", [128, 512], F32))
        p0tr = [E1(nc.sbuf_tensor(f"p0tr{i}", [128, 1024], BF16)) for i in range(1)]
        p0acc = E1(nc.sbuf_tensor("p0acc", [128, 4], F32))
        p0red = E1(nc.sbuf_tensor("p0red", [1, 128], F32))
        p0sw = E1(nc.sbuf_tensor("p0sw", [1, 8], F32))

        wq_sb = E1(nc.sbuf_tensor("wq_sb", [128, 8 * D], BF16))
        wk_sb = E1(nc.sbuf_tensor("wk_sb", [128, 8 * D], BF16))
        wv_sb = E1(nc.sbuf_tensor("wv_sb", [128, 8 * D], BF16))
        wo_sb = E1(nc.sbuf_tensor("wo_sb", [128, 8 * D], BF16))
        x_c = [E1(nc.sbuf_tensor(f"x_c{i}", [C, D], F32)) for i in range(5)]
        xb_c = [E1(nc.sbuf_tensor(f"xb_c{i}", [C, D], BF16)) for i in range(2)]
        xT2 = [E1(nc.sbuf_tensor(f"xT2_{i}", [128, 4 * D], BF16)) for i in range(2)]
        qs_all = E1(nc.sbuf_tensor("qs_all", [128, 8 * 4 * C], BF16))
        ks_all = E1(nc.sbuf_tensor("ks_all", [128, 8 * 4 * C], BF16))
        qpd_all = E1(nc.sbuf_tensor("qpd_all", [128, 8 * 4 * 2 * C], BF16))
        k_c2 = E1(nc.sbuf_tensor("k_c2", [C, 4 * D], BF16))
        v_c2 = E1(nc.sbuf_tensor("v_c2", [C, 4 * D], BF16))
        bts_a = E1(nc.sbuf_tensor("bts_a", [C, 16 * C], BF16))
        ats_a = E1(nc.sbuf_tensor("ats_a", [C, 16 * C], BF16))
        uts_a = E1(nc.sbuf_tensor("uts_a", [C, 16 * C], BF16))
        rts_a = E1(nc.sbuf_tensor("rts_a", [128, 8 * C], BF16))
        x2s_a = E1(nc.sbuf_tensor("x2s_a", [C, H * DK], BF16))
        o_c = [E1(nc.sbuf_tensor(f"o_c{i}", [C, D], BF16)) for i in range(1)]
        ot_c = [E1(nc.sbuf_tensor(f"ot_c{i}", [128, 8 * C], BF16)) for i in range(1)]
        skf = E1(nc.sbuf_tensor("skf", [128, H * DK], F32))
        df = E1(nc.sbuf_tensor("df", [128, H * DK], F32))
        skb = E1(nc.sbuf_tensor("skb", [128, H * DK], BF16))
        db = E1(nc.sbuf_tensor("db", [128, H * DK], BF16))
        bn6 = E1(nc.sbuf_tensor("bn6", [C, 12], F32))
        bnmv = E1(nc.sbuf_tensor("bnmv", [C, 2], F32))
        rstd = E1(nc.sbuf_tensor("rstd", [C, 1], F32))
        nmr = E1(nc.sbuf_tensor("nmr", [C, 1], F32))
        pb = [E1(nc.psum_tensor(f"pb{i}", [128, 512], F32)) for i in range(8)]

        def ld_w(sb, dr):
            return S.dma("GP", st_auxg, lambda e, sb=sb, dr=dr: e.dma_start(
                out=sb.ap().rearrange("p (ic c) -> p ic c", ic=8),
                in_=dr.rearrange("(ic p) c -> p ic c", p=128)))
        ld_w(wq_sb, wq_d)
        ld_w(wk_sb, wk_d)
        ld_w(wv_sb, wv_d)
        ld_w(wo_sb, wo_d)
        m_aux_all = (st_aux.sem, st_aux.count)
        m_auxg_all = (st_auxg.sem, st_auxg.count)

        m_st0 = [S.v(lambda e, t=t_: e.memset(t[:], 0.0))
                 for t_ in (skf, df, skb, db, qpd_all)]
        w_dep = [m_aux_all, m_auxg_all, m_aux1, mk_cons] + m_st0
        m_wo = m_auxg_all

        W3 = [(wg_d, wgT_hbm), (wu_d, wuT_hbm), (wd_d, wdT_hbm)]
        p0A_units = []
        p0B_units = []
        for wi, (wdr, _) in enumerate(W3):
            rows, cols = wdr.shape[0], wdr.shape[1]
            for b in range(rows // 128):
                for ss in range(cols // D):
                    p0A_units.append((wi, b, ss))
                    p0B_units.append((wi, b, ss))

        p0st = {
            "ldbuf": 0, "use": [(semV, 0), (semV, 0)],
            "accfirst": [True, True, True], "Adone": (semV, 0),
            "trbuf": 0, "tr_free": [(st_p0s.sem, 0), (st_p0s.sem, 0)],
            "bank": 0,
            "bfbuf": 0, "bf_free": [[(semT, 0)], [(semT, 0)]],
            "qfree": (semV, 0),
        }
        sw_ready = []
        p0_wdone = {}

        def emit_p0A(u):
            wi, b, ss = u
            buf = p0st["ldbuf"]; p0st["ldbuf"] ^= 1
            wdr = W3[wi][0]
            ldm = S.dma("GP", st_p0in[buf],
                        lambda e, wdr=wdr, b=b, ss=ss, buf=buf: e.dma_start(
                            out=p0in[buf][:],
                            in_=wdr[b * 128:(b + 1) * 128, ss * D:(ss + 1) * D]),
                        deps=[p0st["use"][buf]])
            m1 = S.v(lambda e, buf=buf: e.reduce_sum(
                p0acc[:, 3:4], p0in[buf][:], axis=AX.X, apply_absolute_value=True),
                deps=[ldm, p0st["Adone"]])
            if p0st["accfirst"][wi]:
                p0st["accfirst"][wi] = False
                m2 = S.v(lambda e, wi=wi: e.tensor_copy(
                    p0acc[:, wi:wi + 1], p0acc[:, 3:4]), deps=[m1])
            else:
                m2 = S.v(lambda e, wi=wi: e.tensor_add(
                    p0acc[:, wi:wi + 1], p0acc[:, wi:wi + 1], p0acc[:, 3:4]), deps=[m1])
            p0st["use"][buf] = m2
            p0st["Adone"] = m2

        def emit_p0_scales():
            mst = S.dma("SY", st_red, lambda e: e.dma_start(
                out=red_hbm[:, 0:3], in_=p0acc[:, 0:3]), deps=[p0st["Adone"]])
            for wi in range(3):
                nel = float(W3[wi][0].shape[0] * W3[wi][0].shape[1])
                mld = S.dma("SY", st_rd2[wi], lambda e, wi=wi: e.dma_start(
                    out=p0red[:], in_=red_hbm[:, wi:wi + 1].transpose([1, 0]),
                    allow_slow_non_contiguous=True),
                    deps=[mst, (semV, S.cnt["V"])])
                mr = S.v(lambda e: e.reduce_sum(p0sw[0:1, 6:7], p0red[:], axis=AX.X),
                         deps=[mld])
                mm = S.v(lambda e, nel=nel, wi=wi: e.tensor_scalar(
                    p0sw[0:1, 4 + (wi % 2) * 0 + 0:5] if False else p0sw[0:1, 6:7],
                    p0sw[0:1, 6:7], 1.0 / nel, 1e-5, ALU.mult, ALU.max), deps=[mr])
                mcpy = S.v(lambda e, wi=wi: e.tensor_copy(
                    p0sw[0:1, 3 + wi:4 + wi], p0sw[0:1, 6:7]), deps=[mm])
                mrc = S.v(lambda e, wi=wi: e.reciprocal(
                    p0sw[0:1, wi:wi + 1], p0sw[0:1, 6:7]), deps=[mcpy])
                mstw = S.dma("SY", st_rd2[wi], lambda e, wi=wi: e.dma_start(
                    out=sw_hbm[wi:wi + 1, :], in_=p0sw[0:1, wi:wi + 1]), deps=[mrc])
                mstm = S.dma("SY", st_rd2[wi], lambda e, wi=wi: e.dma_start(
                    out=sw_hbm[4 + wi:5 + wi, :], in_=p0sw[0:1, 3 + wi:4 + wi]),
                    deps=[mstw])
                mb1 = S.dma("SY", st_rd2[wi], lambda e, wi=wi: e.dma_start(
                    out=swv[:, wi:wi + 1],
                    in_=sw_hbm[wi:wi + 1, 0:1].partition_broadcast(128)),
                    deps=[mstm])
                mb2 = S.dma("SY", st_rd2[wi], lambda e, wi=wi: e.dma_start(
                    out=swv[:, 4 + wi:5 + wi],
                    in_=sw_hbm[4 + wi:5 + wi, 0:1].partition_broadcast(128)),
                    deps=[mb1])
                sw_ready.append(mb1)
                sw_ready.append(mb2)

        def emit_p0B(u):
            wi, b, ss = u
            buf = p0st["ldbuf"]; p0st["ldbuf"] ^= 1
            wdr, wtd = W3[wi]
            ldm = S.dma("GP", st_p0in[buf],
                        lambda e, wdr=wdr, b=b, ss=ss, buf=buf: e.dma_start(
                            out=p0in[buf][:],
                            in_=wdr[b * 128:(b + 1) * 128, ss * D:(ss + 1) * D]),
                        deps=[p0st["use"][buf]])
            deps = [ldm, p0st["qfree"]] + sw_ready
            bfb = p0st["bfbuf"]; p0st["bfbuf"] ^= 1
            mq = None
            for sp in range(2):
                sl = slice(sp * 512, (sp + 1) * 512)
                mt1 = S.v(lambda e, wi=wi, buf=buf, sl=sl: e.tensor_scalar(
                    p0t1[:], p0in[buf][:, sl], swv[:, wi:wi + 1], 0.5,
                    ALU.mult, ALU.is_gt), deps=deps + [p0st["qfree"]])
                mt2 = S.v(lambda e, wi=wi, buf=buf, sl=sl: e.tensor_scalar(
                    p0t2[:], p0in[buf][:, sl], swv[:, wi:wi + 1], -0.5,
                    ALU.mult, ALU.is_lt), deps=deps + [p0st["qfree"]])
                mq = S.v(lambda e, bfb=bfb, sl=sl: e.tensor_sub(
                    p0bf[bfb][:, sl], p0t1[:], p0t2[:]),
                    deps=[mt1, mt2] + p0st["bf_free"][bfb])
                p0st["qfree"] = mq
            p0st["use"][buf] = mq
            stg = 0
            ev_marks = []
            lastmm = None
            for rnd in range(2):
                bank = p0st["bank"]; p0st["bank"] = (p0st["bank"] + 1) % 8
                mm = None
                for k in range(4):
                    blk = rnd * 4 + k
                    mm = S.t(lambda e, bfb=bfb, blk=blk, k=k, bank=bank:
                             e.matmul(pb[bank][:, k * 128:(k + 1) * 128],
                                      p0bf[bfb][:, blk * 128:(blk + 1) * 128],
                                      ident[:], start=(k == 0), stop=(k == 3)),
                             deps=[mq, m_aux2] +
                                  (pbfree[bank] if k == 0 else []))
                mev = S.s(lambda e, stg=stg, rnd=rnd, bank=bank: e.copy(
                    p0tr[stg][:, rnd * 512:(rnd + 1) * 512], pb[bank][:]),
                    deps=[mm, p0st["tr_free"][stg]])
                pbfree[bank] = [mev]
                ev_marks.append(mev)
                lastmm = mm
            if wi < 2:
                dst = wtd[0:8, :, b * 128:(b + 1) * 128]
            else:
                dst = wtd[ss * 8:ss * 8 + 8, :, b * 128:(b + 1) * 128]
            mstb = S.dma("GP", st_p0s,
                         lambda e, dst=dst, stg=stg: e.dma_start(
                             out=dst.transpose([1, 0, 2]),
                             in_=p0tr[stg].ap().rearrange(
                                 "p (ic q) -> p ic q", ic=8)),
                         deps=ev_marks)
            p0st["tr_free"][stg] = (st_p0s.sem, st_p0s.count)
            p0st["bf_free"][bfb] = [lastmm]
            p0_wdone[wi] = mstb

        xuse = [(semV, 0)] * 5
        xbuse = [(st_xt[0].sem, 0), (st_xt[1].sem, 0),
                 (st_xt[0].sem, 0), (st_xt[1].sem, 0)]
        xtuse = [(semT, 0), (semT, 0)]
        pbfree = [[(semV, 0)]] * 8
        st_cast = [(semV, 0)]
        ln_free = [(semS, 0), (semS, 0)]
        ot_free = [(semT, 0), (semT, 0)]
        oc_free = [(st_ot[0].sem, 0), (st_ot[1].sem, 0)]
        s1_free = (st_s1[0].sem, 0)
        m_s1_stores = []

        def st_dep():
            return st_cast[0]

        p0_pump = [lambda n: None]

        def emit_xpipe(qi):
            qp = qi % 2
            mxts = []
            for half in range(4):
                ci = 4 * qi + half
                bx = ci % 2
                r0 = ci * C
                mxb = S.dma("GP", st_xb[ci % 2],
                            lambda e, r0=r0, bx=bx: e.dma_start(
                                out=xb_c[bx][:], in_=x_d[r0:r0 + C, :]),
                            deps=[xbuse[bx]])
                mxt = None
                for ic in range(8):
                    mxt = S.dma("SY", st_xt[ci % 2],
                                lambda e, ic=ic, bx=bx, qp=qp, half=half:
                                e.dma_start(
                                    out=xT2[qp][:, ic * 512 + half * 128:
                                                ic * 512 + half * 128 + 128],
                                    in_=xb_c[bx][:, ic * 128:(ic + 1) * 128],
                                    transpose=True),
                                deps=[mxb, xtuse[qp]])
                xbuse[bx] = mxt
                mxts.append(mxt)
            return mxts

        def emit_quad(qi, mxts, nq):
            qp = qi % 2
            mxs = []
            for half in range(4):
                ci = 4 * qi + half
                bi8 = ci % 5
                r0 = ci * C
                mxs.append(S.dma("SY", st_x[ci % 5],
                                 lambda e, r0=r0, bi8=bi8: e.dma_start(
                                     out=x_c[bi8][:], in_=x_d[r0:r0 + C, :]),
                                 deps=[xuse[bi8]]))
            dep0 = mxts + w_dep

            for r in range(4):
                lastb = {}
                for qsel in (True, False):
                    wsb = wq_sb if qsel else wk_sb
                    for pp2 in range(2):
                        p = r * 2 + pp2
                        bi = (0 if qsel else 4) + (r % 2) * 2 + pp2
                        for ic in range(8):
                            first = ic == 0
                            lastb[bi] = S.t(
                                lambda e, wsb=wsb, ic=ic, p=p, bi=bi,
                                first=first, qp=qp:
                                e.matmul(pb[bi][:, :],
                                         wsb[:, ic * D + p * 128:
                                             ic * D + p * 128 + 128],
                                         xT2[qp][:, ic * 512:(ic + 1) * 512],
                                         start=first, stop=(ic == 7)),
                                deps=dep0 + (pbfree[bi] if first else []))
                for pp2 in range(2):
                    p = r * 2 + pp2
                    bi = (r % 2) * 2 + pp2
                    mq1 = S.s(lambda e, p=p, bi=bi: e.copy(
                        qs_all[:, p * 512:p * 512 + 512], pb[bi][:]),
                        deps=[lastb[bi]])
                    mq2 = S.s(lambda e, p=p, bi=bi: e.copy(
                        qpd_all[0:64, p * 1024:(p + 1) * 1024].rearrange(
                            "p (c w) -> p c w", c=4)[:, :, 0:128],
                        pb[bi][0:64, :].rearrange("p (c w) -> p c w", c=4)),
                        deps=[lastb[bi]])
                    mq3 = S.s(lambda e, p=p, bi=bi: e.copy(
                        qpd_all[64:128, p * 1024:(p + 1) * 1024].rearrange(
                            "p (c w) -> p c w", c=4)[:, :, 128:256],
                        pb[bi][64:128, :].rearrange("p (c w) -> p c w", c=4)),
                        deps=[lastb[bi]])
                    pbfree[bi] = [mq3]
                    bk = 4 + (r % 2) * 2 + pp2
                    mk1 = S.s(lambda e, p=p, bk=bk: e.copy(
                        ks_all[:, p * 512:p * 512 + 512], pb[bk][:]),
                        deps=[lastb[bk]])
                    pbfree[bk] = [mk1]
            m_qk = S.mark("S")
            p0_pump[0](2)

            for sub in range(2):
                lastkv = {}
                for h2 in range(2):
                    half = sub * 2 + h2
                    for vsel in (False, True):
                        wsb = wv_sb if vsel else wk_sb
                        boff = h2 * 4 + (2 if vsel else 0)
                        for dhalf in range(2):
                            bi = boff + dhalf
                            for ic in range(8):
                                first = ic == 0
                                lastkv[bi] = S.t(
                                    lambda e, wsb=wsb, ic=ic, dhalf=dhalf,
                                    bi=bi, qp=qp, half=half, first=first:
                                    e.matmul(pb[bi][:, :],
                                             xT2[qp][:, ic * 512 + half * 128:
                                                     ic * 512 + half * 128 + 128],
                                             wsb[:, ic * D + dhalf * 512:
                                                 ic * D + dhalf * 512 + 512],
                                             start=first, stop=(ic == 7)),
                                    deps=dep0 + (pbfree[bi] if first else []))
                for bi in range(8):
                    h2 = bi // 4
                    half = sub * 2 + h2
                    vsel = (bi % 4) >= 2
                    dhalf = bi % 2
                    dstt = v_c2 if vsel else k_c2
                    co = half * D + dhalf * 512
                    pbfree[bi] = [S.s(lambda e, dstt=dstt, co=co, bi=bi:
                                      e.copy(dstt[:, co:co + 512], pb[bi][:]),
                                      deps=[lastkv[bi]])]
            xtuse[qp] = S.mark("T")
            m_kv = S.mark("S")
            if qi + 1 < nq:
                pend_T[qi + 1] = emit_xpipe(qi + 1)
            p0_pump[0](2)
            for half in range(4):
                emit_heads(4 * qi + half, half, m_qk, m_kv, mxs)
                p0_pump[0](2)
                emit_wo_ln(4 * qi + half, m_qk, m_kv)
                p0_pump[0](3)

        hstate = {}

        def emit_heads(ci, half, m_qk, m_kv, mxs):
            bi4 = ci % 5
            r0 = ci * C
            koff = half * D
            qoff = half * 128

            def qs(p):
                return qs_all[:, p * 512 + qoff: p * 512 + qoff + 128]

            def ks(p):
                return ks_all[:, p * 512 + qoff: p * 512 + qoff + 128]

            def qpd(p):
                return qpd_all[:, p * 1024 + half * 256:
                               p * 1024 + half * 256 + 256]

            def kc(h):
                return k_c2[:, koff + h * DK: koff + (h + 1) * DK]

            def vc(h):
                return v_c2[:, koff + h * DK: koff + (h + 1) * DK]

            def kcp(p):
                return k_c2[:, koff + p * 128: koff + p * 128 + 128]

            def vcp(p):
                return v_c2[:, koff + p * 128: koff + p * 128 + 128]

            def skbp(p):
                return skb[:, p * 128:(p + 1) * 128]

            def dbp(p):
                return db[:, p * 128:(p + 1) * 128]

            lA = {}
            lR = {}
            for g in range(4):
                for pp in range(2):
                    p = 2 * g + pp
                    rb = 4 + p // 4
                    lA[g] = S.t(lambda e, p=p, pp=pp, g=g: e.matmul(
                        pb[g][:, pp * 256: pp * 256 + 256], ks(p), qpd(p),
                        start=(pp == 0), stop=(pp == 1)),
                        deps=[m_qk] + (pbfree[g] if pp == 0 else []))
                    lR[rb] = S.t(lambda e, p=p, rb=rb: e.matmul(
                        pb[rb][:, (p % 4) * 128:(p % 4) * 128 + 128],
                        skbp(p), qs(p),
                        start=(p % 4 == 0), stop=(p % 4 == 3)),
                        deps=[m_qk, st_dep()] +
                             (pbfree[rb] if p % 4 == 0 else []))
            evb = {}
            for g in range(4):
                g0 = g * 512
                mBT = S.v(lambda e, g=g, g0=g0: e.tensor_tensor(
                    bts_a[:, g0:g0 + 512], pb[g][:], trium[:], ALU.mult),
                    deps=[lA[g]])
                mATs = S.s(lambda e, g=g, g0=g0: e.copy(
                    ats_a[:, g0:g0 + 512], pb[g][:]), deps=[lA[g], mBT])
                pbfree[g] = [mATs]
                evb[g] = (mBT, mATs)
            mRT2 = {}
            for rb in (4, 5):
                mRT2[rb] = S.s(lambda e, rb=rb: e.copy(
                    rts_a[:, (rb - 4) * 512:(rb - 4) * 512 + 512],
                    pb[rb][:]), deps=[lR[rb]])
                pbfree[rb] = [mRT2[rb]]

            lXU = {}
            lX2 = {}
            for g in range(4):
                mBT, mATs = evb[g]
                for pp in range(2):
                    p = 2 * g + pp
                    x2b = 4 + p // 4
                    first2 = (p % 4 == 0)
                    lX2[x2b] = S.t(lambda e, p=p, x2b=x2b, first2=first2:
                                   e.matmul(
                        pb[x2b][:, (p % 4) * 128:(p % 4) * 128 + 128],
                        qs(p), skbp(p),
                        start=first2, stop=False),
                        deps=[m_qk, st_dep()] +
                             (pbfree[x2b] if first2 else []))
                    for j2 in range(2):
                        h = 2 * p + j2
                        co = h * 128
                        lXU[g] = S.t(lambda e, co=co, h=h, g=g: e.matmul(
                            pb[g][:, (h % 4) * 128:(h % 4) * 128 + 128],
                            bts_a[:, co:co + 128], ats_a[:, co:co + 128],
                            start=(h % 4 == 0), stop=False),
                            deps=[mBT, mATs] +
                                 (pbfree[g] if h % 4 == 0 else []))
                        lX2[x2b] = S.t(lambda e, co=co, h=h, p=p, x2b=x2b,
                                       j2=j2: e.matmul(
                            pb[x2b][:, (p % 4) * 128 + j2 * DK:
                                    (p % 4) * 128 + (j2 + 1) * DK],
                            bts_a[:, co:co + 128], kc(h),
                            start=False, stop=(p % 4 == 3 and j2 == 1)),
                            deps=[mBT, m_kv])
                    lXU[g] = S.t(lambda e, p=p, pp=pp, g=g: e.matmul(
                        pb[g][:, pp * 256: pp * 256 + 256],
                        rts_a[:, p * 128:(p + 1) * 128], qpd(p),
                        start=False, stop=(pp == 1)),
                        deps=[mRT2[4 + p // 4]])
            evu = {}
            for g in range(4):
                g0 = g * 512
                mUTs = S.v(lambda e, g=g, g0=g0: e.tensor_tensor(
                    uts_a[:, g0:g0 + 512], pb[g][:], trium[:], ALU.mult),
                    deps=[lXU[g]])
                pbfree[g] = [mUTs]
                evu[g] = mUTs
            for bb in (4, 5):
                co = (bb - 4) * 512
                mX2s = S.s(lambda e, bb=bb, co=co: e.copy(
                    x2s_a[:, co:co + 512], pb[bb][:]), deps=[lX2[bb]])
                pbfree[bb] = [mX2s]
                evu[bb] = mX2s

            lO = {}
            lKD = {}
            for p in range(8):
                g = p // 2
                ob = 6 + p // 4
                kb = p // 4
                db_ = 2 + p // 4
                oco = (p % 4) * 128
                for j2 in range(2):
                    h = 2 * p + j2
                    co = h * 128
                    firsto = (p % 4 == 0 and j2 == 0)
                    lO[ob] = S.t(lambda e, co=co, h=h, ob=ob, oco=oco, j2=j2,
                                 firsto=firsto: e.matmul(
                        pb[ob][:, oco + j2 * DK: oco + (j2 + 1) * DK],
                        uts_a[:, co:co + 128], vc(h),
                        start=firsto, stop=False),
                        deps=[evu[g], m_kv] + (pbfree[ob] if firsto else []))
                lO[ob] = S.t(lambda e, p=p, ob=ob, oco=oco: e.matmul(
                    pb[ob][:, oco: oco + 128], qs(p), dbp(p),
                    start=False, stop=(p % 4 == 3)), deps=[st_dep()])
                lKD[kb] = S.t(lambda e, p=p, kb=kb: e.matmul(
                    pb[kb][:, (p % 4) * 128:(p % 4) * 128 + 128],
                    kcp(p), kcp(p), start=(p % 4 == 0), stop=(p % 4 == 3)),
                    deps=[m_kv] + (pbfree[kb] if p % 4 == 0 else []))
                lKD[db_] = S.t(lambda e, p=p, db_=db_: e.matmul(
                    pb[db_][:, (p % 4) * 128:(p % 4) * 128 + 128],
                    x2s_a[:, p * 128:(p + 1) * 128], vcp(p),
                    start=(p % 4 == 0), stop=(p % 4 == 3)),
                    deps=[evu[4 + p // 4], m_kv] +
                         (pbfree[db_] if p % 4 == 0 else []))
            m_heads_T = S.mark("T")
            ot_marks = []
            pp2o = 0
            for ob in (6, 7):
                co = (ob - 6) * 512
                mo = S.s(lambda e, ob=ob, co=co, pp2o=pp2o: e.copy(
                    o_c[pp2o][:, co:co + 512], pb[ob][:]),
                    deps=[lO[ob]] + ([oc_free[pp2o]] if ob == 6 else []))
                pbfree[ob] = [mo]
                ot_marks.append(mo)
            mupd = []
            for kb in (0, 1):
                dst = skf[0:64, kb * 512:(kb + 1) * 512].rearrange(
                    "p (pr w) -> p pr w", pr=4)[:, :, 0:64]
                src = pb[kb][0:64, :].rearrange(
                    "p (pr w) -> p pr w", pr=4)[:, :, 0:64]
                mSKa = S.v(lambda e, dst=dst, src=src, kb=kb: e.tensor_add(
                    dst, dst, src), deps=[lKD[kb]])
                dst2 = skf[64:128, kb * 512:(kb + 1) * 512].rearrange(
                    "p (pr w) -> p pr w", pr=4)[:, :, 64:128]
                src2 = pb[kb][64:128, :].rearrange(
                    "p (pr w) -> p pr w", pr=4)[:, :, 64:128]
                mSKb = S.v(lambda e, dst2=dst2, src2=src2, kb=kb: e.tensor_add(
                    dst2, dst2, src2), deps=[lKD[kb]])
                pbfree[kb] = [mSKb]
                mupd += [mSKa, mSKb]
            for db2 in (2, 3):
                dst = df[0:64, (db2 - 2) * 512:(db2 - 1) * 512].rearrange(
                    "p (pr w) -> p pr w", pr=4)[:, :, 0:64]
                src = pb[db2][0:64, :].rearrange(
                    "p (pr w) -> p pr w", pr=4)[:, :, 0:64]
                mDFa = S.v(lambda e, dst=dst, src=src: e.tensor_add(
                    dst, dst, src), deps=[lKD[db2]])
                dst2 = df[64:128, (db2 - 2) * 512:(db2 - 1) * 512].rearrange(
                    "p (pr w) -> p pr w", pr=4)[:, :, 64:128]
                src2 = pb[db2][64:128, :].rearrange(
                    "p (pr w) -> p pr w", pr=4)[:, :, 64:128]
                mDFb = S.v(lambda e, dst2=dst2, src2=src2: e.tensor_add(
                    dst2, dst2, src2), deps=[lKD[db2]])
                pbfree[db2] = [mDFb]
                mupd += [mDFa, mDFb]
            mskb = S.v(lambda e: e.tensor_copy(skb[:], skf[:]),
                       deps=mupd + [m_heads_T])
            mdbb = S.v(lambda e: e.tensor_copy(db[:], df[:]),
                       deps=[mskb])
            st_cast[0] = mdbb

            pp2 = 0
            motd = None
            lastmm = None
            for rnd in range(2):
                bank = 6 + rnd
                mm = None
                for k in range(4):
                    blk = rnd * 4 + k
                    mm = S.t(lambda e, blk=blk, k=k, bank=bank, pp2=pp2:
                             e.matmul(pb[bank][:, k * 128:(k + 1) * 128],
                                      o_c[pp2][:, blk * 128:(blk + 1) * 128],
                                      ident[:], start=(k == 0), stop=(k == 3)),
                             deps=[ot_marks[blk // 4], m_aux2] +
                                  (pbfree[bank] if k == 0 else []))
                motd = S.s(lambda e, rnd=rnd, bank=bank, pp2=pp2: e.copy(
                    ot_c[pp2][:, rnd * 512:(rnd + 1) * 512], pb[bank][:]),
                    deps=[mm, ot_free[pp2]])
                pbfree[bank] = [motd]
                lastmm = mm
            oc_free[pp2] = lastmm
            hstate[ci] = (motd, mxs, half)

        def emit_wo_ln(ci, m_qk, m_kv):
            nonlocal s1_free
            motd, mxs, half = hstate[ci]
            bi4 = ci % 5
            pp2 = 0
            r0 = ci * C
            mWop = {}
            for dhalf in range(2):
                bi = 4 + dhalf
                for ic in range(8):
                    first = ic == 0
                    mWop[bi] = S.t(lambda e, ic=ic, dhalf=dhalf, bi=bi,
                                   first=first, pp2=pp2: e.matmul(
                        pb[bi][:], ot_c[pp2][:, ic * 128:(ic + 1) * 128],
                        wo_sb[:, ic * D + dhalf * 512:
                              ic * D + dhalf * 512 + 512],
                        start=first, stop=(ic == 7)),
                        deps=[motd, m_wo] + (pbfree[bi] if first else []))
            ot_free[pp2] = S.mark("T")
            mr1a = S.v(lambda e, bi4=bi4: e.tensor_add(
                x_c[bi4][:, 0:512], x_c[bi4][:, 0:512], pb[4][:]),
                deps=[mWop[4], mxs[half]])
            pbfree[4] = [mr1a]
            mr1b = S.v(lambda e, bi4=bi4: e.tensor_add(
                x_c[bi4][:, 512:1024], x_c[bi4][:, 512:1024], pb[5][:]),
                deps=[mWop[5], mxs[half]])
            pbfree[5] = [mr1b]
            mb1 = S.v(lambda e, bi4=bi4: e.bn_stats(bn6[:, 0:6],
                                                    x_c[bi4][:, 0:512]),
                      deps=[mr1a])
            mb2 = S.v(lambda e, bi4=bi4: e.bn_stats(bn6[:, 6:12],
                                                    x_c[bi4][:, 512:1024]),
                      deps=[mr1b])
            mb3 = S.v(lambda e: e.bn_aggr(bnmv[:], bn6[:]),
                      deps=[mb1, mb2, ln_free[0]])
            msq = S.s(lambda e: e.activation(rstd[:], bnmv[:, 1:2], AF.Sqrt,
                                             bias=CEPS, scale=1.0),
                      deps=[mb3, mk_cons])
            mrc = S.v(lambda e: e.reciprocal(rstd[:], rstd[:]), deps=[msq])
            mnm = S.v(lambda e: e.scalar_tensor_tensor(
                nmr[:], bnmv[:, 0:1], -1.0, rstd[:], ALU.mult, ALU.mult),
                deps=[mrc, ln_free[1]])
            mact = S.s(lambda e, bi4=bi4: e.activation(
                x_c[bi4][:], x_c[bi4][:], AF.Identity, bias=nmr[:],
                scale=rstd[:]), deps=[mnm])
            ln_free[0] = msq
            ln_free[1] = mact
            mst = S.dma("SY", st_s1[ci % 2], lambda e, r0=r0, bi4=bi4:
                        e.dma_start(
                out=s1_hbm[r0:r0 + C, :], in_=x_c[bi4][:]), deps=[mact])
            xuse[bi4] = mst
            m_s1_stores.append(mst)
            if debug and ci == 0:
                dv = S.mark("V")
                ds = S.mark("S")
                for nm, tile in [("d_qt", qs_all), ("d_kt", ks_all),
                                 ("d_kc", k_c2), ("d_vc", v_c2),
                                 ("d_ot", ot_c[0]), ("d_uts", uts_a),
                                 ("d_bts", bts_a), ("d_ats", ats_a),
                                 ("d_rts", rts_a), ("d_x2", x2s_a),
                                 ("d_skf", skf), ("d_df", df)]:
                    S.dma("GP", st_auxg, lambda e, nm=nm, tile=tile: e.dma_start(
                        out=dbg[nm], in_=tile[:]), deps=[dv, ds])

        itA = iter(p0A_units)
        itB = iter(p0B_units)
        pump_state = {"dA": 0, "dB": 0, "scales": False}

        def do_pump(n):
            for _ in range(n):
                u = next(itA, None)
                if u is not None:
                    emit_p0A(u)
                    pump_state["dA"] += 1
                    continue
                if not pump_state["scales"]:
                    emit_p0_scales()
                    pump_state["scales"] = True
                    return
                u = next(itB, None)
                if u is None:
                    return
                emit_p0B(u)
                pump_state["dB"] += 1
        p0_pump[0] = do_pump

        pend_T = {0: emit_xpipe(0)}
        for qi in range(nch // 4):
            emit_quad(qi, pend_T.pop(qi), nch // 4)
        for u in itA:
            emit_p0A(u)
        if not pump_state["scales"]:
            emit_p0_scales()
        for u in itB:
            emit_p0B(u)

        BAR = [S.mark("T"), S.mark("V"), S.mark("S"),
               (st_p0s.sem, st_p0s.count), m_s1_stores[-1],
               (st_s1[0].sem, st_s1[0].count), (st_s1[1].sem, st_s1[1].count),
               (st_xt[0].sem, st_xt[0].count), (st_xt[1].sem, st_xt[1].count),
               *[(s.sem, s.count) for s in st_x],
               (st_aux2.sem, st_aux2.count),
               *[(s.sem, s.count) for s in st_rd2],
               (st_xb[0].sem, st_xb[0].count), (st_xb[1].sem, st_xb[1].count),
               (st_p0t[0].sem, st_p0t[0].count), (st_p0t[1].sem, st_p0t[1].count),
               (st_p0in[0].sem, st_p0in[0].count),
               (st_p0in[1].sem, st_p0in[1].count),
               (st_aux.sem, st_aux.count), (st_auxg.sem, st_auxg.count),
               (st_red.sem, st_red.count)]
        p01.close()

        p2 = ExitStack()
        E2 = p2.enter_context
        tile_ch = 4
        ntile = nch // tile_ch if nch >= tile_ch else 1
        s1e = [E2(nc.sbuf_tensor(f"s1e{i}", [C, D], F32)) for i in range(2)]
        s1x = [E2(nc.sbuf_tensor(f"s1x{i}", [C, D], F32)) for i in range(2)]
        xq = [E2(nc.sbuf_tensor(f"xq{i}", [C, D], BF16)) for i in range(4)]
        xqT = [E2(nc.sbuf_tensor(f"xqT{i}", [128, tile_ch * D], BF16))
               for i in range(2)]
        ht = E2(nc.sbuf_tensor("ht", [C, tile_ch * INNER], F32))
        wgub = [E2(nc.sbuf_tensor(f"wgub{i}", [128, 8192], BF16)) for i in range(2)]
        hq = [E2(nc.sbuf_tensor(f"hq{i}", [C, INNER], BF16)) for i in range(2)]
        hqT = E2(nc.sbuf_tensor("hqT", [128, tile_ch * INNER], BF16))
        wdpb = [E2(nc.sbuf_tensor(f"wdpb{i}", [128, 2 * D], BF16)) for i in range(2)]
        qzp = E2(nc.sbuf_tensor("qzp", [128, 512], F32))
        qcp = [E2(nc.sbuf_tensor(f"qcp{i}", [128, 512], I32)) for i in range(2)]
        am_parts = E2(nc.sbuf_tensor("am_parts", [C, 32], F32))
        r2t = [E2(nc.sbuf_tensor(f"r2t{i}", [C, D], F32)) for i in range(2)]
        am2 = E2(nc.sbuf_tensor("am2", [C, 4], F32))
        sv2 = E2(nc.sbuf_tensor("sv2", [C, 2], F32))
        dscl = E2(nc.sbuf_tensor("dscl", [C, 16], F32))
        bn6b = E2(nc.sbuf_tensor("bn6b", [C, 12], F32))
        bnmvb = E2(nc.sbuf_tensor("bnmvb", [C, 2], F32))
        rstd2 = E2(nc.sbuf_tensor("rstd2", [C, 1], F32))
        nmr2 = E2(nc.sbuf_tensor("nmr2", [C, 1], F32))
        pb2 = [E2(nc.psum_tensor(f"pb2_{i}", [128, 512], F32)) for i in range(8)]

        BAR1 = BAR[0]
        wgu_free = [BAR1, BAR1]
        wdp_free = [BAR1, BAR1]
        s1x_free = [BAR1, BAR1]
        s1e_free = [BAR1, BAR1]
        pb2free = [[BAR[1]]] * 8
        hq_free = [BAR1, BAR1]
        hqT_free = BAR1
        xq_free = [BAR1, BAR1, BAR1, BAR1]
        xqT_free = [BAR1, BAR1]
        r2_free = [BAR[2], BAR[2]]
        y_free = [(st_y[0].sem, 0), (st_y[1].sem, 0)]
        qtmp_free = BAR[1]
        p2first = list(BAR)
        wg_done = [p0_wdone.get(0), p0_wdone.get(1), p0_wdone.get(2)]
        s1_all = [BAR[5], BAR[6]]
        ln2_free = [BAR[2], BAR[2]]

        qsub = [0]
        qcp_free = [BAR[1], BAR[1]]

        def quant_to(src_ap, dst_ap, svcol, deps, width=512):
            nonlocal qtmp_free
            par = qsub[0]; qsub[0] ^= 1
            mzp = S.v(lambda e, src_ap=src_ap, svcol=svcol, width=width:
                      e.tensor_scalar(qzp[:, 0:width], src_ap, svcol,
                                      MRND - 0.5, ALU.mult, ALU.add),
                      deps=list(deps) + [qtmp_free])
            mcp = S.s(lambda e, src_ap=src_ap, svcol=svcol, width=width,
                      par=par:
                      e.activation(qcp[par][:, 0:width], src_ap, AF.Identity,
                                   bias=CHI, scale=svcol),
                      deps=list(deps) + [qcp_free[par]])
            mt1 = S.v(lambda e, width=width, par=par: e.tensor_tensor(
                qzp[:, 0:width], qzp[:, 0:width], qcp[par][:, 0:width],
                ALU.is_gt), deps=[mzp, mcp])
            mr = S.v(lambda e, dst_ap=dst_ap, width=width, par=par:
                     e.scalar_tensor_tensor(
                dst_ap, qcp[par][:, 0:width], -MRND, qzp[:, 0:width],
                ALU.add, ALU.add), deps=[mt1])
            qcp_free[par] = mr
            qtmp_free = mr
            return mr

        m_y_fin = []
        tile_state = {}

        def emit_xq_prep_q(ti, lo, hi):
            c0 = ti * tile_ch
            qm = {}
            for cc in range(lo, hi):
                ci = c0 + cc
                sb = cc % 2
                r0 = ci * C
                ms1 = S.dma("SY", st_s1l[sb], lambda e, r0=r0, sb=sb: e.dma_start(
                    out=s1x[sb][:], in_=s1_hbm[r0:r0 + C, :]),
                    deps=[s1x_free[sb]] + s1_all + p2first)
                src2 = s1x[sb]
                mam = S.v(lambda e, src2=src2: e.reduce_max(
                    am2[:, 0:1], src2[:], axis=AX.X, apply_absolute_value=True),
                    deps=[ms1] + p2first)
                mamc = S.v(lambda e: e.tensor_scalar(
                    am2[:, 1:2], am2[:, 0:1], 1e-5, None, ALU.max), deps=[mam])
                mrc = S.v(lambda e: e.reciprocal(sv2[:, 0:1], am2[:, 1:2]),
                          deps=[mamc])
                msc = S.v(lambda e: e.tensor_scalar(
                    sv2[:, 0:1], sv2[:, 0:1], 127.0, None, ALU.mult), deps=[mrc])
                S.v(lambda e, cc=cc: e.scalar_tensor_tensor(
                    dscl[:, cc:cc + 1], am2[:, 1:2], 1.0 / 127.0, swv[:, 4:5],
                    ALU.mult, ALU.mult), deps=[mamc] + sw_ready)
                S.v(lambda e, cc=cc: e.scalar_tensor_tensor(
                    dscl[:, 4 + cc:5 + cc], am2[:, 1:2], 1.0 / 127.0,
                    swv[:, 5:6], ALU.mult, ALU.mult), deps=[mamc])
                mq1 = quant_to(src2[:, 0:512], xq[cc % 4][:, 0:512],
                               sv2[:, 0:1], [msc, xq_free[cc % 4]])
                mq2 = quant_to(src2[:, 512:1024], xq[cc % 4][:, 512:1024],
                               sv2[:, 0:1], [msc])
                s1x_free[sb] = mq2
                qm[cc] = (mq1, mq2)
            return qm

        def emit_xq_prep_T(ti, lo, hi, qm):
            tp2 = ti % 2
            mxqt = None
            for cc in range(lo, hi):
                mq1, mq2 = qm[cc]
                lastmm = None
                for rnd in range(2):
                    bank = 4 + rnd
                    mm = None
                    for k in range(4):
                        blk = rnd * 4 + k
                        mm = S.t(lambda e, blk=blk, k=k, bank=bank, cc=cc:
                                 e.matmul(pb2[bank][:, k * 128:(k + 1) * 128],
                                          xq[cc % 4][:, blk * 128:(blk + 1) * 128],
                                          ident[:], start=(k == 0), stop=(k == 3)),
                                 deps=[mq1, mq2, m_aux2] +
                                      (pb2free[bank] if k == 0 else []))
                    mxqt = S.s(lambda e, cc=cc, rnd=rnd, bank=bank, tp2=tp2:
                               e.copy(
                        xqT[tp2][:, cc * D + rnd * 512: cc * D + (rnd + 1) * 512],
                        pb2[bank][:]), deps=[mm, xqT_free[tp2]])
                    pb2free[bank] = [mxqt]
                    lastmm = mm
                xq_free[cc % 4] = lastmm
            return mxqt

        def emit_stageA(ti, m_xqT):
            tp2 = ti % 2
            h_marks = [None] * tile_ch
            for pcs in range(8):
                pbuf = pcs % 2
                mldg = S.dma("GP", st_wgu[pbuf],
                             lambda e, pcs=pcs, pbuf=pbuf: e.dma_start(
                                 out=wgub[pbuf].ap()[:, 0:4096].rearrange(
                                     "p (ic n) -> p ic n", ic=8),
                                 in_=wgT_hbm[:, :, pcs * 512:(pcs + 1) * 512]
                                 .transpose([1, 0, 2])),
                             deps=[wgu_free[pbuf], wg_done[0]] + p2first)
                mldu = S.dma("GP", st_wgu[pbuf],
                             lambda e, pcs=pcs, pbuf=pbuf: e.dma_start(
                                 out=wgub[pbuf].ap()[:, 4096:8192].rearrange(
                                     "p (ic n) -> p ic n", ic=8),
                                 in_=wuT_hbm[:, :, pcs * 512:(pcs + 1) * 512]
                                 .transpose([1, 0, 2])),
                             deps=[wg_done[1]])
                for cc in range(tile_ch):
                    bg = 4 + (cc % 2) * 2
                    bu = bg + 1
                    lg = lu = None
                    for ic in range(8):
                        first = ic == 0
                        lg = S.t(lambda e, ic=ic, cc=cc, pbuf=pbuf, bg=bg,
                                 first=first, tp2=tp2:
                                 e.matmul(pb2[bg][:],
                                          xqT[tp2][:, cc * D + ic * 128:
                                                   cc * D + (ic + 1) * 128],
                                          wgub[pbuf][:, ic * 512:(ic + 1) * 512],
                                          start=first, stop=(ic == 7)),
                                 deps=[m_xqT, mldu] +
                                      (pb2free[bg] if first else []))
                        lu = S.t(lambda e, ic=ic, cc=cc, pbuf=pbuf, bu=bu,
                                 first=first, tp2=tp2:
                                 e.matmul(pb2[bu][:],
                                          xqT[tp2][:, cc * D + ic * 128:
                                                   cc * D + (ic + 1) * 128],
                                          wgub[pbuf][:, 4096 + ic * 512:
                                                     4096 + (ic + 1) * 512],
                                          start=first, stop=(ic == 7)),
                                 deps=[mldu] + (pb2free[bu] if first else []))
                    hsl = slice(cc * INNER + pcs * 512,
                                cc * INNER + (pcs + 1) * 512)
                    pq = (tile_state[ti - 1]["qmarks"][cc][pcs]
                          if ti > 0 else BAR[1])
                    msil = S.s(lambda e, cc=cc, bg=bg, hsl=hsl, pq=pq:
                               e.activation(
                        ht[:, hsl], pb2[bg][:], AF.Silu, bias=CZERO,
                        scale=dscl[:, cc:cc + 1]), deps=[lg, pq])
                    pb2free[bg] = [msil]
                    mh = S.v(lambda e, cc=cc, pcs=pcs, bu=bu, hsl=hsl:
                             e.scalar_tensor_tensor(
                        ht[:, hsl],
                        pb2[bu][:], dscl[:, 4 + cc:5 + cc], ht[:, hsl],
                        ALU.mult, ALU.mult), deps=[msil, lu])
                    pb2free[bu] = [mh]
                    mam = S.v(lambda e, cc=cc, pcs=pcs: e.reduce_max(
                        am_parts[:, cc * 8 + pcs: cc * 8 + pcs + 1],
                        ht[:, cc * INNER + pcs * 512:
                           cc * INNER + (pcs + 1) * 512],
                        axis=AX.X, apply_absolute_value=True), deps=[mh])
                    h_marks[cc] = mam
                wgu_free[pbuf] = S.mark("T")
            tile_state[ti] = {"h_marks": h_marks}

        def emit_quant(ti, lo, hi):
            nonlocal hq_free
            h_marks = tile_state[ti]["h_marks"]
            qmarks = tile_state[ti].setdefault("qmarks", {})
            for cc in range(lo, hi):
                mam = S.v(lambda e, cc=cc: e.reduce_max(
                    am2[:, 2:3], am_parts[:, cc * 8:(cc + 1) * 8], axis=AX.X),
                    deps=[h_marks[cc]])
                mamc = S.v(lambda e: e.tensor_scalar(
                    am2[:, 3:4], am2[:, 2:3], 1e-5, None, ALU.max), deps=[mam])
                mrc = S.v(lambda e: e.reciprocal(sv2[:, 1:2], am2[:, 3:4]),
                          deps=[mamc])
                msc = S.v(lambda e: e.tensor_scalar(
                    sv2[:, 1:2], sv2[:, 1:2], 127.0, None, ALU.mult), deps=[mrc])
                dco = 8 + (ti % 2) * 4 + cc
                S.v(lambda e, dco=dco: e.scalar_tensor_tensor(
                    dscl[:, dco:dco + 1], am2[:, 3:4], 1.0 / 127.0,
                    swv[:, 6:7], ALU.mult, ALU.mult), deps=[mamc] + sw_ready)
                subm = []
                for sub in range(8):
                    subm.append(quant_to(
                        ht[:, cc * INNER + sub * 512:
                           cc * INNER + (sub + 1) * 512],
                        hq[cc % 2][:, sub * 512:(sub + 1) * 512], sv2[:, 1:2],
                        [msc, hq_free[cc % 2]]))
                qmarks[cc] = subm

        def emit_hqT(ti, lo, hi):
            qmarks = tile_state[ti]["qmarks"]
            hqt_marks = tile_state[ti].setdefault("hqt_marks", {})
            for cc in range(lo, hi):
                subm = qmarks[cc]
                lastmm = None
                mev = None
                for rnd in range(8):
                    bank = rnd % 2
                    mm = None
                    for k in range(4):
                        blk = rnd * 4 + k
                        mm = S.t(lambda e, blk=blk, k=k, bank=bank, cc=cc:
                                 e.matmul(pb2[bank][:, k * 128:(k + 1) * 128],
                                          hq[cc % 2][:, blk * 128:(blk + 1) * 128],
                                          ident[:], start=(k == 0), stop=(k == 3)),
                                 deps=[subm[rnd], m_aux2] +
                                      (pb2free[bank] if k == 0 else []))
                    mev = S.s(lambda e, cc=cc, rnd=rnd, bank=bank: e.copy(
                        hqT[:, cc * INNER + rnd * 512:
                            cc * INNER + (rnd + 1) * 512],
                        pb2[bank][:]), deps=[mm, hqT_free])
                    pb2free[bank] = [mev]
                    lastmm = mm
                hq_free[cc % 2] = lastmm
                hqt_marks[cc] = mev

        def emit_down_ep(ti):
            nonlocal hqT_free
            c0 = ti * tile_ch
            hqt_marks = tile_state[ti]["hqt_marks"]
            lastd = {}
            for pcs in range(16):
                pbuf = pcs % 2
                mldd = S.dma("GP", st_wdp[pbuf],
                             lambda e, pcs=pcs, pbuf=pbuf: e.dma_start(
                                 out=wdpb[pbuf].ap().rearrange(
                                     "p (i2 o) -> p i2 o", i2=2),
                                 in_=wdT_hbm[pcs * 2:(pcs + 1) * 2, :, :]
                                 .transpose([1, 0, 2])),
                             deps=[wdp_free[pbuf], wg_done[2]] + p2first)
                for cc in range(tile_ch):
                    for i2l in range(2):
                        i2 = pcs * 2 + i2l
                        for dh in range(2):
                            bi = cc * 2 + dh
                            first = pcs == 0 and i2l == 0
                            lastd[bi] = S.t(
                                lambda e, cc=cc, i2=i2, i2l=i2l, dh=dh,
                                bi=bi, pbuf=pbuf, first=first, pcs=pcs:
                                e.matmul(pb2[bi][:],
                                         hqT[:, cc * INNER + i2 * 128:
                                             cc * INNER + (i2 + 1) * 128],
                                         wdpb[pbuf][:, i2l * D + dh * 512:
                                                    i2l * D + dh * 512 + 512],
                                         start=first,
                                         stop=(pcs == 15 and i2l == 1)),
                                deps=[hqt_marks[cc], mldd] +
                                     (pb2free[bi] if first else []))
                wdp_free[pbuf] = S.mark("T")
            for cc in range(tile_ch):
                ci = c0 + cc
                r0 = ci * C
                eb = ci % 2
                yb = ci % 2
                ms1 = S.dma("SY", st_s1e[eb], lambda e, r0=r0, eb=eb:
                            e.dma_start(out=s1e[eb][:],
                                        in_=s1_hbm[r0:r0 + C, :]),
                            deps=[s1e_free[eb]] + s1_all)
                dco = 8 + (ti % 2) * 4 + cc
                md0 = S.s(lambda e, cc=cc, eb=eb, dco=dco: e.activation(
                    r2t[eb][:, 0:512], pb2[cc * 2][:], AF.Identity,
                    bias=CZERO, scale=dscl[:, dco:dco + 1]),
                    deps=[lastd[cc * 2], r2_free[eb]])
                pb2free[cc * 2] = [md0]
                md1 = S.s(lambda e, cc=cc, eb=eb, dco=dco: e.activation(
                    r2t[eb][:, 512:1024], pb2[cc * 2 + 1][:], AF.Identity,
                    bias=CZERO, scale=dscl[:, dco:dco + 1]),
                    deps=[lastd[cc * 2 + 1]])
                pb2free[cc * 2 + 1] = [md1]
                mr2 = S.v(lambda e, eb=eb: e.tensor_add(
                    r2t[eb][:], r2t[eb][:], s1e[eb][:]),
                    deps=[md0, md1, ms1])
                s1e_free[eb] = mr2
                mb1 = S.v(lambda e, eb=eb: e.bn_stats(
                    bn6b[:, 0:6], r2t[eb][:, 0:512]), deps=[mr2])
                mb2 = S.v(lambda e, eb=eb: e.bn_stats(
                    bn6b[:, 6:12], r2t[eb][:, 512:1024]), deps=[mr2])
                mb3 = S.v(lambda e: e.bn_aggr(bnmvb[:], bn6b[:]),
                          deps=[mb1, mb2, ln2_free[0]])
                msq = S.s(lambda e: e.activation(
                    rstd2[:], bnmvb[:, 1:2], AF.Sqrt, bias=CEPS, scale=1.0),
                    deps=[mb3])
                mrc2 = S.v(lambda e: e.reciprocal(rstd2[:], rstd2[:]),
                           deps=[msq])
                mnm2 = S.v(lambda e: e.scalar_tensor_tensor(
                    nmr2[:], bnmvb[:, 0:1], -1.0, rstd2[:],
                    ALU.mult, ALU.mult), deps=[mrc2, ln2_free[1]])
                mact = S.s(lambda e, eb=eb: e.activation(
                    r2t[eb][:], r2t[eb][:], AF.Identity, bias=nmr2[:],
                    scale=rstd2[:]), deps=[mnm2, y_free[yb]])
                ln2_free[0] = msq
                ln2_free[1] = mact
                mys = S.dma("SY", st_y[yb], lambda e, r0=r0, eb=eb:
                            e.dma_start(out=y_d[r0:r0 + C, :],
                                        in_=r2t[eb][:]), deps=[mact])
                y_free[yb] = mys
                r2_free[eb] = mys
                m_y_fin.append(mys)
            hqT_free = S.mark("T")
            xqT_free[ti % 2] = S.mark("T")

        prev_qm = None
        for ti in range(ntile):
            qm = prev_qm if prev_qm is not None else emit_xq_prep_q(ti, 0, 4)
            mx = emit_xq_prep_T(ti, 0, 4, qm)
            emit_stageA(ti, mx)
            emit_quant(ti, 0, 2)
            if ti > 0:
                emit_down_ep(ti - 1)
            emit_hqT(ti, 0, 2)
            emit_quant(ti, 2, 4)
            if ti + 1 < ntile:
                prev_qm = emit_xq_prep_q(ti + 1, 0, 4)
            emit_hqT(ti, 2, 4)
            p2first = []
        emit_down_ep(ntile - 1)

        def fin(e):
            return e.nop()
        S.ops["SY"].append((S._waits("SY", m_y_fin[-2:] +
                                     [(st_y[0].sem, st_y[0].count),
                                      (st_y[1].sem, st_y[1].count)]),
                            fin, None))

        blk = E(nc.Block())

        def runner(key):
            def go(eng):
                for waits, fn, inc in S.ops[key]:
                    for (sem, val) in waits:
                        eng.wait_ge(sem, val)
                    ins = fn(eng)
                    if inc is not None:
                        ins.then_inc(inc[0], inc[1])
            return go

        blk.tensor(runner("T"))
        blk.vector(runner("V"))
        blk.scalar(runner("S"))
        blk.sync(runner("SY"))
        blk.gpsimd(runner("GP"))

        p2.close()
    return nc


_NC_CACHE = {}


def _get_nc():
    if "nc" not in _NC_CACHE:
        _NC_CACHE["nc"] = build_program()
    return _NC_CACHE["nc"]


def kernel(L0_attn_x, L0_attn_W_q, L0_attn_W_k, L0_attn_W_v, L0_attn_W_o,
           L0_ffn_w_gate, L0_ffn_w_up, L0_ffn_w_down):
    x = np.ascontiguousarray(np.asarray(L0_attn_x, dtype=np.float32))
    wq = np.ascontiguousarray(np.asarray(L0_attn_W_q, dtype=np.float32))
    wk = np.ascontiguousarray(np.asarray(L0_attn_W_k, dtype=np.float32))
    wv = np.ascontiguousarray(np.asarray(L0_attn_W_v, dtype=np.float32))
    wo = np.ascontiguousarray(np.asarray(L0_attn_W_o, dtype=np.float32))
    wg = np.ascontiguousarray(np.asarray(L0_ffn_w_gate, dtype=np.float32))
    wu = np.ascontiguousarray(np.asarray(L0_ffn_w_up, dtype=np.float32))
    wd = np.ascontiguousarray(np.asarray(L0_ffn_w_down, dtype=np.float32))

    tril = np.tril(np.ones((C, C), np.float32))
    trium = np.tile(tril.T, (1, 4)).astype(ml_dtypes.bfloat16)
    ident = np.eye(C, dtype=np.float32).astype(ml_dtypes.bfloat16)

    nc = _get_nc()
    in_maps = []
    for b in range(NCORE):
        in_maps.append(dict(x=x[b], wq=wq, wk=wk, wv=wv, wo=wo,
                            wg=wg, wu=wu, wd=wd, trium=trium, ident=ident))
    res = run_bass_kernel_spmd(nc, in_maps, core_ids=list(range(NCORE)))
    return np.stack([res.results[b]["y"] for b in range(NCORE)], axis=0)
